# version 21
# baseline (speedup 1.0000x reference)
import numpy as np
import contextlib
import concourse.bass as bass
import concourse.mybir as mybir
from concourse.bass_utils import run_bass_kernel_spmd

F32 = mybir.dt.float32
AF = mybir.ActivationFunctionType
ALU = mybir.AluOpType
AX = mybir.AxisListType

B_FULL = 16
B2 = 2
L = 2048
D = 1024
DI = 2048
NST = 16
RDT = 64
IFF = 2816
T = 128
NCH = L // T
KD = D // 128
KDI = DI // 128
KI = IFF // 128
TT2 = 2 * T
JSLABS = [(0, 512), (512, 512), (1024, 512), (1536, 512), (2048, 512), (2560, 256)]

_DEBUG = False
_PHASES = 99

_NCHUNKS = NCH

ENGS = ("sync", "tensor", "vector", "scalar", "gpsimd")


class _StopRec(Exception):
    pass


class Sem:
    __slots__ = ("h", "n")

    def __init__(self, h):
        self.h = h
        self.n = 0


def build_nc():
    nc = bass.Bass("TRN2", target_bir_lowering=False, debug=False)

    x_d = nc.dram_tensor("x", [B2, L, D], F32, kind="ExternalInput").ap()
    lng_d = nc.dram_tensor("ln_g", [D], F32, kind="ExternalInput").ap()
    lnb_d = nc.dram_tensor("ln_b", [D], F32, kind="ExternalInput").ap()
    win_d = nc.dram_tensor("W_in", [D, 2 * DI], F32, kind="ExternalInput").ap()
    convw_d = nc.dram_tensor("conv_w", [DI, 1, 4], F32, kind="ExternalInput").ap()
    convb_d = nc.dram_tensor("conv_b", [DI], F32, kind="ExternalInput").ap()
    wxp_d = nc.dram_tensor("W_xproj", [DI, 96], F32, kind="ExternalInput").ap()
    wdt_d = nc.dram_tensor("W_dt", [RDT, DI], F32, kind="ExternalInput").ap()
    bdt_d = nc.dram_tensor("b_dt", [DI], F32, kind="ExternalInput").ap()
    alog_d = nc.dram_tensor("A_log", [DI, NST], F32, kind="ExternalInput").ap()
    dpar_d = nc.dram_tensor("D_param", [DI], F32, kind="ExternalInput").ap()
    wout_d = nc.dram_tensor("W_out", [DI, D], F32, kind="ExternalInput").ap()
    wg_d = nc.dram_tensor("w_gate", [IFF, D], F32, kind="ExternalInput").ap()
    wu_d = nc.dram_tensor("w_up", [IFF, D], F32, kind="ExternalInput").ap()
    wdn_d = nc.dram_tensor("w_down", [D, IFF], F32, kind="ExternalInput").ap()
    ident_d = nc.dram_tensor("ident", [128, 128], F32, kind="ExternalInput").ap()

    out_d = nc.dram_tensor("out", [B2, L, D], F32, kind="ExternalOutput").ap()

    winr_d = nc.dram_tensor("winr", [32, 128, 8 * 128], F32, kind="Internal").ap()
    wgT_d = nc.dram_tensor("wgT", [D, IFF], F32, kind="Internal").ap()
    wuT_d = nc.dram_tensor("wuT", [D, IFF], F32, kind="Internal").ap()
    wdT_d = nc.dram_tensor("wdT", [IFF, D], F32, kind="Internal").ap()

    dbg = {}
    if _DEBUG:
        for nm, shp in [("dbg_xn", [128, 2048]), ("dbg_hfm", [128, 2048]),
                        ("dbg_xq", [128, 2048]), ("dbg_xc", [128, 4096]),
                        ("dbg_sz", [128, 4096]), ("dbg_dbc", [96, 256]),
                        ("dbg_delta", [128, 256]), ("dbg_y", [128, 4096]),
                        ("dbg_ffn", [128, 2048]), ("dbg_B", [128, 4096]),
                        ("dbg_wq", [128, 1024]), ("dbg_stat", [128, 32]),
                        ("dbg_tokscr", [128, 2048]), ("dbg_done", [128, 128]),
                        ("dbg_scar", [128, 512]), ("dbg_dbu", [128, 1024]),
                        ("dbg_da", [128, 2048]), ("dbg_fix", [128, 64]),
                        ("dbg_wgt", [128, 2816]), ("dbg_wdt", [128, 1024]),
                        ("dbg_up", [128, 512]), ("dbg_gate", [128, 512]),
                        ("dbg_h", [128, 2048])]:
            dbg[nm] = nc.dram_tensor(nm, shp, F32, kind="ExternalOutput").ap()

    es = contextlib.ExitStack()
    with es:
        def sb(name, shape):
            return es.enter_context(nc.sbuf_tensor(name, shape, F32))

        ident_s = sb("ident_s", [128, 128])
        lngf_s = sb("lngf_s", [128, KD])
        lnbf_s = sb("lnbf_s", [128, KD])
        wxp_s = sb("wxp_s", [128, KDI * 96])
        wdt_s = sb("wdt_s", [RDT, DI])
        nA_s = sb("nA_s", [128, KDI * NST])
        convw_s = sb("convw_s", [128, KDI * 4])
        convb_s = sb("convb_s", [128, KDI])
        bdt_s = sb("bdt_s", [128, KDI])
        dpar_s = sb("dpar_s", [128, KDI])
        scar_s = sb("scar_s", [128, 512])
        halo_s = sb("halo_s", [128, KDI * 6])
        stat_s = sb("stat_s", [128, 32])
        cst_s = sb("cst_s", [128, 2])
        xtok_s = sb("xtok_s", [128, 2 * D])
        xntok_s = sb("xntok_s", [128, 2 * D])
        tokscr_s = sb("tokscr_s", [128, 2 * D])
        hfm_s = sb("hfm_s", [128, KD * TT2])
        xqfm_s = sb("xqfm_s", [128, KD * TT2])
        xc_s = sb("xc_s", [128, KDI * TT2])
        sz_s = sb("sz_s", [128, KDI * TT2])
        dtmp_s = sb("dtmp_s", [128, TT2])
        delta_s = sb("delta_s", [128, 4 * TT2])
        du_s = sb("du_s", [128, TT2])
        dbc_s = sb("dbc_s", [96, TT2])
        Bbc_s = sb("Bbc_s", [128, B2 * NST * T])
        Cbc_s = sb("Cbc_s", [128, B2 * NST * T])
        dA0_s = sb("dA0_s", [128, B2 * 8 * T])
        dA1_s = sb("dA1_s", [128, B2 * 8 * T])
        dBu0_s = sb("dBu0_s", [128, 8 * T])
        dBu1_s = sb("dBu1_s", [128, 8 * T])
        ss_s = sb("ss_s", [128, 8 * T])
        tree_s = sb("tree_s", [128, 2 * 3 * T])
        ytmp_s = sb("ytmp_s", [128, TT2])
        qs_s = sb("qs_s", [128, 3 * D])
        scl_s = sb("scl_s", [128, 2 * 176])
        ffnout_s = sb("ffnout_s", [128, 2 * D])
        xin_s = sb("xin_s", [128, 2 * (T + 3)])
        cacc_s = sb("cacc_s", [128, TT2])
        winb_s = sb("winb_s", [128, 2 * 8 * 128])
        wgu_s = sb("wgu_s", [128, 2048])
        wd_s = sb("wd_s", [128, 512])
        wout_s = sb("wout_s", [128, 1024])

        pb = [es.enter_context(nc.psum_tensor("pb%d" % i, [128, 512], F32))
              for i in range(8)]

        sdma = Sem(es.enter_context(nc.semaphore("sdma")))
        qdma = Sem(es.enter_context(nc.semaphore("qdma")))
        sT = Sem(es.enter_context(nc.semaphore("sT")))
        sV = Sem(es.enter_context(nc.semaphore("sV")))
        sS = Sem(es.enter_context(nc.semaphore("sS")))
        sG = Sem(es.enter_context(nc.semaphore("sG")))
        block = es.enter_context(nc.Block())

        P = {e: [] for e in ENGS}
        waited = {e: {} for e in ENGS}
        own = {"sync": None, "tensor": sT, "vector": sV, "scalar": sS,
               "gpsimd": sG}

        def op(eng, f, sig=None, k=1):
            if sig is not None:
                sig.n += k
                tgt = sig.h

                def thunk(e, f=f, tgt=tgt, k=k):
                    f(e).then_inc(tgt, k)
            else:
                def thunk(e, f=f):
                    f(e)
            P[eng].append(thunk)

        def W(eng, sem, v=None):
            v = sem.n if v is None else v
            if v <= 0 or sem is own[eng]:
                return
            if waited[eng].get(id(sem), -1) >= v:
                return
            waited[eng][id(sem)] = v
            P[eng].append(lambda e, s=sem.h, v=v: e.wait_ge(s, v))

        def dma(eng, out, in_, sig):
            op(eng, lambda e, o=out, i=in_: e.dma_start(out=o, in_=i), sig, 16)

        def dma_nc(eng, out, in_, sig):
            op(eng, lambda e, o=out, i=in_: e.dma_start(
                out=o, in_=i, allow_slow_non_contiguous=True), sig, 16)

        dma("sync", ident_s[:], ident_d, sdma)
        dma_nc("sync", lngf_s[:], lng_d.rearrange("(kc p) -> p kc", p=128), sdma)
        dma_nc("sync", lnbf_s[:], lnb_d.rearrange("(kc p) -> p kc", p=128), sdma)
        dma("sync", wxp_s[:].rearrange("p (dc f) -> p dc f", f=96),
            wxp_d.rearrange("(dc p) f -> p dc f", p=128), sdma)
        dma("sync", wdt_s[:], wdt_d, sdma)
        dma("sync", nA_s[:].rearrange("p (dc n) -> p dc n", n=NST),
            alog_d.rearrange("(dc p) n -> p dc n", p=128), sdma)
        dma("sync", convw_s[:].rearrange("p (dc k) -> p dc k", k=4),
            convw_d.rearrange("(dc p) o k -> p dc (o k)", p=128), sdma)
        dma_nc("sync", convb_s[:], convb_d.rearrange("(dc p) -> p dc", p=128), sdma)
        dma_nc("sync", bdt_s[:], bdt_d.rearrange("(dc p) -> p dc", p=128), sdma)
        dma_nc("sync", dpar_s[:], dpar_d.rearrange("(dc p) -> p dc", p=128), sdma)
        n_const_dma = sdma.n

        W("scalar", sdma, n_const_dma)
        op("scalar", lambda e: e.activation(qs_s[:, 0:KDI * NST], nA_s[:], AF.Exp), sS)
        W("vector", sS)
        op("vector", lambda e: e.tensor_scalar(nA_s[:], qs_s[:, 0:KDI * NST], -1.0,
                                               None, op0=ALU.mult), sV)
        op("vector", lambda e: e.memset(scar_s[:], 0.0), sV)
        op("vector", lambda e: e.memset(cst_s[:, 0:1], 1e-5), sV)
        op("vector", lambda e: e.memset(cst_s[:, 1:2], 1.0), sV)
        op("vector", lambda e: e.memset(halo_s[:], 0.0), sV)
        op("vector", lambda e: e.memset(ffnout_s[:], 0.0), sV)

        winr_out_done = {}
        for cs in range(32):
            buf = cs % 2
            if cs >= 2:
                W("sync", sdma, winr_out_done[cs - 2])
            dma("sync",
                winb_s[:, buf * 1024:(buf + 1) * 1024].rearrange(
                    "p (kc c) -> p kc c", kc=8),
                win_d[:, cs * 128:(cs + 1) * 128].rearrange(
                    "(kc p) c -> p kc c", p=128), sdma)
            W("sync", sdma, sdma.n)
            dma("sync", winr_d[cs], winb_s[:, buf * 1024:(buf + 1) * 1024], sdma)
            winr_out_done[cs] = sdma.n

        def quant_rows(src_ap, ncols, dst_d, rowchunk):
            W("sync", sT)
            dma("sync", xc_s[:, 0:ncols], src_ap, sdma)
            ld_done = sdma.n
            nblk = ncols // 16
            W("scalar", sdma, ld_done)
            W("scalar", sV)
            op("scalar", lambda e: e.activation(sz_s[:, 0:ncols], xc_s[:, 0:ncols],
                                                AF.Abs), sS)
            W("vector", sS)
            op("vector", lambda e: e.reduce_max(
                scl_s[:, 0:nblk].unsqueeze(2),
                sz_s[:, 0:ncols].rearrange("p (a b) -> p a b", b=16),
                axis=AX.X, apply_absolute_value=False), sV)
            op("vector", lambda e: e.nop(cycle_cnt=700), sV)
            op("vector", lambda e: e.tensor_scalar(
                scl_s[:, 176:176 + nblk], scl_s[:, 0:nblk], 1.0 / 6.0, 1e-12,
                op0=ALU.mult, op1=ALU.max), sV)
            op("vector", lambda e: e.nop(cycle_cnt=700), sV)
            op("vector", lambda e: e.reciprocal(scl_s[:, 0:nblk],
                                                scl_s[:, 176:176 + nblk]), sV)
            op("vector", lambda e: e.nop(cycle_cnt=700), sV)
            rsb = scl_s[:, 0:nblk].unsqueeze(2).broadcast_to([128, nblk, 16])
            op("vector", lambda e, rsb=rsb: e.tensor_tensor(
                xc_s[:, 0:ncols].rearrange("p (a b) -> p a b", b=16),
                xc_s[:, 0:ncols].rearrange("p (a b) -> p a b", b=16),
                rsb, op=ALU.mult), sV)
            W("scalar", sV)
            op("scalar", lambda e: e.activation(sz_s[:, 0:ncols], xc_s[:, 0:ncols],
                                                AF.Abs), sS)
            W("vector", sS)
            op("vector", lambda e: e.tensor_single_scalar(
                Bbc_s[:, 0:ncols], sz_s[:, 0:ncols], 0.25, op=ALU.is_gt), sV)
            for th in (0.75, 1.25, 1.75):
                op("vector", lambda e, th=th: e.scalar_tensor_tensor(
                    Bbc_s[:, 0:ncols], sz_s[:, 0:ncols], th, Bbc_s[:, 0:ncols],
                    op0=ALU.is_gt, op1=ALU.add), sV)
            op("vector", lambda e: e.tensor_single_scalar(
                Cbc_s[:, 0:ncols], sz_s[:, 0:ncols], 2.5, op=ALU.is_gt), sV)
            for th in (3.5, 5.0, 5.0):
                op("vector", lambda e, th=th: e.scalar_tensor_tensor(
                    Cbc_s[:, 0:ncols], sz_s[:, 0:ncols], th, Cbc_s[:, 0:ncols],
                    op0=ALU.is_gt, op1=ALU.add), sV)
            W("scalar", sV)
            op("scalar", lambda e: e.activation(sz_s[:, 0:ncols], xc_s[:, 0:ncols],
                                                AF.Sign), sS)
            W("vector", sS)
            op("vector", lambda e: e.scalar_tensor_tensor(
                Bbc_s[:, 0:ncols], Bbc_s[:, 0:ncols], 0.5, Cbc_s[:, 0:ncols],
                op0=ALU.mult, op1=ALU.add), sV)
            op("vector", lambda e: e.tensor_tensor(
                Cbc_s[:, 0:ncols], Bbc_s[:, 0:ncols], sz_s[:, 0:ncols],
                op=ALU.mult), sV)
            ssb = scl_s[:, 176:176 + nblk].unsqueeze(2).broadcast_to([128, nblk, 16])
            op("vector", lambda e, ssb=ssb: e.tensor_tensor(
                xc_s[:, 0:ncols].rearrange("p (a b) -> p a b", b=16),
                Cbc_s[:, 0:ncols].rearrange("p (a b) -> p a b", b=16),
                ssb, op=ALU.mult), sV)
            wq_done = sV.n
            if _DEBUG and rowchunk == 0 and ncols == D and dst_d is wgT_d:
                W("sync", sV, wq_done)
                dma("sync", dbg["dbg_wq"], xc_s[:, 0:1024], sdma)
            ntile = ncols // 128
            for r0 in range(0, ntile, 4):
                nr = min(4, ntile - r0)
                W("tensor", sV, wq_done)
                if r0 >= 4:
                    W("tensor", sV)
                for r in range(nr):
                    op("tensor", lambda e, r=r, r0=r0: e.transpose(
                        pb[0][:, r * 128:(r + 1) * 128],
                        xc_s[:, (r0 + r) * 128:(r0 + r + 1) * 128],
                        ident_s[:]), sT)
                W("vector", sT)
                W("vector", sdma)
                op("vector", lambda e, nr=nr: e.tensor_copy(
                    tokscr_s[:, 0:nr * 128], pb[0][:, 0:nr * 128]), sV)
                ev = sV.n
                for r in range(nr):
                    W("sync", sV, ev)
                    dma("sync", dst_d[(r0 + r) * 128:(r0 + r + 1) * 128,
                                      rowchunk * 128:(rowchunk + 1) * 128],
                        tokscr_s[:, r * 128:(r + 1) * 128], sdma)
            return sdma.n

        for rc in range(KI):
            quant_rows(wg_d[rc * 128:(rc + 1) * 128, :], D, wgT_d, rc)
        for rc in range(KI):
            quant_rows(wu_d[rc * 128:(rc + 1) * 128, :], D, wuT_d, rc)
        for rc in range(KD):
            quant_rows(wdn_d[rc * 128:(rc + 1) * 128, :], IFF, wdT_d, rc)
        prep_spill_done = sdma.n
        if _DEBUG:
            W("sync", sdma, prep_spill_done)
            dma("sync", dbg["dbg_wgt"], wgT_d[0:128, :], sdma)
            dma("sync", dbg["dbg_wdt"], wdT_d[0:128, :], sdma)

        chunk_out_read = 0
        prev_scan_done = 0
        prev_wout_done = 0
        prev_pb3_free = 0
        prev_bc_done = 0

        for c in range(min(NCH, _NCHUNKS)):
          try:
            def ck(k):
                if _PHASES < k:
                    raise _StopRec()
            W("sync", sV, chunk_out_read)
            for b in range(B2):
                dma("sync", xtok_s[:, b * D:(b + 1) * D], x_d[b, c * T:(c + 1) * T, :],
                    sdma)
            x_loaded = sdma.n
            if c == 0:
                W("sync", sdma, prep_spill_done)

            for b in range(B2):
                xsl = xtok_s[:, b * D:(b + 1) * D]
                xnsl = xntok_s[:, b * D:(b + 1) * D]
                tsl = tokscr_s[:, b * D:(b + 1) * D]
                s0 = 16 * b
                W("vector", sdma, x_loaded)
                op("vector", lambda e, xsl=xsl, s0=s0: e.tensor_reduce(
                    stat_s[:, s0:s0 + 1], xsl, axis=AX.X, op=ALU.add), sV)
                W("scalar", sV)
                op("scalar", lambda e, s0=s0: e.activation(
                    stat_s[:, s0 + 1:s0 + 2], stat_s[:, s0:s0 + 1], AF.Copy,
                    scale=1.0 / D), sS)
                W("vector", sS)
                op("vector", lambda e, xsl=xsl, tsl=tsl, s0=s0: e.tensor_scalar(
                    tsl, xsl, stat_s[:, s0 + 1:s0 + 2], None, op0=ALU.subtract), sV)
                W("scalar", sV)
                op("scalar", lambda e, tsl=tsl, s0=s0, b=b: e.activation(
                    qs_s[:, b * D:(b + 1) * D], tsl, AF.Square,
                    accum_out=stat_s[:, s0 + 2:s0 + 3]), sS)
                op("scalar", lambda e, s0=s0: e.activation(
                    stat_s[:, s0 + 3:s0 + 4], stat_s[:, s0 + 2:s0 + 3], AF.Sqrt,
                    scale=1.0 / D, bias=cst_s[:, 0:1]), sS)
                W("vector", sS)
                op("vector", lambda e, s0=s0: e.reciprocal(
                    stat_s[:, s0 + 4:s0 + 5], stat_s[:, s0 + 3:s0 + 4]), sV)
                op("vector", lambda e: e.nop(cycle_cnt=700), sV)
                op("vector", lambda e, xnsl=xnsl, tsl=tsl, s0=s0: e.tensor_scalar(
                    xnsl, tsl, stat_s[:, s0 + 4:s0 + 5], None, op0=ALU.mult), sV)
                op("vector", lambda e, xsl=xsl, xnsl=xnsl: e.tensor_tensor(
                    xsl, xsl, xnsl, op=ALU.add), sV)
                op("vector", lambda e, xnsl=xnsl, s0=s0: e.tensor_reduce(
                    stat_s[:, s0 + 5:s0 + 6], xnsl, axis=AX.X, op=ALU.add), sV)
                W("scalar", sV)
                op("scalar", lambda e, s0=s0: e.activation(
                    stat_s[:, s0 + 6:s0 + 7], stat_s[:, s0 + 5:s0 + 6], AF.Copy,
                    scale=1.0 / D), sS)
                W("vector", sS)
                op("vector", lambda e, xnsl=xnsl, tsl=tsl, s0=s0: e.tensor_scalar(
                    tsl, xnsl, stat_s[:, s0 + 6:s0 + 7], None, op0=ALU.subtract), sV)
                W("scalar", sV)
                op("scalar", lambda e, tsl=tsl, s0=s0, b=b: e.activation(
                    qs_s[:, b * D:(b + 1) * D], tsl, AF.Square,
                    accum_out=stat_s[:, s0 + 7:s0 + 8]), sS)
                op("scalar", lambda e, s0=s0: e.activation(
                    stat_s[:, s0 + 8:s0 + 9], stat_s[:, s0 + 7:s0 + 8], AF.Sqrt,
                    scale=1.0 / D, bias=cst_s[:, 0:1]), sS)
                W("vector", sS)
                op("vector", lambda e, s0=s0: e.reciprocal(
                    stat_s[:, s0 + 9:s0 + 10], stat_s[:, s0 + 8:s0 + 9]), sV)
                op("vector", lambda e: e.nop(cycle_cnt=700), sV)
                op("vector", lambda e, tsl=tsl, s0=s0: e.tensor_scalar(
                    tsl, tsl, stat_s[:, s0 + 9:s0 + 10], None, op0=ALU.mult), sV)
            h1_done = sV.n
            if _DEBUG and c == 0:
                W("sync", sV, h1_done)
                dma("sync", dbg["dbg_xn"], xntok_s[:], sdma)
                dma("sync", dbg["dbg_stat"], stat_s[:], sdma)
                for _e in ("vector", "scalar", "tensor"):
                    W(_e, sdma)
            ck(1)

            for half in range(2):
                W("tensor", sV, h1_done)
                if half == 1:
                    W("tensor", sV)
                for b in range(B2):
                    for r in range(4):
                        kc = half * 4 + r
                        op("tensor", lambda e, b=b, r=r, kc=kc: e.transpose(
                            pb[b][:, r * 128:(r + 1) * 128],
                            tokscr_s[:, b * D + kc * 128:b * D + (kc + 1) * 128],
                            ident_s[:]), sT)
                W("vector", sT)
                for b in range(B2):
                    for r in range(4):
                        kc = half * 4 + r
                        op("vector", lambda e, b=b, r=r, kc=kc: e.tensor_scalar(
                            hfm_s[:, kc * TT2 + b * T:kc * TT2 + (b + 1) * T],
                            pb[b][:, r * 128:(r + 1) * 128],
                            lngf_s[:, kc:kc + 1], lnbf_s[:, kc:kc + 1],
                            op0=ALU.mult, op1=ALU.add), sV)
            hfm_done = sV.n
            ck(1.3)

            for b in range(B2):
                xnsl = xntok_s[:, b * D:(b + 1) * D]
                tsl = tokscr_s[:, b * D:(b + 1) * D]
                q0 = qs_s[:, 0 * D:1 * D]
                q1 = qs_s[:, 1 * D:2 * D]
                q2 = qs_s[:, 2 * D:3 * D]
                W("scalar", sV)
                W("vector", sT)
                op("scalar", lambda e, xnsl=xnsl, q0=q0: e.activation(
                    q0, xnsl, AF.Abs), sS)
                W("vector", sS)
                op("vector", lambda e, q0=q0: e.reduce_max(
                    scl_s[:, 0:64].unsqueeze(2),
                    q0.rearrange("p (a b) -> p a b", b=16),
                    axis=AX.X, apply_absolute_value=False), sV)
                op("vector", lambda e: e.nop(cycle_cnt=700), sV)
                op("vector", lambda e: e.tensor_scalar(
                    scl_s[:, 176:176 + 64], scl_s[:, 0:64], 1.0 / 6.0, 1e-12,
                    op0=ALU.mult, op1=ALU.max), sV)
                op("vector", lambda e: e.nop(cycle_cnt=700), sV)
                op("vector", lambda e: e.reciprocal(
                    scl_s[:, 0:64], scl_s[:, 176:176 + 64]), sV)
                op("vector", lambda e: e.nop(cycle_cnt=700), sV)
                rsb = scl_s[:, 0:64].unsqueeze(2).broadcast_to([128, 64, 16])
                op("vector", lambda e, xnsl=xnsl, tsl=tsl, rsb=rsb: e.tensor_tensor(
                    tsl.rearrange("p (a b) -> p a b", b=16),
                    xnsl.rearrange("p (a b) -> p a b", b=16), rsb, op=ALU.mult), sV)
                W("scalar", sV)
                op("scalar", lambda e, tsl=tsl, q0=q0: e.activation(
                    q0, tsl, AF.Abs), sS)
                W("vector", sS)
                op("vector", lambda e, q0=q0, q1=q1: e.tensor_single_scalar(
                    q1, q0, 0.25, op=ALU.is_gt), sV)
                for th in (0.75, 1.25, 1.75):
                    op("vector", lambda e, th=th, q0=q0, q1=q1: e.scalar_tensor_tensor(
                        q1, q0, th, q1, op0=ALU.is_gt, op1=ALU.add), sV)
                op("vector", lambda e, q0=q0, q2=q2: e.tensor_single_scalar(
                    q2, q0, 2.5, op=ALU.is_gt), sV)
                for th in (3.5, 5.0, 5.0):
                    op("vector", lambda e, th=th, q0=q0, q2=q2: e.scalar_tensor_tensor(
                        q2, q0, th, q2, op0=ALU.is_gt, op1=ALU.add), sV)
                W("scalar", sV)
                op("scalar", lambda e, tsl=tsl, q0=q0: e.activation(
                    q0, tsl, AF.Sign), sS)
                W("vector", sS)
                op("vector", lambda e, q1=q1, q2=q2: e.scalar_tensor_tensor(
                    q1, q1, 0.5, q2, op0=ALU.mult, op1=ALU.add), sV)
                op("vector", lambda e, q0=q0, q1=q1, q2=q2: e.tensor_tensor(
                    q2, q1, q0, op=ALU.mult), sV)
                ssb = scl_s[:, 176:176 + 64].unsqueeze(2).broadcast_to([128, 64, 16])
                op("vector", lambda e, tsl=tsl, q2=q2, ssb=ssb: e.tensor_tensor(
                    tsl.rearrange("p (a b) -> p a b", b=16),
                    q2.rearrange("p (a b) -> p a b", b=16), ssb, op=ALU.mult), sV)
            xq_done = sV.n
            if _DEBUG and c == 0:
                W("sync", sV, xq_done)
                dma("sync", dbg["dbg_tokscr"], tokscr_s[:], sdma)
                for _e in ("vector", "scalar", "tensor"):
                    W(_e, sdma)
            ck(1.6)

            for half in range(2):
                W("tensor", sV, xq_done)
                if half == 1:
                    W("tensor", sS)
                for b in range(B2):
                    for r in range(4):
                        kc = half * 4 + r
                        op("tensor", lambda e, b=b, r=r, kc=kc: e.transpose(
                            pb[b][:, r * 128:(r + 1) * 128],
                            tokscr_s[:, b * D + kc * 128:b * D + (kc + 1) * 128],
                            ident_s[:]), sT)
                W("scalar", sT)
                for b in range(B2):
                    for r in range(4):
                        kc = half * 4 + r
                        op("scalar", lambda e, b=b, r=r, kc=kc: e.copy(
                            xqfm_s[:, kc * TT2 + b * T:kc * TT2 + (b + 1) * T],
                            pb[b][:, r * 128:(r + 1) * 128]), sS)
            xqfm_done = sS.n

            if _DEBUG and c == 0:
                W("sync", sV, xq_done)
                dma("sync", dbg["dbg_hfm"], hfm_s[:], sdma)
                W("sync", sS, xqfm_done)
                dma("sync", dbg["dbg_xq"], xqfm_s[:], sdma)
                for _e in ("vector", "scalar", "tensor"):
                    W(_e, sdma)
            ck(2)

            win_mm_done = {}
            if c > 0:
                win_mm_done[-2] = win_mm_done[-1] = prev_win_last
            evac_done = {}
            xproj_emitted = []

            def emit_xproj(dc):
                W("tensor", sS, silu_done[dc])
                op("tensor", lambda e, dc=dc: e.matmul(
                    pb[2][0:96, 0:TT2],
                    wxp_s[:, dc * 96:(dc + 1) * 96],
                    xc_s[:, dc * TT2:(dc + 1) * TT2],
                    start=(dc == 0), stop=(dc == KDI - 1)), sT)
                xproj_emitted.append(dc)

            silu_done = {}
            for cs in range(32):
                buf = cs % 2
                if cs >= 2 or c > 0:
                    W("sync", sT, win_mm_done[cs - 2])
                dma("sync",
                    winb_s[:, buf * 1024:(buf + 1) * 1024],
                    winr_d[cs], sdma)
                win_ld = sdma.n
                if 2 <= cs < 18:
                    emit_xproj(cs - 2)
                W("tensor", sdma, win_ld)
                W("tensor", sV, hfm_done)
                W("tensor", sS, xqfm_done)
                if cs >= 2:
                    W("tensor", evac_done[cs - 2][1], evac_done[cs - 2][0])
                for kc in range(8):
                    op("tensor", lambda e, buf=buf, kc=kc, cs=cs: e.matmul(
                        pb[buf][:, 0:TT2],
                        winb_s[:, buf * 1024 + kc * 128:buf * 1024 + (kc + 1) * 128],
                        hfm_s[:, kc * TT2:(kc + 1) * TT2],
                        start=(kc == 0), stop=(kc == 7)), sT)
                win_mm_done[cs] = sT.n
                if cs < 16:
                    dc = cs
                    W("vector", sT, win_mm_done[cs])
                    if dc >= 1:
                        W("vector", sS, silu_done[dc - 1])
                    for b in range(B2):
                        op("vector", lambda e, b=b, dc=dc: e.tensor_copy(
                            xin_s[:, b * (T + 3):b * (T + 3) + 3],
                            halo_s[:, dc * 6 + b * 3:dc * 6 + (b + 1) * 3]), sV)
                        op("vector", lambda e, b=b, buf=buf: e.tensor_copy(
                            xin_s[:, b * (T + 3) + 3:(b + 1) * (T + 3)],
                            pb[buf][:, b * T:(b + 1) * T]), sV)
                    evac_done[cs] = (sV.n, sV)
                    for b in range(B2):
                        op("vector", lambda e, b=b, dc=dc: e.tensor_copy(
                            halo_s[:, dc * 6 + b * 3:dc * 6 + (b + 1) * 3],
                            xin_s[:, b * (T + 3) + T:(b + 1) * (T + 3)]), sV)
                        op("vector", lambda e, b=b, dc=dc: e.tensor_scalar(
                            cacc_s[:, b * T:(b + 1) * T],
                            xin_s[:, b * (T + 3):b * (T + 3) + T],
                            convw_s[:, dc * 4:dc * 4 + 1], None, op0=ALU.mult), sV)
                        for k in range(1, 4):
                            op("vector", lambda e, b=b, dc=dc, k=k:
                               e.scalar_tensor_tensor(
                                   cacc_s[:, b * T:(b + 1) * T],
                                   xin_s[:, b * (T + 3) + k:b * (T + 3) + k + T],
                                   convw_s[:, dc * 4 + k:dc * 4 + k + 1],
                                   cacc_s[:, b * T:(b + 1) * T],
                                   op0=ALU.mult, op1=ALU.add), sV)
                    W("scalar", sV)
                    op("scalar", lambda e, dc=dc: e.activation(
                        xc_s[:, dc * TT2:(dc + 1) * TT2], cacc_s[:, 0:TT2],
                        AF.Silu, bias=convb_s[:, dc:dc + 1]), sS)
                    silu_done[dc] = sS.n
                else:
                    zc = cs - 16
                    W("scalar", sT, win_mm_done[cs])
                    op("scalar", lambda e, buf=buf, zc=zc: e.activation(
                        sz_s[:, zc * TT2:(zc + 1) * TT2], pb[buf][:, 0:TT2],
                        AF.Silu), sS)
                    evac_done[cs] = (sS.n, sS)
            for dc in range(KDI):
                if dc not in xproj_emitted:
                    emit_xproj(dc)
            xproj_done = sT.n
            prev_win_last = win_mm_done[31]

            W("vector", sT, xproj_done)
            W("vector", sdma, prev_bc_done)
            op("vector", lambda e: e.tensor_copy(dbc_s[:], pb[2][0:96, 0:TT2]), sV)
            dbc_done = sV.n

            if _DEBUG and c == 0:
                W("sync", sV, dbc_done)
                dma("sync", dbg["dbg_xc"], xc_s[:], sdma)
                dma("sync", dbg["dbg_sz"], sz_s[:], sdma)
                dma("sync", dbg["dbg_dbc"], dbc_s[:], sdma)
                for _e in ("vector", "scalar", "tensor"):
                    W(_e, sdma)

            W("sync", sV, dbc_done)
            for b in range(B2):
                for n in range(NST):
                    dma("sync",
                        Bbc_s[:, (b * NST + n) * T:(b * NST + n + 1) * T],
                        dbc_s[64 + n:65 + n, b * T:(b + 1) * T]
                        .unsqueeze(1).broadcast_to([1, 128, T]), sdma)
                    dma("sync",
                        Cbc_s[:, (b * NST + n) * T:(b * NST + n + 1) * T],
                        dbc_s[80 + n:81 + n, b * T:(b + 1) * T]
                        .unsqueeze(1).broadcast_to([1, 128, T]), sdma)
            ck(3)
            bc_done = sdma.n
            prev_bc_done = bc_done

            dA_bufs = (dA0_s, dA1_s)
            dBu_bufs = (dBu0_s, dBu1_s)
            dA_scan_done = {-2: prev_scan_done, -1: prev_scan_done}
            delta_used = {k: prev_scan_done for k in range(-4, 0)}
            scan_it = [0]
            dbuf_guard = {0: prev_scan_done, 1: prev_scan_done}
            delta_mm_done = {}
            pb3_free = {-1: prev_pb3_free}
            delta_ready = {}

            def emit_delta(dc):
                dbuf = delta_s[:, (dc % 4) * TT2:(dc % 4 + 1) * TT2]
                W("tensor", sS, pb3_free[dc - 1])
                W("tensor", sV, dbc_done)
                op("tensor", lambda e: e.matmul(
                    pb[3][:, 0:TT2],
                    wdt_s[:, dc * 128:(dc + 1) * 128],
                    dbc_s[0:64, :], start=True, stop=True), sT)
                delta_mm_done[dc] = sT.n
                W("scalar", sT, delta_mm_done[dc])
                W("scalar", sV, delta_used[dc - 4])
                op("scalar", lambda e: e.activation(
                    dtmp_s[:], pb[3][:, 0:TT2], AF.Exp,
                    bias=bdt_s[:, dc:dc + 1]), sS)
                pb3_free[dc] = sS.n
                op("scalar", lambda e, dbuf=dbuf: e.activation(
                    dbuf, dtmp_s[:], AF.Ln, bias=cst_s[:, 1:2]), sS)
                delta_ready[dc] = sS.n

            def emit_scan_dc(dc):
                dbuf = delta_s[:, (dc % 4) * TT2:(dc % 4 + 1) * TT2]
                W("vector", sS, delta_ready[dc])
                op("vector", lambda e, dbuf=dbuf: e.tensor_tensor(
                    du_s[:], dbuf, xc_s[:, dc * TT2:(dc + 1) * TT2],
                    op=ALU.mult), sV)
                du_done = sV.n
                for half in range(2):
                    ab = dA_bufs[(2 * dc + half) % 2]
                    W("scalar", sV, dA_scan_done[(dc * 2 + half) - 2])
                    dbv2 = dbuf.rearrange("p (b t) -> p b t", b=2)
                    for n8 in range(8):
                        n = half * 8 + n8
                        o = ab[:].rearrange("p (b n t) -> p b n t", b=2, n=8)[:, :, n8, :]
                        op("scalar", lambda e, o=o, dbv2=dbv2, dc=dc, n=n:
                           e.activation(o, dbv2, AF.Exp,
                                        scale=nA_s[:, dc * NST + n:dc * NST + n + 1]),
                           sS)
                    dA_done = sS.n
                    W("vector", sS, dA_done)
                    W("vector", sdma, bc_done)
                    scoff = dc * 32 + half * 16
                    for b in range(B2):
                        it = scan_it[0]
                        scan_it[0] += 1
                        dbuf_i = dBu_bufs[it % 2]
                        scv_b = scar_s[:, scoff + b * 8:scoff + b * 8 + 8]
                        W("gpsimd", sV, du_done)
                        W("gpsimd", sdma, bc_done)
                        W("gpsimd", sV, dbuf_guard[it % 2])
                        op("gpsimd", lambda e, b=b, half=half, dbuf_i=dbuf_i:
                           e.tensor_tensor(
                               dbuf_i[:].rearrange("p (n t) -> p n t", n=8),
                               du_s[:, b * T:(b + 1) * T].unsqueeze(1)
                               .broadcast_to([128, 8, T]),
                               Bbc_s[:, (b * NST + half * 8) * T:
                                     (b * NST + half * 8 + 8) * T]
                               .rearrange("p (n t) -> p n t", n=8),
                               op=ALU.mult), sG)
                        W("vector", sG)
                        for n8 in range(8):
                            op("vector", lambda e, ab=ab, b=b, n8=n8,
                               dbuf_i=dbuf_i, scol=scoff + b * 8 + n8:
                               e.tensor_tensor_scan(
                                   ss_s[:, n8 * T:(n8 + 1) * T],
                                   ab[:, b * 8 * T + n8 * T:
                                      b * 8 * T + (n8 + 1) * T],
                                   dbuf_i[:, n8 * T:(n8 + 1) * T],
                                   scar_s[:, scol:scol + 1],
                                   op0=ALU.mult, op1=ALU.add), sV)
                        if half == 1 and b == 0:
                            for b0 in range(B2):
                                tb0 = 3 * T * b0
                                op("vector", lambda e, b0=b0, tb0=tb0:
                                   e.tensor_tensor(
                                       ytmp_s[:, b0 * T:(b0 + 1) * T],
                                       tree_s[:, tb0:tb0 + T],
                                       tree_s[:, tb0 + T:tb0 + 2 * T],
                                       op=ALU.add), sV)
                        ssv = ss_s[:].rearrange("p (n t) -> p n t", n=8)
                        op("vector", lambda e, b=b, half=half, ssv=ssv,
                           dbuf_i=dbuf_i: e.tensor_tensor(
                               dbuf_i[:].rearrange("p (n t) -> p n t", n=8), ssv,
                               Cbc_s[:, (b * NST + half * 8) * T:
                                     (b * NST + half * 8 + 8) * T]
                               .rearrange("p (n t) -> p n t", n=8),
                               op=ALU.mult), sV)
                        op("vector", lambda e, ssv=ssv, scv_b=scv_b: e.tensor_copy(
                            scv_b, ssv[:, :, T - 1]), sV)
                        tb = 3 * T * b
                        op("vector", lambda e, dbuf_i=dbuf_i: e.tensor_tensor(
                            ss_s[:, 0:4 * T],
                            dbuf_i[:].rearrange("p (a two t) -> p a two t", two=2,
                                                t=T)[:, :, 0, :],
                            dbuf_i[:].rearrange("p (a two t) -> p a two t", two=2,
                                                t=T)[:, :, 1, :], op=ALU.add), sV)
                        op("vector", lambda e, tb=tb: e.tensor_tensor(
                            tree_s[:, tb:tb + 2 * T],
                            ss_s[:, 0:4 * T].rearrange(
                                "p (a two t) -> p a two t", two=2, t=T)[:, :, 0, :],
                            ss_s[:, 0:4 * T].rearrange(
                                "p (a two t) -> p a two t", two=2, t=T)[:, :, 1, :],
                            op=ALU.add), sV)
                        dbuf_guard[it % 2] = sV.n
                    if half == 1:
                        op("vector", lambda e: e.nop(cycle_cnt=700), sV)
                        for b in range(B2):
                            tb = 3 * T * b
                            op("vector", lambda e, b=b, tb=tb: e.tensor_tensor(
                                tree_s[:, tb + 2 * T:tb + 3 * T],
                                tree_s[:, tb:tb + T],
                                tree_s[:, tb + T:tb + 2 * T], op=ALU.add), sV)
                        op("vector", lambda e: e.nop(cycle_cnt=700), sV)
                        for b in range(B2):
                            tb = 3 * T * b
                            op("vector", lambda e, b=b, tb=tb: e.tensor_tensor(
                                ytmp_s[:, b * T:(b + 1) * T],
                                ytmp_s[:, b * T:(b + 1) * T],
                                tree_s[:, tb + 2 * T:tb + 3 * T], op=ALU.add), sV)
                        op("vector", lambda e: e.nop(cycle_cnt=700), sV)
                        op("vector", lambda e, dc=dc: e.scalar_tensor_tensor(
                            ss_s[:, 0:TT2],
                            xc_s[:, dc * TT2:(dc + 1) * TT2],
                            dpar_s[:, dc:dc + 1],
                            ytmp_s[:], op0=ALU.mult, op1=ALU.add), sV)
                        op("vector", lambda e: e.nop(cycle_cnt=700), sV)
                        op("vector", lambda e, dc=dc: e.tensor_tensor(
                            xc_s[:, dc * TT2:(dc + 1) * TT2],
                            ss_s[:, 0:TT2],
                            sz_s[:, dc * TT2:(dc + 1) * TT2],
                            op=ALU.mult), sV)
                    dA_scan_done[dc * 2 + half] = sV.n
                delta_used[dc] = sV.n

            down_evac = {}

            def emit_ffn_j(j):
                joff, w = JSLABS[j]
                nkc = w // 128
                for wi, (wT, pbs) in enumerate(((wgT_d, (4, 5)), (wuT_d, (6, 7)))):
                    for kh in range(2):
                        W("sync", sT)
                        for kc4 in range(4):
                            kc = kh * 4 + kc4
                            dma("sync", wgu_s[:, kc4 * 512:kc4 * 512 + w],
                                wT[kc * 128:(kc + 1) * 128, joff:joff + w], sdma)
                        W("tensor", sdma)
                        W("tensor", sS, xqfm_done)
                        if j > 0 and wi == 0 and kh == 0:
                            W("tensor", sV, down_evac[j - 1])
                        for b in range(B2):
                            for kc4 in range(4):
                                kc = kh * 4 + kc4
                                op("tensor", lambda e, b=b, kc=kc, kc4=kc4, w=w,
                                   pbx=pbs[b], kh=kh: e.matmul(
                                       pb[pbx][:, 0:w],
                                       xqfm_s[:, kc * TT2 + b * T:
                                              kc * TT2 + (b + 1) * T],
                                       wgu_s[:, kc4 * 512:kc4 * 512 + w],
                                       start=(kh == 0 and kc4 == 0),
                                       stop=(kh == 1 and kc4 == 3)), sT)
                gu_done = sT.n
                for b in range(B2):
                    tsl = tokscr_s[:, b * D:b * D + w]
                    q0 = qs_s[:, 0:w]
                    q1 = qs_s[:, D:D + w]
                    q2 = qs_s[:, 2 * D:2 * D + w]
                    W("scalar", sT, gu_done)
                    op("scalar", lambda e, b=b, tsl=tsl: e.activation(
                        tsl, pb[4 + b][:, 0:w], AF.Silu), sS)
                    if _DEBUG and c == 0 and j == 0 and b == 0:
                        W("vector", sT, gu_done)
                        op("vector", lambda e, w=w: e.tensor_copy(
                            qs_s[:, 0:w], pb[6][:, 0:w]), sV)
                        op("vector", lambda e, w=w: e.tensor_copy(
                            qs_s[:, D:D + w], pb[4][:, 0:w]), sV)
                        W("sync", sV)
                        dma("sync", dbg["dbg_up"], qs_s[:, 0:512], sdma)
                        dma("sync", dbg["dbg_gate"], qs_s[:, D:D + 512], sdma)
                        W("vector", sdma)
                    W("vector", sS)
                    op("vector", lambda e, b=b, tsl=tsl: e.tensor_tensor(
                        tsl, tsl, pb[6 + b][:, 0:w], op=ALU.mult), sV)
                    nblk = w // 16
                    W("scalar", sV)
                    op("scalar", lambda e, tsl=tsl, q0=q0: e.activation(
                        q0, tsl, AF.Abs), sS)
                    W("vector", sS)
                    op("vector", lambda e, q0=q0, nblk=nblk: e.reduce_max(
                        scl_s[:, 0:nblk].unsqueeze(2),
                        q0.rearrange("p (a b) -> p a b", b=16),
                        axis=AX.X, apply_absolute_value=False), sV)
                    op("vector", lambda e: e.nop(cycle_cnt=700), sV)
                    op("vector", lambda e, nblk=nblk: e.tensor_scalar(
                        scl_s[:, 176:176 + nblk], scl_s[:, 0:nblk], 1.0 / 6.0, 1e-12,
                        op0=ALU.mult, op1=ALU.max), sV)
                    op("vector", lambda e: e.nop(cycle_cnt=700), sV)
                    op("vector", lambda e, nblk=nblk: e.reciprocal(
                        scl_s[:, 0:nblk], scl_s[:, 176:176 + nblk]), sV)
                    op("vector", lambda e: e.nop(cycle_cnt=700), sV)
                    rsb = scl_s[:, 0:nblk].unsqueeze(2).broadcast_to([128, nblk, 16])
                    op("vector", lambda e, tsl=tsl, rsb=rsb, nblk=nblk:
                       e.tensor_tensor(
                           tsl.rearrange("p (a b) -> p a b", b=16),
                           tsl.rearrange("p (a b) -> p a b", b=16), rsb,
                           op=ALU.mult), sV)
                    W("scalar", sV)
                    op("scalar", lambda e, tsl=tsl, q0=q0: e.activation(
                        q0, tsl, AF.Abs), sS)
                    W("vector", sS)
                    op("vector", lambda e, q0=q0, q1=q1: e.tensor_single_scalar(
                        q1, q0, 0.25, op=ALU.is_gt), sV)
                    for th in (0.75, 1.25, 1.75):
                        op("vector", lambda e, th=th, q0=q0, q1=q1:
                           e.scalar_tensor_tensor(q1, q0, th, q1,
                                                  op0=ALU.is_gt, op1=ALU.add), sV)
                    op("vector", lambda e, q0=q0, q2=q2: e.tensor_single_scalar(
                        q2, q0, 2.5, op=ALU.is_gt), sV)
                    for th in (3.5, 5.0, 5.0):
                        op("vector", lambda e, th=th, q0=q0, q2=q2:
                           e.scalar_tensor_tensor(q2, q0, th, q2,
                                                  op0=ALU.is_gt, op1=ALU.add), sV)
                    W("scalar", sV)
                    op("scalar", lambda e, tsl=tsl, q0=q0: e.activation(
                        q0, tsl, AF.Sign), sS)
                    W("vector", sS)
                    op("vector", lambda e, q1=q1, q2=q2: e.scalar_tensor_tensor(
                        q1, q1, 0.5, q2, op0=ALU.mult, op1=ALU.add), sV)
                    op("vector", lambda e, q0=q0, q1=q1, q2=q2: e.tensor_tensor(
                        q2, q1, q0, op=ALU.mult), sV)
                    ssb = scl_s[:, 176:176 + nblk].unsqueeze(2).broadcast_to(
                        [128, nblk, 16])
                    op("vector", lambda e, tsl=tsl, q2=q2, ssb=ssb, nblk=nblk:
                       e.tensor_tensor(
                           tsl.rearrange("p (a b) -> p a b", b=16),
                           q2.rearrange("p (a b) -> p a b", b=16), ssb,
                           op=ALU.mult), sV)
                    hq_done = sV.n
                    if _DEBUG and c == 0 and j == 0 and b == 0:
                        W("sync", sV, hq_done)
                        dma("sync", dbg["dbg_h"], tokscr_s[:], sdma)
                        W("vector", sdma)
                        W("scalar", sdma)
                    W("tensor", sV, hq_done)
                    for r in range(nkc):
                        op("tensor", lambda e, b=b, r=r, tsl=tsl: e.transpose(
                            pb[4 + b][:, r * 128:(r + 1) * 128],
                            tsl[:, r * 128:(r + 1) * 128], ident_s[:]), sT)
                    W("vector", sT)
                    op("vector", lambda e, b=b, nkc=nkc: e.tensor_copy(
                        hfm_s[:, b * 1024:b * 1024 + nkc * 128],
                        pb[4 + b][:, 0:nkc * 128]), sV)
                    hqT_done = sV.n
                    W("tensor", sV, hqT_done)
                    for ds in range(2):
                        for kc4 in range(nkc):
                            W("sync", sT)
                            kc = joff // 128 + kc4
                            dma("sync", wd_s[:, 0:512],
                                wdT_d[kc * 128:(kc + 1) * 128,
                                      ds * 512:(ds + 1) * 512], sdma)
                            W("tensor", sdma, sdma.n)
                            op("tensor", lambda e, b=b, kc4=kc4, ds=ds,
                               nkc=nkc: e.matmul(
                                   pb[ds][:, 0:512],
                                   hfm_s[:, b * 1024 + kc4 * 128:
                                         b * 1024 + (kc4 + 1) * 128],
                                   wd_s[:, 0:512],
                                   start=(kc4 == 0), stop=(kc4 == nkc - 1)), sT)
                    dn_done = sT.n
                    W("vector", sT, dn_done)
                    for ds in range(2):
                        op("vector", lambda e, b=b, ds=ds: e.tensor_tensor(
                            ffnout_s[:, b * D + ds * 512:b * D + (ds + 1) * 512],
                            pb[ds][:, 0:512],
                            ffnout_s[:, b * D + ds * 512:b * D + (ds + 1) * 512],
                            op=ALU.add), sV)
                down_evac[j] = sV.n

            ffn_after = {0: 0, 3: 1, 5: 2, 7: 3, 9: 4, 11: 5}
            emit_delta(0)
            emit_delta(1)
            for dc in range(KDI):
                if dc + 2 < KDI:
                    emit_delta(dc + 2)
                emit_scan_dc(dc)
                if dc in ffn_after:
                    emit_ffn_j(ffn_after[dc])
            scan_all_done = sV.n
            prev_scan_done = scan_all_done
            prev_pb3_free = pb3_free[KDI - 1]

            mm_done = {(-1, 0): prev_wout_done, (-1, 1): prev_wout_done}
            ld_done2 = {}
            for dc in range(KDI):
                for ds in range(2):
                    W("scalar", sT, mm_done[(dc - 1, ds)])
                    dma("scalar", wout_s[:, ds * 512:(ds + 1) * 512],
                        wout_d[dc * 128:(dc + 1) * 128,
                               ds * 512:(ds + 1) * 512], qdma)
                    ld_done2[(dc, ds)] = qdma.n
                for ds in range(2):
                    W("tensor", qdma, ld_done2[(dc, ds)])
                    W("tensor", sV, scan_all_done)
                    for b in range(B2):
                        op("tensor", lambda e, dc=dc, b=b, ds=ds: e.matmul(
                            pb[4 + 2 * b + ds][:, 0:512],
                            xc_s[:, dc * TT2 + b * T:dc * TT2 + (b + 1) * T],
                            wout_s[:, ds * 512:(ds + 1) * 512],
                            start=(dc == 0), stop=(dc == KDI - 1)), sT)
                    mm_done[(dc, ds)] = sT.n
            wout_done = sT.n
            prev_wout_done = wout_done

            if _DEBUG and c == 0:
                W("sync", sV, scan_all_done)
                dma("sync", dbg["dbg_y"], xc_s[:], sdma)
                dma("sync", dbg["dbg_B"], Bbc_s[:], sdma)
                dma("sync", dbg["dbg_ffn"], ffnout_s[:], sdma)
                W("sync", sS)
                dma("sync", dbg["dbg_delta"], delta_s[:, TT2:2 * TT2], sdma)
                for _e in ("vector", "scalar", "tensor"):
                    W(_e, sdma)

            ck(4)
            W("vector", sT, wout_done)
            for b in range(B2):
                for ds in range(2):
                    op("vector", lambda e, b=b, ds=ds: e.tensor_tensor(
                        xntok_s[:, b * D + ds * 512:b * D + (ds + 1) * 512],
                        pb[4 + 2 * b + ds][:, 0:512],
                        ffnout_s[:, b * D + ds * 512:b * D + (ds + 1) * 512],
                        op=ALU.add), sV)
                op("vector", lambda e, b=b: e.tensor_tensor(
                    xntok_s[:, b * D:(b + 1) * D],
                    xntok_s[:, b * D:(b + 1) * D],
                    xtok_s[:, b * D:(b + 1) * D], op=ALU.add), sV)
                op("vector", lambda e, b=b: e.memset(
                    ffnout_s[:, b * D:(b + 1) * D], 0.0), sV)
            asm_done = sV.n
            chunk_out_read = asm_done
            W("sync", sV, asm_done)
            for b in range(B2):
                dma("sync", out_d[b, c * T:(c + 1) * T, :],
                    xntok_s[:, b * D:(b + 1) * D], sdma)
          except _StopRec:
            pass

        if _DEBUG:
            for _e in ("vector", "scalar", "tensor"):
                W("sync", own[_e])
            dma("sync", dbg["dbg_done"], ident_s[:], sdma)
        final_dma = sdma.n
        P["sync"].append(lambda e, v=final_dma: e.wait_ge(sdma.h, v))

        @block.sync
        def _(sync):
            for f in P["sync"]:
                f(sync)

        @block.tensor
        def _(tensor):
            for f in P["tensor"]:
                f(tensor)

        @block.vector
        def _(vector):
            for f in P["vector"]:
                f(vector)

        @block.scalar
        def _(scalar):
            for f in P["scalar"]:
                f(scalar)

        @block.gpsimd
        def _(gpsimd):
            for f in P["gpsimd"]:
                f(gpsimd)

    return nc


_NC_CACHE = {}


def kernel(**inputs):
    key = "nc"
    if key not in _NC_CACHE:
        _NC_CACHE[key] = build_nc()
    nc = _NC_CACHE[key]

    x = np.ascontiguousarray(np.asarray(inputs["L0_attn_x"], dtype=np.float32))
    wmap = {
        "ln_g": "L0_attn_ln_g", "ln_b": "L0_attn_ln_b", "W_in": "L0_attn_W_in",
        "conv_w": "L0_attn_conv_w", "conv_b": "L0_attn_conv_b",
        "W_xproj": "L0_attn_W_xproj", "W_dt": "L0_attn_W_dt",
        "b_dt": "L0_attn_b_dt", "A_log": "L0_attn_A_log",
        "D_param": "L0_attn_D_param", "W_out": "L0_attn_W_out",
        "w_gate": "L0_ffn_w_gate", "w_up": "L0_ffn_w_up",
        "w_down": "L0_ffn_w_down",
    }
    common = {k: np.ascontiguousarray(np.asarray(inputs[v], dtype=np.float32))
              for k, v in wmap.items()}
    common["ident"] = np.eye(128, dtype=np.float32)

    in_maps = []
    for c in range(8):
        m = dict(common)
        m["x"] = np.ascontiguousarray(x[2 * c:2 * c + 2])
        in_maps.append(m)

    res = run_bass_kernel_spmd(nc, in_maps, core_ids=list(range(8)))
    out = np.concatenate([np.asarray(r["out"]) for r in res.results], axis=0)
    return out


# version 22
# speedup vs baseline: 1.0218x; 1.0218x over previous
import numpy as np
import contextlib
import concourse.bass as bass
import concourse.mybir as mybir
from concourse.bass_utils import run_bass_kernel_spmd

F32 = mybir.dt.float32
AF = mybir.ActivationFunctionType
ALU = mybir.AluOpType
AX = mybir.AxisListType

B_FULL = 16
B2 = 2
L = 2048
D = 1024
DI = 2048
NST = 16
RDT = 64
IFF = 2816
T = 128
NCH = L // T
KD = D // 128
KDI = DI // 128
KI = IFF // 128
TT2 = 2 * T
JSLABS = [(0, 512), (512, 512), (1024, 512), (1536, 512), (2048, 512), (2560, 256)]

_DEBUG = False
_PHASES = 99

_NCHUNKS = NCH

ENGS = ("sync", "tensor", "vector", "scalar", "gpsimd")


class _StopRec(Exception):
    pass


class Sem:
    __slots__ = ("h", "n")

    def __init__(self, h):
        self.h = h
        self.n = 0


def build_nc():
    nc = bass.Bass("TRN2", target_bir_lowering=False, debug=False)

    x_d = nc.dram_tensor("x", [B2, L, D], F32, kind="ExternalInput").ap()
    lng_d = nc.dram_tensor("ln_g", [D], F32, kind="ExternalInput").ap()
    lnb_d = nc.dram_tensor("ln_b", [D], F32, kind="ExternalInput").ap()
    win_d = nc.dram_tensor("W_in", [D, 2 * DI], F32, kind="ExternalInput").ap()
    convw_d = nc.dram_tensor("conv_w", [DI, 1, 4], F32, kind="ExternalInput").ap()
    convb_d = nc.dram_tensor("conv_b", [DI], F32, kind="ExternalInput").ap()
    wxp_d = nc.dram_tensor("W_xproj", [DI, 96], F32, kind="ExternalInput").ap()
    wdt_d = nc.dram_tensor("W_dt", [RDT, DI], F32, kind="ExternalInput").ap()
    bdt_d = nc.dram_tensor("b_dt", [DI], F32, kind="ExternalInput").ap()
    alog_d = nc.dram_tensor("A_log", [DI, NST], F32, kind="ExternalInput").ap()
    dpar_d = nc.dram_tensor("D_param", [DI], F32, kind="ExternalInput").ap()
    wout_d = nc.dram_tensor("W_out", [DI, D], F32, kind="ExternalInput").ap()
    wg_d = nc.dram_tensor("w_gate", [IFF, D], F32, kind="ExternalInput").ap()
    wu_d = nc.dram_tensor("w_up", [IFF, D], F32, kind="ExternalInput").ap()
    wdn_d = nc.dram_tensor("w_down", [D, IFF], F32, kind="ExternalInput").ap()
    ident_d = nc.dram_tensor("ident", [128, 128], F32, kind="ExternalInput").ap()

    out_d = nc.dram_tensor("out", [B2, L, D], F32, kind="ExternalOutput").ap()

    winr_d = nc.dram_tensor("winr", [32, 128, 8 * 128], F32, kind="Internal").ap()
    wgT_d = nc.dram_tensor("wgT", [D, IFF], F32, kind="Internal").ap()
    wuT_d = nc.dram_tensor("wuT", [D, IFF], F32, kind="Internal").ap()
    wdT_d = nc.dram_tensor("wdT", [IFF, D], F32, kind="Internal").ap()

    dbg = {}
    if _DEBUG:
        for nm, shp in [("dbg_xn", [128, 2048]), ("dbg_hfm", [128, 2048]),
                        ("dbg_xq", [128, 2048]), ("dbg_xc", [128, 4096]),
                        ("dbg_sz", [128, 4096]), ("dbg_dbc", [96, 256]),
                        ("dbg_delta", [128, 256]), ("dbg_y", [128, 4096]),
                        ("dbg_ffn", [128, 2048]), ("dbg_B", [128, 4096]),
                        ("dbg_wq", [128, 1024]), ("dbg_stat", [128, 32]),
                        ("dbg_tokscr", [128, 2048]), ("dbg_done", [128, 128]),
                        ("dbg_scar", [128, 512]), ("dbg_dbu", [128, 1024]),
                        ("dbg_da", [128, 2048]), ("dbg_fix", [128, 64]),
                        ("dbg_wgt", [128, 2816]), ("dbg_wdt", [128, 1024]),
                        ("dbg_up", [128, 512]), ("dbg_gate", [128, 512]),
                        ("dbg_h", [128, 2048])]:
            dbg[nm] = nc.dram_tensor(nm, shp, F32, kind="ExternalOutput").ap()

    es = contextlib.ExitStack()
    with es:
        def sb(name, shape):
            return es.enter_context(nc.sbuf_tensor(name, shape, F32))

        ident_s = sb("ident_s", [128, 128])
        lngf_s = sb("lngf_s", [128, KD])
        lnbf_s = sb("lnbf_s", [128, KD])
        wxp_s = sb("wxp_s", [128, KDI * 96])
        wdt_s = sb("wdt_s", [RDT, DI])
        nA_s = sb("nA_s", [128, KDI * NST])
        convw_s = sb("convw_s", [128, KDI * 4])
        convb_s = sb("convb_s", [128, KDI])
        bdt_s = sb("bdt_s", [128, KDI])
        dpar_s = sb("dpar_s", [128, KDI])
        scar_s = sb("scar_s", [128, 512])
        halo_s = sb("halo_s", [128, KDI * 6])
        stat_s = sb("stat_s", [128, 32])
        cst_s = sb("cst_s", [128, 2])
        xtok_s = sb("xtok_s", [128, 2 * D])
        xntok_s = sb("xntok_s", [128, 2 * D])
        tokscr_s = sb("tokscr_s", [128, 2 * D])
        hfm_s = sb("hfm_s", [128, KD * TT2])
        xqfm_s = sb("xqfm_s", [128, KD * TT2])
        xc_s = sb("xc_s", [128, KDI * TT2])
        sz_s = sb("sz_s", [128, KDI * TT2])
        dtmp_s = sb("dtmp_s", [128, TT2])
        delta_s = sb("delta_s", [128, 4 * TT2])
        du_s = sb("du_s", [128, TT2])
        dbc_s = sb("dbc_s", [96, TT2])
        Bbc_s = sb("Bbc_s", [128, B2 * NST * T])
        Cbc_s = sb("Cbc_s", [128, B2 * NST * T])
        dA0_s = sb("dA0_s", [128, B2 * 8 * T])
        dA1_s = sb("dA1_s", [128, B2 * 8 * T])
        dBu0_s = sb("dBu0_s", [128, 8 * T])
        dBu1_s = sb("dBu1_s", [128, 8 * T])
        ss_s = sb("ss_s", [128, 8 * T])
        tree_s = sb("tree_s", [128, 2 * 3 * T])
        ytmp_s = sb("ytmp_s", [128, TT2])
        qs_s = sb("qs_s", [128, 3 * D])
        scl_s = sb("scl_s", [128, 2 * 176])
        ffnout_s = sb("ffnout_s", [128, 2 * D])
        xin_s = sb("xin_s", [128, 2 * (T + 3)])
        cacc_s = sb("cacc_s", [128, TT2])
        winb_s = sb("winb_s", [128, 2 * 8 * 128])
        wgu_s = sb("wgu_s", [128, 2048])
        wd_s = sb("wd_s", [128, 512])
        wout_s = sb("wout_s", [128, 1024])

        pb = [es.enter_context(nc.psum_tensor("pb%d" % i, [128, 512], F32))
              for i in range(8)]

        sdma = Sem(es.enter_context(nc.semaphore("sdma")))
        qdma = Sem(es.enter_context(nc.semaphore("qdma")))
        sT = Sem(es.enter_context(nc.semaphore("sT")))
        sV = Sem(es.enter_context(nc.semaphore("sV")))
        sS = Sem(es.enter_context(nc.semaphore("sS")))
        sG = Sem(es.enter_context(nc.semaphore("sG")))
        block = es.enter_context(nc.Block())

        P = {e: [] for e in ENGS}
        waited = {e: {} for e in ENGS}
        own = {"sync": None, "tensor": sT, "vector": sV, "scalar": sS,
               "gpsimd": sG}

        def op(eng, f, sig=None, k=1):
            if sig is not None:
                sig.n += k
                tgt = sig.h

                def thunk(e, f=f, tgt=tgt, k=k):
                    f(e).then_inc(tgt, k)
            else:
                def thunk(e, f=f):
                    f(e)
            P[eng].append(thunk)

        def W(eng, sem, v=None):
            v = sem.n if v is None else v
            if v <= 0 or sem is own[eng]:
                return
            if waited[eng].get(id(sem), -1) >= v:
                return
            waited[eng][id(sem)] = v
            P[eng].append(lambda e, s=sem.h, v=v: e.wait_ge(s, v))

        def dma(eng, out, in_, sig):
            op(eng, lambda e, o=out, i=in_: e.dma_start(out=o, in_=i), sig, 16)

        def dma_nc(eng, out, in_, sig):
            op(eng, lambda e, o=out, i=in_: e.dma_start(
                out=o, in_=i, allow_slow_non_contiguous=True), sig, 16)

        dma("sync", ident_s[:], ident_d, sdma)
        dma_nc("sync", lngf_s[:], lng_d.rearrange("(kc p) -> p kc", p=128), sdma)
        dma_nc("sync", lnbf_s[:], lnb_d.rearrange("(kc p) -> p kc", p=128), sdma)
        dma("sync", wxp_s[:].rearrange("p (dc f) -> p dc f", f=96),
            wxp_d.rearrange("(dc p) f -> p dc f", p=128), sdma)
        dma("sync", wdt_s[:], wdt_d, sdma)
        dma("sync", nA_s[:].rearrange("p (dc n) -> p dc n", n=NST),
            alog_d.rearrange("(dc p) n -> p dc n", p=128), sdma)
        dma("sync", convw_s[:].rearrange("p (dc k) -> p dc k", k=4),
            convw_d.rearrange("(dc p) o k -> p dc (o k)", p=128), sdma)
        dma_nc("sync", convb_s[:], convb_d.rearrange("(dc p) -> p dc", p=128), sdma)
        dma_nc("sync", bdt_s[:], bdt_d.rearrange("(dc p) -> p dc", p=128), sdma)
        dma_nc("sync", dpar_s[:], dpar_d.rearrange("(dc p) -> p dc", p=128), sdma)
        n_const_dma = sdma.n

        W("scalar", sdma, n_const_dma)
        op("scalar", lambda e: e.activation(qs_s[:, 0:KDI * NST], nA_s[:], AF.Exp), sS)
        W("vector", sS)
        op("vector", lambda e: e.tensor_scalar(nA_s[:], qs_s[:, 0:KDI * NST], -1.0,
                                               None, op0=ALU.mult), sV)
        op("vector", lambda e: e.memset(scar_s[:], 0.0), sV)
        op("vector", lambda e: e.memset(cst_s[:, 0:1], 1e-5), sV)
        op("vector", lambda e: e.memset(cst_s[:, 1:2], 1.0), sV)
        op("vector", lambda e: e.memset(halo_s[:], 0.0), sV)
        op("vector", lambda e: e.memset(ffnout_s[:], 0.0), sV)

        winr_out_done = {}
        for cs in range(32):
            buf = cs % 2
            if cs >= 2:
                W("sync", sdma, winr_out_done[cs - 2])
            dma("sync",
                winb_s[:, buf * 1024:(buf + 1) * 1024].rearrange(
                    "p (kc c) -> p kc c", kc=8),
                win_d[:, cs * 128:(cs + 1) * 128].rearrange(
                    "(kc p) c -> p kc c", p=128), sdma)
            W("sync", sdma, sdma.n)
            dma("sync", winr_d[cs], winb_s[:, buf * 1024:(buf + 1) * 1024], sdma)
            winr_out_done[cs] = sdma.n

        def quant_rows(src_ap, ncols, dst_d, rowchunk):
            W("sync", sT)
            dma("sync", xc_s[:, 0:ncols], src_ap, sdma)
            ld_done = sdma.n
            nblk = ncols // 16
            W("scalar", sdma, ld_done)
            W("scalar", sV)
            op("scalar", lambda e: e.activation(sz_s[:, 0:ncols], xc_s[:, 0:ncols],
                                                AF.Abs), sS)
            W("vector", sS)
            op("vector", lambda e: e.reduce_max(
                scl_s[:, 0:nblk].unsqueeze(2),
                sz_s[:, 0:ncols].rearrange("p (a b) -> p a b", b=16),
                axis=AX.X, apply_absolute_value=False), sV)
            op("vector", lambda e: e.nop(cycle_cnt=700), sV)
            op("vector", lambda e: e.tensor_scalar(
                scl_s[:, 176:176 + nblk], scl_s[:, 0:nblk], 1.0 / 6.0, 1e-12,
                op0=ALU.mult, op1=ALU.max), sV)
            op("vector", lambda e: e.nop(cycle_cnt=700), sV)
            op("vector", lambda e: e.reciprocal(scl_s[:, 0:nblk],
                                                scl_s[:, 176:176 + nblk]), sV)
            op("vector", lambda e: e.nop(cycle_cnt=700), sV)
            rsb = scl_s[:, 0:nblk].unsqueeze(2).broadcast_to([128, nblk, 16])
            op("vector", lambda e, rsb=rsb: e.tensor_tensor(
                xc_s[:, 0:ncols].rearrange("p (a b) -> p a b", b=16),
                xc_s[:, 0:ncols].rearrange("p (a b) -> p a b", b=16),
                rsb, op=ALU.mult), sV)
            W("scalar", sV)
            op("scalar", lambda e: e.activation(sz_s[:, 0:ncols], xc_s[:, 0:ncols],
                                                AF.Abs), sS)
            W("vector", sS)
            op("vector", lambda e: e.tensor_single_scalar(
                Bbc_s[:, 0:ncols], sz_s[:, 0:ncols], 0.25, op=ALU.is_gt), sV)
            for th in (0.75, 1.25, 1.75):
                op("vector", lambda e, th=th: e.scalar_tensor_tensor(
                    Bbc_s[:, 0:ncols], sz_s[:, 0:ncols], th, Bbc_s[:, 0:ncols],
                    op0=ALU.is_gt, op1=ALU.add), sV)
            op("vector", lambda e: e.tensor_single_scalar(
                Cbc_s[:, 0:ncols], sz_s[:, 0:ncols], 2.5, op=ALU.is_gt), sV)
            for th in (3.5, 5.0, 5.0):
                op("vector", lambda e, th=th: e.scalar_tensor_tensor(
                    Cbc_s[:, 0:ncols], sz_s[:, 0:ncols], th, Cbc_s[:, 0:ncols],
                    op0=ALU.is_gt, op1=ALU.add), sV)
            W("scalar", sV)
            op("scalar", lambda e: e.activation(sz_s[:, 0:ncols], xc_s[:, 0:ncols],
                                                AF.Sign), sS)
            W("vector", sS)
            op("vector", lambda e: e.scalar_tensor_tensor(
                Bbc_s[:, 0:ncols], Bbc_s[:, 0:ncols], 0.5, Cbc_s[:, 0:ncols],
                op0=ALU.mult, op1=ALU.add), sV)
            op("vector", lambda e: e.tensor_tensor(
                Cbc_s[:, 0:ncols], Bbc_s[:, 0:ncols], sz_s[:, 0:ncols],
                op=ALU.mult), sV)
            ssb = scl_s[:, 176:176 + nblk].unsqueeze(2).broadcast_to([128, nblk, 16])
            op("vector", lambda e, ssb=ssb: e.tensor_tensor(
                xc_s[:, 0:ncols].rearrange("p (a b) -> p a b", b=16),
                Cbc_s[:, 0:ncols].rearrange("p (a b) -> p a b", b=16),
                ssb, op=ALU.mult), sV)
            wq_done = sV.n
            if _DEBUG and rowchunk == 0 and ncols == D and dst_d is wgT_d:
                W("sync", sV, wq_done)
                dma("sync", dbg["dbg_wq"], xc_s[:, 0:1024], sdma)
            ntile = ncols // 128
            for r0 in range(0, ntile, 4):
                nr = min(4, ntile - r0)
                W("tensor", sV, wq_done)
                if r0 >= 4:
                    W("tensor", sV)
                for r in range(nr):
                    op("tensor", lambda e, r=r, r0=r0: e.transpose(
                        pb[0][:, r * 128:(r + 1) * 128],
                        xc_s[:, (r0 + r) * 128:(r0 + r + 1) * 128],
                        ident_s[:]), sT)
                W("vector", sT)
                W("vector", sdma)
                op("vector", lambda e, nr=nr: e.tensor_copy(
                    tokscr_s[:, 0:nr * 128], pb[0][:, 0:nr * 128]), sV)
                ev = sV.n
                for r in range(nr):
                    W("sync", sV, ev)
                    dma("sync", dst_d[(r0 + r) * 128:(r0 + r + 1) * 128,
                                      rowchunk * 128:(rowchunk + 1) * 128],
                        tokscr_s[:, r * 128:(r + 1) * 128], sdma)
            return sdma.n

        for rc in range(KI):
            quant_rows(wg_d[rc * 128:(rc + 1) * 128, :], D, wgT_d, rc)
        for rc in range(KI):
            quant_rows(wu_d[rc * 128:(rc + 1) * 128, :], D, wuT_d, rc)
        for rc in range(KD):
            quant_rows(wdn_d[rc * 128:(rc + 1) * 128, :], IFF, wdT_d, rc)
        prep_spill_done = sdma.n
        if _DEBUG:
            W("sync", sdma, prep_spill_done)
            dma("sync", dbg["dbg_wgt"], wgT_d[0:128, :], sdma)
            dma("sync", dbg["dbg_wdt"], wdT_d[0:128, :], sdma)

        chunk_out_read = 0
        prev_scan_done = 0
        prev_wout_done = 0
        prev_pb3_free = 0
        prev_bc_done = 0

        for c in range(min(NCH, _NCHUNKS)):
          try:
            def ck(k):
                if _PHASES < k:
                    raise _StopRec()
            W("sync", sV, chunk_out_read)
            for b in range(B2):
                dma("sync", xtok_s[:, b * D:(b + 1) * D], x_d[b, c * T:(c + 1) * T, :],
                    sdma)
            x_loaded = sdma.n
            if c == 0:
                W("sync", sdma, prep_spill_done)

            for b in range(B2):
                xsl = xtok_s[:, b * D:(b + 1) * D]
                xnsl = xntok_s[:, b * D:(b + 1) * D]
                tsl = tokscr_s[:, b * D:(b + 1) * D]
                s0 = 16 * b
                W("vector", sdma, x_loaded)
                op("vector", lambda e, xsl=xsl, s0=s0: e.tensor_reduce(
                    stat_s[:, s0:s0 + 1], xsl, axis=AX.X, op=ALU.add), sV)
                W("scalar", sV)
                op("scalar", lambda e, s0=s0: e.activation(
                    stat_s[:, s0 + 1:s0 + 2], stat_s[:, s0:s0 + 1], AF.Copy,
                    scale=1.0 / D), sS)
                W("vector", sS)
                op("vector", lambda e, xsl=xsl, tsl=tsl, s0=s0: e.tensor_scalar(
                    tsl, xsl, stat_s[:, s0 + 1:s0 + 2], None, op0=ALU.subtract), sV)
                W("scalar", sV)
                op("scalar", lambda e, tsl=tsl, s0=s0, b=b: e.activation(
                    qs_s[:, b * D:(b + 1) * D], tsl, AF.Square,
                    accum_out=stat_s[:, s0 + 2:s0 + 3]), sS)
                op("scalar", lambda e, s0=s0: e.activation(
                    stat_s[:, s0 + 3:s0 + 4], stat_s[:, s0 + 2:s0 + 3], AF.Sqrt,
                    scale=1.0 / D, bias=cst_s[:, 0:1]), sS)
                W("vector", sS)
                op("vector", lambda e, s0=s0: e.reciprocal(
                    stat_s[:, s0 + 4:s0 + 5], stat_s[:, s0 + 3:s0 + 4]), sV)
                op("vector", lambda e: e.nop(cycle_cnt=700), sV)
                op("vector", lambda e, xnsl=xnsl, tsl=tsl, s0=s0: e.tensor_scalar(
                    xnsl, tsl, stat_s[:, s0 + 4:s0 + 5], None, op0=ALU.mult), sV)
                op("vector", lambda e, xsl=xsl, xnsl=xnsl: e.tensor_tensor(
                    xsl, xsl, xnsl, op=ALU.add), sV)
                op("vector", lambda e, xnsl=xnsl, s0=s0: e.tensor_reduce(
                    stat_s[:, s0 + 5:s0 + 6], xnsl, axis=AX.X, op=ALU.add), sV)
                W("scalar", sV)
                op("scalar", lambda e, s0=s0: e.activation(
                    stat_s[:, s0 + 6:s0 + 7], stat_s[:, s0 + 5:s0 + 6], AF.Copy,
                    scale=1.0 / D), sS)
                W("vector", sS)
                op("vector", lambda e, xnsl=xnsl, tsl=tsl, s0=s0: e.tensor_scalar(
                    tsl, xnsl, stat_s[:, s0 + 6:s0 + 7], None, op0=ALU.subtract), sV)
                W("scalar", sV)
                op("scalar", lambda e, tsl=tsl, s0=s0, b=b: e.activation(
                    qs_s[:, b * D:(b + 1) * D], tsl, AF.Square,
                    accum_out=stat_s[:, s0 + 7:s0 + 8]), sS)
                op("scalar", lambda e, s0=s0: e.activation(
                    stat_s[:, s0 + 8:s0 + 9], stat_s[:, s0 + 7:s0 + 8], AF.Sqrt,
                    scale=1.0 / D, bias=cst_s[:, 0:1]), sS)
                W("vector", sS)
                op("vector", lambda e, s0=s0: e.reciprocal(
                    stat_s[:, s0 + 9:s0 + 10], stat_s[:, s0 + 8:s0 + 9]), sV)
                op("vector", lambda e: e.nop(cycle_cnt=700), sV)
                op("vector", lambda e, tsl=tsl, s0=s0: e.tensor_scalar(
                    tsl, tsl, stat_s[:, s0 + 9:s0 + 10], None, op0=ALU.mult), sV)
            h1_done = sV.n
            if _DEBUG and c == 0:
                W("sync", sV, h1_done)
                dma("sync", dbg["dbg_xn"], xntok_s[:], sdma)
                dma("sync", dbg["dbg_stat"], stat_s[:], sdma)
                for _e in ("vector", "scalar", "tensor"):
                    W(_e, sdma)
            ck(1)

            for half in range(2):
                W("tensor", sV, h1_done)
                if half == 1:
                    W("tensor", sV)
                for b in range(B2):
                    for r in range(4):
                        kc = half * 4 + r
                        op("tensor", lambda e, b=b, r=r, kc=kc: e.transpose(
                            pb[b][:, r * 128:(r + 1) * 128],
                            tokscr_s[:, b * D + kc * 128:b * D + (kc + 1) * 128],
                            ident_s[:]), sT)
                W("vector", sT)
                for b in range(B2):
                    for r in range(4):
                        kc = half * 4 + r
                        op("vector", lambda e, b=b, r=r, kc=kc: e.tensor_scalar(
                            hfm_s[:, kc * TT2 + b * T:kc * TT2 + (b + 1) * T],
                            pb[b][:, r * 128:(r + 1) * 128],
                            lngf_s[:, kc:kc + 1], lnbf_s[:, kc:kc + 1],
                            op0=ALU.mult, op1=ALU.add), sV)
            hfm_done = sV.n
            ck(1.3)

            for b in range(B2):
                xnsl = xntok_s[:, b * D:(b + 1) * D]
                tsl = tokscr_s[:, b * D:(b + 1) * D]
                q0 = qs_s[:, 0 * D:1 * D]
                q1 = qs_s[:, 1 * D:2 * D]
                q2 = qs_s[:, 2 * D:3 * D]
                W("scalar", sV)
                W("vector", sT)
                op("scalar", lambda e, xnsl=xnsl, q0=q0: e.activation(
                    q0, xnsl, AF.Abs), sS)
                W("vector", sS)
                op("vector", lambda e, q0=q0: e.reduce_max(
                    scl_s[:, 0:64].unsqueeze(2),
                    q0.rearrange("p (a b) -> p a b", b=16),
                    axis=AX.X, apply_absolute_value=False), sV)
                op("vector", lambda e: e.nop(cycle_cnt=700), sV)
                op("vector", lambda e: e.tensor_scalar(
                    scl_s[:, 176:176 + 64], scl_s[:, 0:64], 1.0 / 6.0, 1e-12,
                    op0=ALU.mult, op1=ALU.max), sV)
                op("vector", lambda e: e.nop(cycle_cnt=700), sV)
                op("vector", lambda e: e.reciprocal(
                    scl_s[:, 0:64], scl_s[:, 176:176 + 64]), sV)
                op("vector", lambda e: e.nop(cycle_cnt=700), sV)
                rsb = scl_s[:, 0:64].unsqueeze(2).broadcast_to([128, 64, 16])
                op("vector", lambda e, xnsl=xnsl, tsl=tsl, rsb=rsb: e.tensor_tensor(
                    tsl.rearrange("p (a b) -> p a b", b=16),
                    xnsl.rearrange("p (a b) -> p a b", b=16), rsb, op=ALU.mult), sV)
                W("scalar", sV)
                op("scalar", lambda e, tsl=tsl, q0=q0: e.activation(
                    q0, tsl, AF.Abs), sS)
                W("vector", sS)
                op("vector", lambda e, q0=q0, q1=q1: e.tensor_single_scalar(
                    q1, q0, 0.25, op=ALU.is_gt), sV)
                for th in (0.75, 1.25, 1.75):
                    op("vector", lambda e, th=th, q0=q0, q1=q1: e.scalar_tensor_tensor(
                        q1, q0, th, q1, op0=ALU.is_gt, op1=ALU.add), sV)
                op("vector", lambda e, q0=q0, q2=q2: e.tensor_single_scalar(
                    q2, q0, 2.5, op=ALU.is_gt), sV)
                for th in (3.5, 5.0, 5.0):
                    op("vector", lambda e, th=th, q0=q0, q2=q2: e.scalar_tensor_tensor(
                        q2, q0, th, q2, op0=ALU.is_gt, op1=ALU.add), sV)
                W("scalar", sV)
                op("scalar", lambda e, tsl=tsl, q0=q0: e.activation(
                    q0, tsl, AF.Sign), sS)
                W("vector", sS)
                op("vector", lambda e, q1=q1, q2=q2: e.scalar_tensor_tensor(
                    q1, q1, 0.5, q2, op0=ALU.mult, op1=ALU.add), sV)
                op("vector", lambda e, q0=q0, q1=q1, q2=q2: e.tensor_tensor(
                    q2, q1, q0, op=ALU.mult), sV)
                ssb = scl_s[:, 176:176 + 64].unsqueeze(2).broadcast_to([128, 64, 16])
                op("vector", lambda e, tsl=tsl, q2=q2, ssb=ssb: e.tensor_tensor(
                    tsl.rearrange("p (a b) -> p a b", b=16),
                    q2.rearrange("p (a b) -> p a b", b=16), ssb, op=ALU.mult), sV)
            xq_done = sV.n
            if _DEBUG and c == 0:
                W("sync", sV, xq_done)
                dma("sync", dbg["dbg_tokscr"], tokscr_s[:], sdma)
                for _e in ("vector", "scalar", "tensor"):
                    W(_e, sdma)
            ck(1.6)

            for half in range(2):
                W("tensor", sV, xq_done)
                if half == 1:
                    W("tensor", sS)
                for b in range(B2):
                    for r in range(4):
                        kc = half * 4 + r
                        op("tensor", lambda e, b=b, r=r, kc=kc: e.transpose(
                            pb[b][:, r * 128:(r + 1) * 128],
                            tokscr_s[:, b * D + kc * 128:b * D + (kc + 1) * 128],
                            ident_s[:]), sT)
                W("scalar", sT)
                for b in range(B2):
                    for r in range(4):
                        kc = half * 4 + r
                        op("scalar", lambda e, b=b, r=r, kc=kc: e.copy(
                            xqfm_s[:, kc * TT2 + b * T:kc * TT2 + (b + 1) * T],
                            pb[b][:, r * 128:(r + 1) * 128]), sS)
            xqfm_done = sS.n

            if _DEBUG and c == 0:
                W("sync", sV, xq_done)
                dma("sync", dbg["dbg_hfm"], hfm_s[:], sdma)
                W("sync", sS, xqfm_done)
                dma("sync", dbg["dbg_xq"], xqfm_s[:], sdma)
                for _e in ("vector", "scalar", "tensor"):
                    W(_e, sdma)
            ck(2)

            win_mm_done = {}
            if c > 0:
                win_mm_done[-2] = win_mm_done[-1] = prev_win_last
            evac_done = {}
            xproj_emitted = []

            def emit_xproj(dc):
                W("tensor", sS, silu_done[dc])
                op("tensor", lambda e, dc=dc: e.matmul(
                    pb[2][0:96, 0:TT2],
                    wxp_s[:, dc * 96:(dc + 1) * 96],
                    xc_s[:, dc * TT2:(dc + 1) * TT2],
                    start=(dc == 0), stop=(dc == KDI - 1)), sT)
                xproj_emitted.append(dc)

            silu_done = {}
            for cs in range(32):
                buf = cs % 2
                if cs >= 2 or c > 0:
                    W("sync", sT, win_mm_done[cs - 2])
                dma("sync",
                    winb_s[:, buf * 1024:(buf + 1) * 1024],
                    winr_d[cs], sdma)
                win_ld = sdma.n
                if 2 <= cs < 18:
                    emit_xproj(cs - 2)
                W("tensor", sdma, win_ld)
                W("tensor", sV, hfm_done)
                W("tensor", sS, xqfm_done)
                if cs >= 2:
                    W("tensor", evac_done[cs - 2][1], evac_done[cs - 2][0])
                for kc in range(8):
                    op("tensor", lambda e, buf=buf, kc=kc, cs=cs: e.matmul(
                        pb[buf][:, 0:TT2],
                        winb_s[:, buf * 1024 + kc * 128:buf * 1024 + (kc + 1) * 128],
                        hfm_s[:, kc * TT2:(kc + 1) * TT2],
                        start=(kc == 0), stop=(kc == 7)), sT)
                win_mm_done[cs] = sT.n
                if cs < 16:
                    dc = cs
                    W("vector", sT, win_mm_done[cs])
                    if dc >= 1:
                        W("vector", sS, silu_done[dc - 1])
                    for b in range(B2):
                        op("vector", lambda e, b=b, dc=dc: e.tensor_copy(
                            xin_s[:, b * (T + 3):b * (T + 3) + 3],
                            halo_s[:, dc * 6 + b * 3:dc * 6 + (b + 1) * 3]), sV)
                        op("vector", lambda e, b=b, buf=buf: e.tensor_copy(
                            xin_s[:, b * (T + 3) + 3:(b + 1) * (T + 3)],
                            pb[buf][:, b * T:(b + 1) * T]), sV)
                    evac_done[cs] = (sV.n, sV)
                    for b in range(B2):
                        op("vector", lambda e, b=b, dc=dc: e.tensor_copy(
                            halo_s[:, dc * 6 + b * 3:dc * 6 + (b + 1) * 3],
                            xin_s[:, b * (T + 3) + T:(b + 1) * (T + 3)]), sV)
                        op("vector", lambda e, b=b, dc=dc: e.tensor_scalar(
                            cacc_s[:, b * T:(b + 1) * T],
                            xin_s[:, b * (T + 3):b * (T + 3) + T],
                            convw_s[:, dc * 4:dc * 4 + 1], None, op0=ALU.mult), sV)
                        for k in range(1, 4):
                            op("vector", lambda e, b=b, dc=dc, k=k:
                               e.scalar_tensor_tensor(
                                   cacc_s[:, b * T:(b + 1) * T],
                                   xin_s[:, b * (T + 3) + k:b * (T + 3) + k + T],
                                   convw_s[:, dc * 4 + k:dc * 4 + k + 1],
                                   cacc_s[:, b * T:(b + 1) * T],
                                   op0=ALU.mult, op1=ALU.add), sV)
                    W("scalar", sV)
                    op("scalar", lambda e, dc=dc: e.activation(
                        xc_s[:, dc * TT2:(dc + 1) * TT2], cacc_s[:, 0:TT2],
                        AF.Silu, bias=convb_s[:, dc:dc + 1]), sS)
                    silu_done[dc] = sS.n
                else:
                    zc = cs - 16
                    W("scalar", sT, win_mm_done[cs])
                    op("scalar", lambda e, buf=buf, zc=zc: e.activation(
                        sz_s[:, zc * TT2:(zc + 1) * TT2], pb[buf][:, 0:TT2],
                        AF.Silu), sS)
                    evac_done[cs] = (sS.n, sS)
            for dc in range(KDI):
                if dc not in xproj_emitted:
                    emit_xproj(dc)
            xproj_done = sT.n
            prev_win_last = win_mm_done[31]

            W("vector", sT, xproj_done)
            W("vector", sdma, prev_bc_done)
            op("vector", lambda e: e.tensor_copy(dbc_s[:], pb[2][0:96, 0:TT2]), sV)
            dbc_done = sV.n

            if _DEBUG and c == 0:
                W("sync", sV, dbc_done)
                dma("sync", dbg["dbg_xc"], xc_s[:], sdma)
                dma("sync", dbg["dbg_sz"], sz_s[:], sdma)
                dma("sync", dbg["dbg_dbc"], dbc_s[:], sdma)
                for _e in ("vector", "scalar", "tensor"):
                    W(_e, sdma)

            W("sync", sV, dbc_done)
            for b in range(B2):
                for n in range(NST):
                    dma("sync",
                        Bbc_s[:, (b * NST + n) * T:(b * NST + n + 1) * T],
                        dbc_s[64 + n:65 + n, b * T:(b + 1) * T]
                        .unsqueeze(1).broadcast_to([1, 128, T]), sdma)
                    dma("sync",
                        Cbc_s[:, (b * NST + n) * T:(b * NST + n + 1) * T],
                        dbc_s[80 + n:81 + n, b * T:(b + 1) * T]
                        .unsqueeze(1).broadcast_to([1, 128, T]), sdma)
            ck(3)
            bc_done = sdma.n
            prev_bc_done = bc_done

            dA_bufs = (dA0_s, dA1_s)
            dBu_bufs = (dBu0_s, dBu1_s)
            dA_scan_done = {-2: prev_scan_done, -1: prev_scan_done}
            delta_used = {k: prev_scan_done for k in range(-4, 0)}
            scan_it = [0]
            dbuf_guard = {0: prev_scan_done, 1: prev_scan_done}
            delta_mm_done = {}
            pb3_free = {-1: prev_pb3_free}
            delta_ready = {}

            def emit_delta(dc):
                dbuf = delta_s[:, (dc % 4) * TT2:(dc % 4 + 1) * TT2]
                W("tensor", sS, pb3_free[dc - 1])
                W("tensor", sV, dbc_done)
                op("tensor", lambda e: e.matmul(
                    pb[3][:, 0:TT2],
                    wdt_s[:, dc * 128:(dc + 1) * 128],
                    dbc_s[0:64, :], start=True, stop=True), sT)
                delta_mm_done[dc] = sT.n
                W("scalar", sT, delta_mm_done[dc])
                W("scalar", sV, delta_used[dc - 4])
                op("scalar", lambda e: e.activation(
                    dtmp_s[:], pb[3][:, 0:TT2], AF.Exp,
                    bias=bdt_s[:, dc:dc + 1]), sS)
                pb3_free[dc] = sS.n
                op("scalar", lambda e, dbuf=dbuf: e.activation(
                    dbuf, dtmp_s[:], AF.Ln, bias=cst_s[:, 1:2]), sS)
                delta_ready[dc] = sS.n

            def emit_scan_dc(dc):
                dbuf = delta_s[:, (dc % 4) * TT2:(dc % 4 + 1) * TT2]
                W("vector", sS, delta_ready[dc])
                op("vector", lambda e, dbuf=dbuf: e.tensor_tensor(
                    du_s[:], dbuf, xc_s[:, dc * TT2:(dc + 1) * TT2],
                    op=ALU.mult), sV)
                du_done = sV.n
                for half in range(2):
                    ab = dA_bufs[(2 * dc + half) % 2]
                    W("scalar", sV, dA_scan_done[(dc * 2 + half) - 2])
                    dbv2 = dbuf.rearrange("p (b t) -> p b t", b=2)
                    for n8 in range(8):
                        n = half * 8 + n8
                        o = ab[:].rearrange("p (b n t) -> p b n t", b=2, n=8)[:, :, n8, :]
                        op("scalar", lambda e, o=o, dbv2=dbv2, dc=dc, n=n:
                           e.activation(o, dbv2, AF.Exp,
                                        scale=nA_s[:, dc * NST + n:dc * NST + n + 1]),
                           sS)
                    dA_done = sS.n
                    W("vector", sS, dA_done)
                    W("vector", sdma, bc_done)
                    scoff = dc * 32 + half * 16
                    for b in range(B2):
                        it = scan_it[0]
                        scan_it[0] += 1
                        dbuf_i = dBu_bufs[it % 2]
                        scv_b = scar_s[:, scoff + b * 8:scoff + b * 8 + 8]
                        W("gpsimd", sV, du_done)
                        W("gpsimd", sdma, bc_done)
                        W("gpsimd", sV, dbuf_guard[it % 2])
                        op("gpsimd", lambda e, b=b, half=half, dbuf_i=dbuf_i:
                           e.tensor_tensor(
                               dbuf_i[:].rearrange("p (n t) -> p n t", n=8),
                               du_s[:, b * T:(b + 1) * T].unsqueeze(1)
                               .broadcast_to([128, 8, T]),
                               Bbc_s[:, (b * NST + half * 8) * T:
                                     (b * NST + half * 8 + 8) * T]
                               .rearrange("p (n t) -> p n t", n=8),
                               op=ALU.mult), sG)
                        W("vector", sG)
                        for n8 in range(8):
                            op("vector", lambda e, ab=ab, b=b, n8=n8,
                               dbuf_i=dbuf_i, scol=scoff + b * 8 + n8:
                               e.tensor_tensor_scan(
                                   ss_s[:, n8 * T:(n8 + 1) * T],
                                   ab[:, b * 8 * T + n8 * T:
                                      b * 8 * T + (n8 + 1) * T],
                                   dbuf_i[:, n8 * T:(n8 + 1) * T],
                                   scar_s[:, scol:scol + 1],
                                   op0=ALU.mult, op1=ALU.add), sV)
                        if half == 1 and b == 0:
                            for b0 in range(B2):
                                tb0 = 3 * T * b0
                                op("vector", lambda e, b0=b0, tb0=tb0:
                                   e.tensor_tensor(
                                       ytmp_s[:, b0 * T:(b0 + 1) * T],
                                       tree_s[:, tb0:tb0 + T],
                                       tree_s[:, tb0 + T:tb0 + 2 * T],
                                       op=ALU.add), sV)
                        ssv = ss_s[:].rearrange("p (n t) -> p n t", n=8)
                        op("vector", lambda e, b=b, half=half, ssv=ssv,
                           dbuf_i=dbuf_i: e.tensor_tensor(
                               dbuf_i[:].rearrange("p (n t) -> p n t", n=8), ssv,
                               Cbc_s[:, (b * NST + half * 8) * T:
                                     (b * NST + half * 8 + 8) * T]
                               .rearrange("p (n t) -> p n t", n=8),
                               op=ALU.mult), sV)
                        op("vector", lambda e, ssv=ssv, scv_b=scv_b: e.tensor_copy(
                            scv_b, ssv[:, :, T - 1]), sV)
                        tb = 3 * T * b
                        op("vector", lambda e, dbuf_i=dbuf_i: e.tensor_tensor(
                            ss_s[:, 0:4 * T],
                            dbuf_i[:].rearrange("p (a two t) -> p a two t", two=2,
                                                t=T)[:, :, 0, :],
                            dbuf_i[:].rearrange("p (a two t) -> p a two t", two=2,
                                                t=T)[:, :, 1, :], op=ALU.add), sV)
                        op("vector", lambda e, tb=tb: e.tensor_tensor(
                            tree_s[:, tb:tb + 2 * T],
                            ss_s[:, 0:4 * T].rearrange(
                                "p (a two t) -> p a two t", two=2, t=T)[:, :, 0, :],
                            ss_s[:, 0:4 * T].rearrange(
                                "p (a two t) -> p a two t", two=2, t=T)[:, :, 1, :],
                            op=ALU.add), sV)
                        dbuf_guard[it % 2] = sV.n
                    if half == 1:
                        op("vector", lambda e: e.nop(cycle_cnt=700), sV)
                        for b in range(B2):
                            tb = 3 * T * b
                            op("vector", lambda e, b=b, tb=tb: e.tensor_tensor(
                                tree_s[:, tb + 2 * T:tb + 3 * T],
                                tree_s[:, tb:tb + T],
                                tree_s[:, tb + T:tb + 2 * T], op=ALU.add), sV)
                        op("vector", lambda e: e.nop(cycle_cnt=700), sV)
                        for b in range(B2):
                            tb = 3 * T * b
                            op("vector", lambda e, b=b, tb=tb: e.tensor_tensor(
                                ytmp_s[:, b * T:(b + 1) * T],
                                ytmp_s[:, b * T:(b + 1) * T],
                                tree_s[:, tb + 2 * T:tb + 3 * T], op=ALU.add), sV)
                        op("vector", lambda e: e.nop(cycle_cnt=700), sV)
                        op("vector", lambda e, dc=dc: e.scalar_tensor_tensor(
                            ss_s[:, 0:TT2],
                            xc_s[:, dc * TT2:(dc + 1) * TT2],
                            dpar_s[:, dc:dc + 1],
                            ytmp_s[:], op0=ALU.mult, op1=ALU.add), sV)
                        op("vector", lambda e: e.nop(cycle_cnt=700), sV)
                        op("vector", lambda e, dc=dc: e.tensor_tensor(
                            xc_s[:, dc * TT2:(dc + 1) * TT2],
                            ss_s[:, 0:TT2],
                            sz_s[:, dc * TT2:(dc + 1) * TT2],
                            op=ALU.mult), sV)
                    dA_scan_done[dc * 2 + half] = sV.n
                delta_used[dc] = sV.n

            down_evac = {}

            def emit_ffn_j(j):
                joff, w = JSLABS[j]
                nkc = w // 128
                for wi, (wT, pbs) in enumerate(((wgT_d, (4, 5)), (wuT_d, (6, 7)))):
                    for kh in range(2):
                        W("sync", sT)
                        for kc4 in range(4):
                            kc = kh * 4 + kc4
                            dma("sync", wgu_s[:, kc4 * 512:kc4 * 512 + w],
                                wT[kc * 128:(kc + 1) * 128, joff:joff + w], sdma)
                        W("tensor", sdma)
                        W("tensor", sS, xqfm_done)
                        if j > 0 and wi == 0 and kh == 0:
                            W("tensor", sV, down_evac[j - 1])
                        for b in range(B2):
                            for kc4 in range(4):
                                kc = kh * 4 + kc4
                                op("tensor", lambda e, b=b, kc=kc, kc4=kc4, w=w,
                                   pbx=pbs[b], kh=kh: e.matmul(
                                       pb[pbx][:, 0:w],
                                       xqfm_s[:, kc * TT2 + b * T:
                                              kc * TT2 + (b + 1) * T],
                                       wgu_s[:, kc4 * 512:kc4 * 512 + w],
                                       start=(kh == 0 and kc4 == 0),
                                       stop=(kh == 1 and kc4 == 3)), sT)
                gu_done = sT.n
                for b in range(B2):
                    tsl = tokscr_s[:, b * D:b * D + w]
                    q0 = qs_s[:, 0:w]
                    q1 = qs_s[:, D:D + w]
                    q2 = qs_s[:, 2 * D:2 * D + w]
                    W("scalar", sT, gu_done)
                    op("scalar", lambda e, b=b, tsl=tsl: e.activation(
                        tsl, pb[4 + b][:, 0:w], AF.Silu), sS)
                    if _DEBUG and c == 0 and j == 0 and b == 0:
                        W("vector", sT, gu_done)
                        op("vector", lambda e, w=w: e.tensor_copy(
                            qs_s[:, 0:w], pb[6][:, 0:w]), sV)
                        op("vector", lambda e, w=w: e.tensor_copy(
                            qs_s[:, D:D + w], pb[4][:, 0:w]), sV)
                        W("sync", sV)
                        dma("sync", dbg["dbg_up"], qs_s[:, 0:512], sdma)
                        dma("sync", dbg["dbg_gate"], qs_s[:, D:D + 512], sdma)
                        W("vector", sdma)
                    W("vector", sS)
                    op("vector", lambda e, b=b, tsl=tsl: e.tensor_tensor(
                        tsl, tsl, pb[6 + b][:, 0:w], op=ALU.mult), sV)
                    nblk = w // 16
                    W("scalar", sV)
                    op("scalar", lambda e, tsl=tsl, q0=q0: e.activation(
                        q0, tsl, AF.Abs), sS)
                    W("vector", sS)
                    op("vector", lambda e, q0=q0, nblk=nblk: e.reduce_max(
                        scl_s[:, 0:nblk].unsqueeze(2),
                        q0.rearrange("p (a b) -> p a b", b=16),
                        axis=AX.X, apply_absolute_value=False), sV)
                    op("vector", lambda e: e.nop(cycle_cnt=700), sV)
                    op("vector", lambda e, nblk=nblk: e.tensor_scalar(
                        scl_s[:, 176:176 + nblk], scl_s[:, 0:nblk], 1.0 / 6.0, 1e-12,
                        op0=ALU.mult, op1=ALU.max), sV)
                    op("vector", lambda e: e.nop(cycle_cnt=700), sV)
                    op("vector", lambda e, nblk=nblk: e.reciprocal(
                        scl_s[:, 0:nblk], scl_s[:, 176:176 + nblk]), sV)
                    op("vector", lambda e: e.nop(cycle_cnt=700), sV)
                    rsb = scl_s[:, 0:nblk].unsqueeze(2).broadcast_to([128, nblk, 16])
                    op("vector", lambda e, tsl=tsl, rsb=rsb, nblk=nblk:
                       e.tensor_tensor(
                           tsl.rearrange("p (a b) -> p a b", b=16),
                           tsl.rearrange("p (a b) -> p a b", b=16), rsb,
                           op=ALU.mult), sV)
                    W("scalar", sV)
                    op("scalar", lambda e, tsl=tsl, q0=q0: e.activation(
                        q0, tsl, AF.Abs), sS)
                    W("vector", sS)
                    op("vector", lambda e, q0=q0, q1=q1: e.tensor_single_scalar(
                        q1, q0, 0.25, op=ALU.is_gt), sV)
                    for th in (0.75, 1.25, 1.75):
                        op("vector", lambda e, th=th, q0=q0, q1=q1:
                           e.scalar_tensor_tensor(q1, q0, th, q1,
                                                  op0=ALU.is_gt, op1=ALU.add), sV)
                    op("vector", lambda e, q0=q0, q2=q2: e.tensor_single_scalar(
                        q2, q0, 2.5, op=ALU.is_gt), sV)
                    for th in (3.5, 5.0, 5.0):
                        op("vector", lambda e, th=th, q0=q0, q2=q2:
                           e.scalar_tensor_tensor(q2, q0, th, q2,
                                                  op0=ALU.is_gt, op1=ALU.add), sV)
                    W("scalar", sV)
                    op("scalar", lambda e, tsl=tsl, q0=q0: e.activation(
                        q0, tsl, AF.Sign), sS)
                    W("vector", sS)
                    op("vector", lambda e, q1=q1, q2=q2: e.scalar_tensor_tensor(
                        q1, q1, 0.5, q2, op0=ALU.mult, op1=ALU.add), sV)
                    op("vector", lambda e, q0=q0, q1=q1, q2=q2: e.tensor_tensor(
                        q2, q1, q0, op=ALU.mult), sV)
                    ssb = scl_s[:, 176:176 + nblk].unsqueeze(2).broadcast_to(
                        [128, nblk, 16])
                    op("vector", lambda e, tsl=tsl, q2=q2, ssb=ssb, nblk=nblk:
                       e.tensor_tensor(
                           tsl.rearrange("p (a b) -> p a b", b=16),
                           q2.rearrange("p (a b) -> p a b", b=16), ssb,
                           op=ALU.mult), sV)
                    hq_done = sV.n
                    if _DEBUG and c == 0 and j == 0 and b == 0:
                        W("sync", sV, hq_done)
                        dma("sync", dbg["dbg_h"], tokscr_s[:], sdma)
                        W("vector", sdma)
                        W("scalar", sdma)
                    W("tensor", sV, hq_done)
                    for r in range(nkc):
                        op("tensor", lambda e, b=b, r=r, tsl=tsl: e.transpose(
                            pb[4 + b][:, r * 128:(r + 1) * 128],
                            tsl[:, r * 128:(r + 1) * 128], ident_s[:]), sT)
                    W("vector", sT)
                    op("vector", lambda e, b=b, nkc=nkc: e.tensor_copy(
                        hfm_s[:, b * 1024:b * 1024 + nkc * 128],
                        pb[4 + b][:, 0:nkc * 128]), sV)
                    hqT_done = sV.n
                    W("tensor", sV, hqT_done)
                    for ds in range(2):
                        for kc4 in range(nkc):
                            W("sync", sT)
                            kc = joff // 128 + kc4
                            dma("sync", wd_s[:, 0:512],
                                wdT_d[kc * 128:(kc + 1) * 128,
                                      ds * 512:(ds + 1) * 512], sdma)
                            W("tensor", sdma, sdma.n)
                            op("tensor", lambda e, b=b, kc4=kc4, ds=ds,
                               nkc=nkc: e.matmul(
                                   pb[ds][:, 0:512],
                                   hfm_s[:, b * 1024 + kc4 * 128:
                                         b * 1024 + (kc4 + 1) * 128],
                                   wd_s[:, 0:512],
                                   start=(kc4 == 0), stop=(kc4 == nkc - 1)), sT)
                    dn_done = sT.n
                    W("vector", sT, dn_done)
                    for ds in range(2):
                        op("vector", lambda e, b=b, ds=ds: e.tensor_tensor(
                            ffnout_s[:, b * D + ds * 512:b * D + (ds + 1) * 512],
                            pb[ds][:, 0:512],
                            ffnout_s[:, b * D + ds * 512:b * D + (ds + 1) * 512],
                            op=ALU.add), sV)
                down_evac[j] = sV.n

            ffn_after = {1: 0, 4: 1, 6: 2, 8: 3, 10: 4, 12: 5}
            emit_delta(0)
            emit_delta(1)
            for dc in range(KDI):
                if dc + 2 < KDI:
                    emit_delta(dc + 2)
                emit_scan_dc(dc)
                if dc in ffn_after:
                    emit_ffn_j(ffn_after[dc])
            scan_all_done = sV.n
            prev_scan_done = scan_all_done
            prev_pb3_free = pb3_free[KDI - 1]

            mm_done = {(-1, 0): prev_wout_done, (-1, 1): prev_wout_done}
            ld_done2 = {}
            for dc in range(KDI):
                for ds in range(2):
                    W("scalar", sT, mm_done[(dc - 1, ds)])
                    dma("scalar", wout_s[:, ds * 512:(ds + 1) * 512],
                        wout_d[dc * 128:(dc + 1) * 128,
                               ds * 512:(ds + 1) * 512], qdma)
                    ld_done2[(dc, ds)] = qdma.n
                for ds in range(2):
                    W("tensor", qdma, ld_done2[(dc, ds)])
                    W("tensor", sV, scan_all_done)
                    for b in range(B2):
                        op("tensor", lambda e, dc=dc, b=b, ds=ds: e.matmul(
                            pb[4 + 2 * b + ds][:, 0:512],
                            xc_s[:, dc * TT2 + b * T:dc * TT2 + (b + 1) * T],
                            wout_s[:, ds * 512:(ds + 1) * 512],
                            start=(dc == 0), stop=(dc == KDI - 1)), sT)
                    mm_done[(dc, ds)] = sT.n
            wout_done = sT.n
            prev_wout_done = wout_done

            if _DEBUG and c == 0:
                W("sync", sV, scan_all_done)
                dma("sync", dbg["dbg_y"], xc_s[:], sdma)
                dma("sync", dbg["dbg_B"], Bbc_s[:], sdma)
                dma("sync", dbg["dbg_ffn"], ffnout_s[:], sdma)
                W("sync", sS)
                dma("sync", dbg["dbg_delta"], delta_s[:, TT2:2 * TT2], sdma)
                for _e in ("vector", "scalar", "tensor"):
                    W(_e, sdma)

            ck(4)
            W("vector", sT, wout_done)
            for b in range(B2):
                for ds in range(2):
                    op("vector", lambda e, b=b, ds=ds: e.tensor_tensor(
                        xntok_s[:, b * D + ds * 512:b * D + (ds + 1) * 512],
                        pb[4 + 2 * b + ds][:, 0:512],
                        ffnout_s[:, b * D + ds * 512:b * D + (ds + 1) * 512],
                        op=ALU.add), sV)
                op("vector", lambda e, b=b: e.tensor_tensor(
                    xntok_s[:, b * D:(b + 1) * D],
                    xntok_s[:, b * D:(b + 1) * D],
                    xtok_s[:, b * D:(b + 1) * D], op=ALU.add), sV)
                op("vector", lambda e, b=b: e.memset(
                    ffnout_s[:, b * D:(b + 1) * D], 0.0), sV)
            asm_done = sV.n
            chunk_out_read = asm_done
            W("sync", sV, asm_done)
            for b in range(B2):
                dma("sync", out_d[b, c * T:(c + 1) * T, :],
                    xntok_s[:, b * D:(b + 1) * D], sdma)
          except _StopRec:
            pass

        if _DEBUG:
            for _e in ("vector", "scalar", "tensor"):
                W("sync", own[_e])
            dma("sync", dbg["dbg_done"], ident_s[:], sdma)
        final_dma = sdma.n
        P["sync"].append(lambda e, v=final_dma: e.wait_ge(sdma.h, v))

        @block.sync
        def _(sync):
            for f in P["sync"]:
                f(sync)

        @block.tensor
        def _(tensor):
            for f in P["tensor"]:
                f(tensor)

        @block.vector
        def _(vector):
            for f in P["vector"]:
                f(vector)

        @block.scalar
        def _(scalar):
            for f in P["scalar"]:
                f(scalar)

        @block.gpsimd
        def _(gpsimd):
            for f in P["gpsimd"]:
                f(gpsimd)

    return nc


_NC_CACHE = {}


def kernel(**inputs):
    key = "nc"
    if key not in _NC_CACHE:
        _NC_CACHE[key] = build_nc()
    nc = _NC_CACHE[key]

    x = np.ascontiguousarray(np.asarray(inputs["L0_attn_x"], dtype=np.float32))
    wmap = {
        "ln_g": "L0_attn_ln_g", "ln_b": "L0_attn_ln_b", "W_in": "L0_attn_W_in",
        "conv_w": "L0_attn_conv_w", "conv_b": "L0_attn_conv_b",
        "W_xproj": "L0_attn_W_xproj", "W_dt": "L0_attn_W_dt",
        "b_dt": "L0_attn_b_dt", "A_log": "L0_attn_A_log",
        "D_param": "L0_attn_D_param", "W_out": "L0_attn_W_out",
        "w_gate": "L0_ffn_w_gate", "w_up": "L0_ffn_w_up",
        "w_down": "L0_ffn_w_down",
    }
    common = {k: np.ascontiguousarray(np.asarray(inputs[v], dtype=np.float32))
              for k, v in wmap.items()}
    common["ident"] = np.eye(128, dtype=np.float32)

    in_maps = []
    for c in range(8):
        m = dict(common)
        m["x"] = np.ascontiguousarray(x[2 * c:2 * c + 2])
        in_maps.append(m)

    res = run_bass_kernel_spmd(nc, in_maps, core_ids=list(range(8)))
    out = np.concatenate([np.asarray(r["out"]) for r in res.results], axis=0)
    return out


# version 23 (final)
# speedup vs baseline: 1.1973x; 1.1717x over previous
import numpy as np
import contextlib
import concourse.bass as bass
import concourse.mybir as mybir
from concourse.bass_utils import run_bass_kernel_spmd

F32 = mybir.dt.float32
AF = mybir.ActivationFunctionType
ALU = mybir.AluOpType
AX = mybir.AxisListType

B_FULL = 16
B2 = 2
L = 2048
D = 1024
DI = 2048
NST = 16
RDT = 64
IFF = 2816
T = 128
NCH = L // T
KD = D // 128
KDI = DI // 128
KI = IFF // 128
TT2 = 2 * T
JSLABS = [(0, 512), (512, 512), (1024, 512), (1536, 512), (2048, 512), (2560, 256)]

_DEBUG = False
_PHASES = 99

_NCHUNKS = NCH

ENGS = ("sync", "tensor", "vector", "scalar", "gpsimd")


class _StopRec(Exception):
    pass


class Sem:
    __slots__ = ("h", "n")

    def __init__(self, h):
        self.h = h
        self.n = 0


def build_nc():
    nc = bass.Bass("TRN2", target_bir_lowering=False, debug=False)

    x_d = nc.dram_tensor("x", [B2, L, D], F32, kind="ExternalInput").ap()
    lng_d = nc.dram_tensor("ln_g", [D], F32, kind="ExternalInput").ap()
    lnb_d = nc.dram_tensor("ln_b", [D], F32, kind="ExternalInput").ap()
    win_d = nc.dram_tensor("W_in", [D, 2 * DI], F32, kind="ExternalInput").ap()
    convw_d = nc.dram_tensor("conv_w", [DI, 1, 4], F32, kind="ExternalInput").ap()
    convb_d = nc.dram_tensor("conv_b", [DI], F32, kind="ExternalInput").ap()
    wxp_d = nc.dram_tensor("W_xproj", [DI, 96], F32, kind="ExternalInput").ap()
    wdt_d = nc.dram_tensor("W_dt", [RDT, DI], F32, kind="ExternalInput").ap()
    bdt_d = nc.dram_tensor("b_dt", [DI], F32, kind="ExternalInput").ap()
    alog_d = nc.dram_tensor("A_log", [DI, NST], F32, kind="ExternalInput").ap()
    dpar_d = nc.dram_tensor("D_param", [DI], F32, kind="ExternalInput").ap()
    wout_d = nc.dram_tensor("W_out", [DI, D], F32, kind="ExternalInput").ap()
    wg_d = nc.dram_tensor("w_gate", [IFF, D], F32, kind="ExternalInput").ap()
    wu_d = nc.dram_tensor("w_up", [IFF, D], F32, kind="ExternalInput").ap()
    wdn_d = nc.dram_tensor("w_down", [D, IFF], F32, kind="ExternalInput").ap()
    ident_d = nc.dram_tensor("ident", [128, 128], F32, kind="ExternalInput").ap()

    out_d = nc.dram_tensor("out", [B2, L, D], F32, kind="ExternalOutput").ap()

    winr_d = nc.dram_tensor("winr", [32, 128, 8 * 128], F32, kind="Internal").ap()
    wgT_d = nc.dram_tensor("wgT", [D, IFF], F32, kind="Internal").ap()
    wuT_d = nc.dram_tensor("wuT", [D, IFF], F32, kind="Internal").ap()
    wdT_d = nc.dram_tensor("wdT", [IFF, D], F32, kind="Internal").ap()

    dbg = {}
    if _DEBUG:
        for nm, shp in [("dbg_xn", [128, 2048]), ("dbg_hfm", [128, 2048]),
                        ("dbg_xq", [128, 2048]), ("dbg_xc", [128, 4096]),
                        ("dbg_sz", [128, 4096]), ("dbg_dbc", [96, 256]),
                        ("dbg_delta", [128, 256]), ("dbg_y", [128, 4096]),
                        ("dbg_ffn", [128, 2048]), ("dbg_B", [128, 4096]),
                        ("dbg_wq", [128, 1024]), ("dbg_stat", [128, 32]),
                        ("dbg_tokscr", [128, 2048]), ("dbg_done", [128, 128]),
                        ("dbg_scar", [128, 512]), ("dbg_dbu", [128, 1024]),
                        ("dbg_da", [128, 2048]), ("dbg_fix", [128, 64]),
                        ("dbg_wgt", [128, 2816]), ("dbg_wdt", [128, 1024]),
                        ("dbg_up", [128, 512]), ("dbg_gate", [128, 512]),
                        ("dbg_h", [128, 2048])]:
            dbg[nm] = nc.dram_tensor(nm, shp, F32, kind="ExternalOutput").ap()

    es = contextlib.ExitStack()
    with es:
        def sb(name, shape):
            return es.enter_context(nc.sbuf_tensor(name, shape, F32))

        ident_s = sb("ident_s", [128, 128])
        lngf_s = sb("lngf_s", [128, KD])
        lnbf_s = sb("lnbf_s", [128, KD])
        wxp_s = sb("wxp_s", [128, KDI * 96])
        wdt_s = sb("wdt_s", [RDT, DI])
        nA_s = sb("nA_s", [128, KDI * NST])
        convw_s = sb("convw_s", [128, KDI * 4])
        convb_s = sb("convb_s", [128, KDI])
        bdt_s = sb("bdt_s", [128, KDI])
        dpar_s = sb("dpar_s", [128, KDI])
        scar_s = sb("scar_s", [128, 512])
        halo_s = sb("halo_s", [128, KDI * 6])
        stat_s = sb("stat_s", [128, 32])
        cst_s = sb("cst_s", [128, 2])
        xtok_s = sb("xtok_s", [128, 2 * D])
        xntok_s = sb("xntok_s", [128, 2 * D])
        tokscr_s = sb("tokscr_s", [128, 2 * D])
        hfm_s = sb("hfm_s", [128, KD * TT2])
        xqfm_s = sb("xqfm_s", [128, KD * TT2])
        xc_s = sb("xc_s", [128, KDI * TT2])
        sz_s = sb("sz_s", [128, KDI * TT2])
        dtmp_s = sb("dtmp_s", [128, TT2])
        delta_s = sb("delta_s", [128, 4 * TT2])
        du_s = sb("du_s", [128, TT2])
        dbc_s = sb("dbc_s", [96, TT2])
        Bbc_s = sb("Bbc_s", [128, B2 * NST * T])
        Cbc_s = sb("Cbc_s", [128, B2 * NST * T])
        dA0_s = sb("dA0_s", [128, B2 * 8 * T])
        dA1_s = sb("dA1_s", [128, B2 * 8 * T])
        dBu0_s = sb("dBu0_s", [128, 8 * T])
        dBu1_s = sb("dBu1_s", [128, 8 * T])
        ss_s = sb("ss_s", [128, 8 * T])
        tree_s = sb("tree_s", [128, 2 * 3 * T])
        ytmp_s = sb("ytmp_s", [128, TT2])
        qs_s = sb("qs_s", [128, 3 * D])
        scl_s = sb("scl_s", [128, 2 * 176])
        ffnout_s = sb("ffnout_s", [128, 2 * D])
        xin_s = sb("xin_s", [128, 2 * (T + 3)])
        cacc_s = sb("cacc_s", [128, TT2])
        winb_s = sb("winb_s", [128, 2 * 8 * 128])
        wgu_s = sb("wgu_s", [128, 2048])
        wd_s = sb("wd_s", [128, 512])
        wout_s = sb("wout_s", [128, 1024])

        pb = [es.enter_context(nc.psum_tensor("pb%d" % i, [128, 512], F32))
              for i in range(8)]

        sdma = Sem(es.enter_context(nc.semaphore("sdma")))
        qdma = Sem(es.enter_context(nc.semaphore("qdma")))
        sT = Sem(es.enter_context(nc.semaphore("sT")))
        sV = Sem(es.enter_context(nc.semaphore("sV")))
        sS = Sem(es.enter_context(nc.semaphore("sS")))
        sG = Sem(es.enter_context(nc.semaphore("sG")))
        block = es.enter_context(nc.Block())

        P = {e: [] for e in ENGS}
        waited = {e: {} for e in ENGS}
        own = {"sync": None, "tensor": sT, "vector": sV, "scalar": sS,
               "gpsimd": sG}

        def op(eng, f, sig=None, k=1):
            if sig is not None:
                sig.n += k
                tgt = sig.h

                def thunk(e, f=f, tgt=tgt, k=k):
                    f(e).then_inc(tgt, k)
            else:
                def thunk(e, f=f):
                    f(e)
            P[eng].append(thunk)

        def W(eng, sem, v=None):
            v = sem.n if v is None else v
            if v <= 0 or sem is own[eng]:
                return
            if waited[eng].get(id(sem), -1) >= v:
                return
            waited[eng][id(sem)] = v
            P[eng].append(lambda e, s=sem.h, v=v: e.wait_ge(s, v))

        def dma(eng, out, in_, sig):
            op(eng, lambda e, o=out, i=in_: e.dma_start(out=o, in_=i), sig, 16)

        def dma_nc(eng, out, in_, sig):
            op(eng, lambda e, o=out, i=in_: e.dma_start(
                out=o, in_=i, allow_slow_non_contiguous=True), sig, 16)

        dma("sync", ident_s[:], ident_d, sdma)
        dma_nc("sync", lngf_s[:], lng_d.rearrange("(kc p) -> p kc", p=128), sdma)
        dma_nc("sync", lnbf_s[:], lnb_d.rearrange("(kc p) -> p kc", p=128), sdma)
        dma("sync", wxp_s[:].rearrange("p (dc f) -> p dc f", f=96),
            wxp_d.rearrange("(dc p) f -> p dc f", p=128), sdma)
        dma("sync", wdt_s[:], wdt_d, sdma)
        dma("sync", nA_s[:].rearrange("p (dc n) -> p dc n", n=NST),
            alog_d.rearrange("(dc p) n -> p dc n", p=128), sdma)
        dma("sync", convw_s[:].rearrange("p (dc k) -> p dc k", k=4),
            convw_d.rearrange("(dc p) o k -> p dc (o k)", p=128), sdma)
        dma_nc("sync", convb_s[:], convb_d.rearrange("(dc p) -> p dc", p=128), sdma)
        dma_nc("sync", bdt_s[:], bdt_d.rearrange("(dc p) -> p dc", p=128), sdma)
        dma_nc("sync", dpar_s[:], dpar_d.rearrange("(dc p) -> p dc", p=128), sdma)
        n_const_dma = sdma.n

        W("scalar", sdma, n_const_dma)
        op("scalar", lambda e: e.activation(qs_s[:, 0:KDI * NST], nA_s[:], AF.Exp), sS)
        W("vector", sS)
        op("vector", lambda e: e.tensor_scalar(nA_s[:], qs_s[:, 0:KDI * NST], -1.0,
                                               None, op0=ALU.mult), sV)
        op("vector", lambda e: e.memset(scar_s[:], 0.0), sV)
        op("vector", lambda e: e.memset(cst_s[:, 0:1], 1e-5), sV)
        op("vector", lambda e: e.memset(cst_s[:, 1:2], 1.0), sV)
        op("vector", lambda e: e.memset(halo_s[:], 0.0), sV)
        op("vector", lambda e: e.memset(ffnout_s[:], 0.0), sV)

        winr_out_done = {}
        for cs in range(32):
            buf = cs % 2
            if cs >= 2:
                W("sync", sdma, winr_out_done[cs - 2])
            dma("sync",
                winb_s[:, buf * 1024:(buf + 1) * 1024].rearrange(
                    "p (kc c) -> p kc c", kc=8),
                win_d[:, cs * 128:(cs + 1) * 128].rearrange(
                    "(kc p) c -> p kc c", p=128), sdma)
            W("sync", sdma, sdma.n)
            dma("sync", winr_d[cs], winb_s[:, buf * 1024:(buf + 1) * 1024], sdma)
            winr_out_done[cs] = sdma.n

        def quant_rows(src_ap, ncols, dst_d, rowchunk):
            W("sync", sT)
            dma("sync", xc_s[:, 0:ncols], src_ap, sdma)
            ld_done = sdma.n
            nblk = ncols // 16
            W("scalar", sdma, ld_done)
            W("scalar", sV)
            op("scalar", lambda e: e.activation(sz_s[:, 0:ncols], xc_s[:, 0:ncols],
                                                AF.Abs), sS)
            W("vector", sS)
            op("vector", lambda e: e.reduce_max(
                scl_s[:, 0:nblk].unsqueeze(2),
                sz_s[:, 0:ncols].rearrange("p (a b) -> p a b", b=16),
                axis=AX.X, apply_absolute_value=False), sV)
            op("vector", lambda e: e.nop(cycle_cnt=700), sV)
            op("vector", lambda e: e.tensor_scalar(
                scl_s[:, 176:176 + nblk], scl_s[:, 0:nblk], 1.0 / 6.0, 1e-12,
                op0=ALU.mult, op1=ALU.max), sV)
            op("vector", lambda e: e.nop(cycle_cnt=700), sV)
            op("vector", lambda e: e.reciprocal(scl_s[:, 0:nblk],
                                                scl_s[:, 176:176 + nblk]), sV)
            op("vector", lambda e: e.nop(cycle_cnt=700), sV)
            rsb = scl_s[:, 0:nblk].unsqueeze(2).broadcast_to([128, nblk, 16])
            op("vector", lambda e, rsb=rsb: e.tensor_tensor(
                xc_s[:, 0:ncols].rearrange("p (a b) -> p a b", b=16),
                xc_s[:, 0:ncols].rearrange("p (a b) -> p a b", b=16),
                rsb, op=ALU.mult), sV)
            W("scalar", sV)
            op("scalar", lambda e: e.activation(sz_s[:, 0:ncols], xc_s[:, 0:ncols],
                                                AF.Abs), sS)
            W("vector", sS)
            op("vector", lambda e: e.tensor_single_scalar(
                Bbc_s[:, 0:ncols], sz_s[:, 0:ncols], 0.25, op=ALU.is_gt), sV)
            for th in (0.75, 1.25, 1.75):
                op("vector", lambda e, th=th: e.scalar_tensor_tensor(
                    Bbc_s[:, 0:ncols], sz_s[:, 0:ncols], th, Bbc_s[:, 0:ncols],
                    op0=ALU.is_gt, op1=ALU.add), sV)
            op("vector", lambda e: e.tensor_single_scalar(
                Cbc_s[:, 0:ncols], sz_s[:, 0:ncols], 2.5, op=ALU.is_gt), sV)
            for th in (3.5, 5.0, 5.0):
                op("vector", lambda e, th=th: e.scalar_tensor_tensor(
                    Cbc_s[:, 0:ncols], sz_s[:, 0:ncols], th, Cbc_s[:, 0:ncols],
                    op0=ALU.is_gt, op1=ALU.add), sV)
            W("scalar", sV)
            op("scalar", lambda e: e.activation(sz_s[:, 0:ncols], xc_s[:, 0:ncols],
                                                AF.Sign), sS)
            W("vector", sS)
            op("vector", lambda e: e.scalar_tensor_tensor(
                Bbc_s[:, 0:ncols], Bbc_s[:, 0:ncols], 0.5, Cbc_s[:, 0:ncols],
                op0=ALU.mult, op1=ALU.add), sV)
            op("vector", lambda e: e.tensor_tensor(
                Cbc_s[:, 0:ncols], Bbc_s[:, 0:ncols], sz_s[:, 0:ncols],
                op=ALU.mult), sV)
            ssb = scl_s[:, 176:176 + nblk].unsqueeze(2).broadcast_to([128, nblk, 16])
            op("vector", lambda e, ssb=ssb: e.tensor_tensor(
                xc_s[:, 0:ncols].rearrange("p (a b) -> p a b", b=16),
                Cbc_s[:, 0:ncols].rearrange("p (a b) -> p a b", b=16),
                ssb, op=ALU.mult), sV)
            wq_done = sV.n
            if _DEBUG and rowchunk == 0 and ncols == D and dst_d is wgT_d:
                W("sync", sV, wq_done)
                dma("sync", dbg["dbg_wq"], xc_s[:, 0:1024], sdma)
            ntile = ncols // 128
            for r0 in range(0, ntile, 4):
                nr = min(4, ntile - r0)
                W("tensor", sV, wq_done)
                if r0 >= 4:
                    W("tensor", sV)
                for r in range(nr):
                    op("tensor", lambda e, r=r, r0=r0: e.transpose(
                        pb[0][:, r * 128:(r + 1) * 128],
                        xc_s[:, (r0 + r) * 128:(r0 + r + 1) * 128],
                        ident_s[:]), sT)
                W("vector", sT)
                W("vector", sdma)
                op("vector", lambda e, nr=nr: e.tensor_copy(
                    tokscr_s[:, 0:nr * 128], pb[0][:, 0:nr * 128]), sV)
                ev = sV.n
                for r in range(nr):
                    W("sync", sV, ev)
                    dma("sync", dst_d[(r0 + r) * 128:(r0 + r + 1) * 128,
                                      rowchunk * 128:(rowchunk + 1) * 128],
                        tokscr_s[:, r * 128:(r + 1) * 128], sdma)
            return sdma.n

        for rc in range(KI):
            quant_rows(wg_d[rc * 128:(rc + 1) * 128, :], D, wgT_d, rc)
        for rc in range(KI):
            quant_rows(wu_d[rc * 128:(rc + 1) * 128, :], D, wuT_d, rc)
        for rc in range(KD):
            quant_rows(wdn_d[rc * 128:(rc + 1) * 128, :], IFF, wdT_d, rc)
        prep_spill_done = sdma.n
        if _DEBUG:
            W("sync", sdma, prep_spill_done)
            dma("sync", dbg["dbg_wgt"], wgT_d[0:128, :], sdma)
            dma("sync", dbg["dbg_wdt"], wdT_d[0:128, :], sdma)

        chunk_out_read = 0
        prev_scan_done = 0
        prev_wout_done = 0
        wd_bufs = (wd_s, wout_s[:, 0:512])
        wd_it = [0]
        wd_guard = {0: 0, 1: 0}
        prev_pb3_free = 0
        prev_bc_done = 0

        for c in range(min(NCH, _NCHUNKS)):
          try:
            def ck(k):
                if _PHASES < k:
                    raise _StopRec()
            W("sync", sV, chunk_out_read)
            for b in range(B2):
                dma("sync", xtok_s[:, b * D:(b + 1) * D], x_d[b, c * T:(c + 1) * T, :],
                    sdma)
            x_loaded = sdma.n
            if c == 0:
                W("sync", sdma, prep_spill_done)

            for b in range(B2):
                xsl = xtok_s[:, b * D:(b + 1) * D]
                xnsl = xntok_s[:, b * D:(b + 1) * D]
                tsl = tokscr_s[:, b * D:(b + 1) * D]
                s0 = 16 * b
                W("vector", sdma, x_loaded)
                op("vector", lambda e, xsl=xsl, s0=s0: e.tensor_reduce(
                    stat_s[:, s0:s0 + 1], xsl, axis=AX.X, op=ALU.add), sV)
                W("scalar", sV)
                op("scalar", lambda e, s0=s0: e.activation(
                    stat_s[:, s0 + 1:s0 + 2], stat_s[:, s0:s0 + 1], AF.Copy,
                    scale=1.0 / D), sS)
                W("vector", sS)
                op("vector", lambda e, xsl=xsl, tsl=tsl, s0=s0: e.tensor_scalar(
                    tsl, xsl, stat_s[:, s0 + 1:s0 + 2], None, op0=ALU.subtract), sV)
                W("scalar", sV)
                op("scalar", lambda e, tsl=tsl, s0=s0, b=b: e.activation(
                    qs_s[:, b * D:(b + 1) * D], tsl, AF.Square,
                    accum_out=stat_s[:, s0 + 2:s0 + 3]), sS)
                op("scalar", lambda e, s0=s0: e.activation(
                    stat_s[:, s0 + 3:s0 + 4], stat_s[:, s0 + 2:s0 + 3], AF.Sqrt,
                    scale=1.0 / D, bias=cst_s[:, 0:1]), sS)
                W("vector", sS)
                op("vector", lambda e, s0=s0: e.reciprocal(
                    stat_s[:, s0 + 4:s0 + 5], stat_s[:, s0 + 3:s0 + 4]), sV)
                op("vector", lambda e: e.nop(cycle_cnt=700), sV)
                op("vector", lambda e, xnsl=xnsl, tsl=tsl, s0=s0: e.tensor_scalar(
                    xnsl, tsl, stat_s[:, s0 + 4:s0 + 5], None, op0=ALU.mult), sV)
                op("vector", lambda e, xsl=xsl, xnsl=xnsl: e.tensor_tensor(
                    xsl, xsl, xnsl, op=ALU.add), sV)
                op("vector", lambda e, xnsl=xnsl, s0=s0: e.tensor_reduce(
                    stat_s[:, s0 + 5:s0 + 6], xnsl, axis=AX.X, op=ALU.add), sV)
                W("scalar", sV)
                op("scalar", lambda e, s0=s0: e.activation(
                    stat_s[:, s0 + 6:s0 + 7], stat_s[:, s0 + 5:s0 + 6], AF.Copy,
                    scale=1.0 / D), sS)
                W("vector", sS)
                op("vector", lambda e, xnsl=xnsl, tsl=tsl, s0=s0: e.tensor_scalar(
                    tsl, xnsl, stat_s[:, s0 + 6:s0 + 7], None, op0=ALU.subtract), sV)
                W("scalar", sV)
                op("scalar", lambda e, tsl=tsl, s0=s0, b=b: e.activation(
                    qs_s[:, b * D:(b + 1) * D], tsl, AF.Square,
                    accum_out=stat_s[:, s0 + 7:s0 + 8]), sS)
                op("scalar", lambda e, s0=s0: e.activation(
                    stat_s[:, s0 + 8:s0 + 9], stat_s[:, s0 + 7:s0 + 8], AF.Sqrt,
                    scale=1.0 / D, bias=cst_s[:, 0:1]), sS)
                W("vector", sS)
                op("vector", lambda e, s0=s0: e.reciprocal(
                    stat_s[:, s0 + 9:s0 + 10], stat_s[:, s0 + 8:s0 + 9]), sV)
                op("vector", lambda e: e.nop(cycle_cnt=700), sV)
                op("vector", lambda e, tsl=tsl, s0=s0: e.tensor_scalar(
                    tsl, tsl, stat_s[:, s0 + 9:s0 + 10], None, op0=ALU.mult), sV)
            h1_done = sV.n
            if _DEBUG and c == 0:
                W("sync", sV, h1_done)
                dma("sync", dbg["dbg_xn"], xntok_s[:], sdma)
                dma("sync", dbg["dbg_stat"], stat_s[:], sdma)
                for _e in ("vector", "scalar", "tensor"):
                    W(_e, sdma)
            ck(1)

            for half in range(2):
                W("tensor", sV, h1_done)
                if half == 1:
                    W("tensor", sV)
                for b in range(B2):
                    for r in range(4):
                        kc = half * 4 + r
                        op("tensor", lambda e, b=b, r=r, kc=kc: e.transpose(
                            pb[b][:, r * 128:(r + 1) * 128],
                            tokscr_s[:, b * D + kc * 128:b * D + (kc + 1) * 128],
                            ident_s[:]), sT)
                W("vector", sT)
                for b in range(B2):
                    for r in range(4):
                        kc = half * 4 + r
                        op("vector", lambda e, b=b, r=r, kc=kc: e.tensor_scalar(
                            hfm_s[:, kc * TT2 + b * T:kc * TT2 + (b + 1) * T],
                            pb[b][:, r * 128:(r + 1) * 128],
                            lngf_s[:, kc:kc + 1], lnbf_s[:, kc:kc + 1],
                            op0=ALU.mult, op1=ALU.add), sV)
            hfm_done = sV.n
            ck(1.3)

            for b in range(B2):
                xnsl = xntok_s[:, b * D:(b + 1) * D]
                tsl = tokscr_s[:, b * D:(b + 1) * D]
                q0 = qs_s[:, 0 * D:1 * D]
                q1 = qs_s[:, 1 * D:2 * D]
                q2 = qs_s[:, 2 * D:3 * D]
                W("scalar", sV)
                W("vector", sT)
                op("scalar", lambda e, xnsl=xnsl, q0=q0: e.activation(
                    q0, xnsl, AF.Abs), sS)
                W("vector", sS)
                op("vector", lambda e, q0=q0: e.reduce_max(
                    scl_s[:, 0:64].unsqueeze(2),
                    q0.rearrange("p (a b) -> p a b", b=16),
                    axis=AX.X, apply_absolute_value=False), sV)
                op("vector", lambda e: e.nop(cycle_cnt=700), sV)
                op("vector", lambda e: e.tensor_scalar(
                    scl_s[:, 176:176 + 64], scl_s[:, 0:64], 1.0 / 6.0, 1e-12,
                    op0=ALU.mult, op1=ALU.max), sV)
                op("vector", lambda e: e.nop(cycle_cnt=700), sV)
                op("vector", lambda e: e.reciprocal(
                    scl_s[:, 0:64], scl_s[:, 176:176 + 64]), sV)
                op("vector", lambda e: e.nop(cycle_cnt=700), sV)
                rsb = scl_s[:, 0:64].unsqueeze(2).broadcast_to([128, 64, 16])
                op("vector", lambda e, xnsl=xnsl, tsl=tsl, rsb=rsb: e.tensor_tensor(
                    tsl.rearrange("p (a b) -> p a b", b=16),
                    xnsl.rearrange("p (a b) -> p a b", b=16), rsb, op=ALU.mult), sV)
                W("scalar", sV)
                op("scalar", lambda e, tsl=tsl, q0=q0: e.activation(
                    q0, tsl, AF.Abs), sS)
                W("vector", sS)
                op("vector", lambda e, q0=q0, q1=q1: e.tensor_single_scalar(
                    q1, q0, 0.25, op=ALU.is_gt), sV)
                for th in (0.75, 1.25, 1.75):
                    op("vector", lambda e, th=th, q0=q0, q1=q1: e.scalar_tensor_tensor(
                        q1, q0, th, q1, op0=ALU.is_gt, op1=ALU.add), sV)
                op("vector", lambda e, q0=q0, q2=q2: e.tensor_single_scalar(
                    q2, q0, 2.5, op=ALU.is_gt), sV)
                for th in (3.5, 5.0, 5.0):
                    op("vector", lambda e, th=th, q0=q0, q2=q2: e.scalar_tensor_tensor(
                        q2, q0, th, q2, op0=ALU.is_gt, op1=ALU.add), sV)
                W("scalar", sV)
                op("scalar", lambda e, tsl=tsl, q0=q0: e.activation(
                    q0, tsl, AF.Sign), sS)
                W("vector", sS)
                op("vector", lambda e, q1=q1, q2=q2: e.scalar_tensor_tensor(
                    q1, q1, 0.5, q2, op0=ALU.mult, op1=ALU.add), sV)
                op("vector", lambda e, q0=q0, q1=q1, q2=q2: e.tensor_tensor(
                    q2, q1, q0, op=ALU.mult), sV)
                ssb = scl_s[:, 176:176 + 64].unsqueeze(2).broadcast_to([128, 64, 16])
                op("vector", lambda e, tsl=tsl, q2=q2, ssb=ssb: e.tensor_tensor(
                    tsl.rearrange("p (a b) -> p a b", b=16),
                    q2.rearrange("p (a b) -> p a b", b=16), ssb, op=ALU.mult), sV)
            xq_done = sV.n
            if _DEBUG and c == 0:
                W("sync", sV, xq_done)
                dma("sync", dbg["dbg_tokscr"], tokscr_s[:], sdma)
                for _e in ("vector", "scalar", "tensor"):
                    W(_e, sdma)
            ck(1.6)

            for half in range(2):
                W("tensor", sV, xq_done)
                if half == 1:
                    W("tensor", sS)
                for b in range(B2):
                    for r in range(4):
                        kc = half * 4 + r
                        op("tensor", lambda e, b=b, r=r, kc=kc: e.transpose(
                            pb[b][:, r * 128:(r + 1) * 128],
                            tokscr_s[:, b * D + kc * 128:b * D + (kc + 1) * 128],
                            ident_s[:]), sT)
                W("scalar", sT)
                for b in range(B2):
                    for r in range(4):
                        kc = half * 4 + r
                        op("scalar", lambda e, b=b, r=r, kc=kc: e.copy(
                            xqfm_s[:, kc * TT2 + b * T:kc * TT2 + (b + 1) * T],
                            pb[b][:, r * 128:(r + 1) * 128]), sS)
            xqfm_done = sS.n

            if _DEBUG and c == 0:
                W("sync", sV, xq_done)
                dma("sync", dbg["dbg_hfm"], hfm_s[:], sdma)
                W("sync", sS, xqfm_done)
                dma("sync", dbg["dbg_xq"], xqfm_s[:], sdma)
                for _e in ("vector", "scalar", "tensor"):
                    W(_e, sdma)
            ck(2)

            win_mm_done = {}
            if c > 0:
                win_mm_done[-2] = win_mm_done[-1] = prev_win_last
            evac_done = {}
            xproj_emitted = []

            def emit_xproj(dc):
                W("tensor", sS, silu_done[dc])
                op("tensor", lambda e, dc=dc: e.matmul(
                    pb[2][0:96, 0:TT2],
                    wxp_s[:, dc * 96:(dc + 1) * 96],
                    xc_s[:, dc * TT2:(dc + 1) * TT2],
                    start=(dc == 0), stop=(dc == KDI - 1)), sT)
                xproj_emitted.append(dc)

            silu_done = {}
            for cs in range(32):
                buf = cs % 2
                if cs >= 2 or c > 0:
                    W("sync", sT, win_mm_done[cs - 2])
                dma("sync",
                    winb_s[:, buf * 1024:(buf + 1) * 1024],
                    winr_d[cs], sdma)
                win_ld = sdma.n
                if 2 <= cs < 18:
                    emit_xproj(cs - 2)
                W("tensor", sdma, win_ld)
                W("tensor", sV, hfm_done)
                W("tensor", sS, xqfm_done)
                if cs >= 2:
                    W("tensor", evac_done[cs - 2][1], evac_done[cs - 2][0])
                for kc in range(8):
                    op("tensor", lambda e, buf=buf, kc=kc, cs=cs: e.matmul(
                        pb[buf][:, 0:TT2],
                        winb_s[:, buf * 1024 + kc * 128:buf * 1024 + (kc + 1) * 128],
                        hfm_s[:, kc * TT2:(kc + 1) * TT2],
                        start=(kc == 0), stop=(kc == 7)), sT)
                win_mm_done[cs] = sT.n
                if cs < 16:
                    dc = cs
                    W("vector", sT, win_mm_done[cs])
                    if dc >= 1:
                        W("vector", sS, silu_done[dc - 1])
                    for b in range(B2):
                        op("vector", lambda e, b=b, dc=dc: e.tensor_copy(
                            xin_s[:, b * (T + 3):b * (T + 3) + 3],
                            halo_s[:, dc * 6 + b * 3:dc * 6 + (b + 1) * 3]), sV)
                        op("vector", lambda e, b=b, buf=buf: e.tensor_copy(
                            xin_s[:, b * (T + 3) + 3:(b + 1) * (T + 3)],
                            pb[buf][:, b * T:(b + 1) * T]), sV)
                    evac_done[cs] = (sV.n, sV)
                    for b in range(B2):
                        op("vector", lambda e, b=b, dc=dc: e.tensor_copy(
                            halo_s[:, dc * 6 + b * 3:dc * 6 + (b + 1) * 3],
                            xin_s[:, b * (T + 3) + T:(b + 1) * (T + 3)]), sV)
                        op("vector", lambda e, b=b, dc=dc: e.tensor_scalar(
                            cacc_s[:, b * T:(b + 1) * T],
                            xin_s[:, b * (T + 3):b * (T + 3) + T],
                            convw_s[:, dc * 4:dc * 4 + 1], None, op0=ALU.mult), sV)
                        for k in range(1, 4):
                            op("vector", lambda e, b=b, dc=dc, k=k:
                               e.scalar_tensor_tensor(
                                   cacc_s[:, b * T:(b + 1) * T],
                                   xin_s[:, b * (T + 3) + k:b * (T + 3) + k + T],
                                   convw_s[:, dc * 4 + k:dc * 4 + k + 1],
                                   cacc_s[:, b * T:(b + 1) * T],
                                   op0=ALU.mult, op1=ALU.add), sV)
                    W("scalar", sV)
                    op("scalar", lambda e, dc=dc: e.activation(
                        xc_s[:, dc * TT2:(dc + 1) * TT2], cacc_s[:, 0:TT2],
                        AF.Silu, bias=convb_s[:, dc:dc + 1]), sS)
                    silu_done[dc] = sS.n
                else:
                    zc = cs - 16
                    W("scalar", sT, win_mm_done[cs])
                    op("scalar", lambda e, buf=buf, zc=zc: e.activation(
                        sz_s[:, zc * TT2:(zc + 1) * TT2], pb[buf][:, 0:TT2],
                        AF.Silu), sS)
                    evac_done[cs] = (sS.n, sS)
            for dc in range(KDI):
                if dc not in xproj_emitted:
                    emit_xproj(dc)
            xproj_done = sT.n
            prev_win_last = win_mm_done[31]

            W("vector", sT, xproj_done)
            W("vector", sdma, prev_bc_done)
            op("vector", lambda e: e.tensor_copy(dbc_s[:], pb[2][0:96, 0:TT2]), sV)
            dbc_done = sV.n

            if _DEBUG and c == 0:
                W("sync", sV, dbc_done)
                dma("sync", dbg["dbg_xc"], xc_s[:], sdma)
                dma("sync", dbg["dbg_sz"], sz_s[:], sdma)
                dma("sync", dbg["dbg_dbc"], dbc_s[:], sdma)
                for _e in ("vector", "scalar", "tensor"):
                    W(_e, sdma)

            W("sync", sV, dbc_done)
            for b in range(B2):
                for n in range(NST):
                    dma("sync",
                        Bbc_s[:, (b * NST + n) * T:(b * NST + n + 1) * T],
                        dbc_s[64 + n:65 + n, b * T:(b + 1) * T]
                        .unsqueeze(1).broadcast_to([1, 128, T]), sdma)
                    dma("sync",
                        Cbc_s[:, (b * NST + n) * T:(b * NST + n + 1) * T],
                        dbc_s[80 + n:81 + n, b * T:(b + 1) * T]
                        .unsqueeze(1).broadcast_to([1, 128, T]), sdma)
            ck(3)
            bc_done = sdma.n
            prev_bc_done = bc_done

            dA_bufs = (dA0_s, dA1_s)
            dBu_bufs = (dBu0_s, dBu1_s)
            dA_scan_done = {-2: prev_scan_done, -1: prev_scan_done}
            delta_used = {k: prev_scan_done for k in range(-4, 0)}
            scan_it = [0]
            dbuf_guard = {0: prev_scan_done, 1: prev_scan_done}
            delta_mm_done = {}
            pb3_free = {-1: prev_pb3_free}
            delta_ready = {}

            def emit_delta(dc):
                dbuf = delta_s[:, (dc % 4) * TT2:(dc % 4 + 1) * TT2]
                W("tensor", sS, pb3_free[dc - 1])
                W("tensor", sV, dbc_done)
                op("tensor", lambda e: e.matmul(
                    pb[3][:, 0:TT2],
                    wdt_s[:, dc * 128:(dc + 1) * 128],
                    dbc_s[0:64, :], start=True, stop=True), sT)
                delta_mm_done[dc] = sT.n
                W("scalar", sT, delta_mm_done[dc])
                W("scalar", sV, delta_used[dc - 4])
                op("scalar", lambda e: e.activation(
                    dtmp_s[:], pb[3][:, 0:TT2], AF.Exp,
                    bias=bdt_s[:, dc:dc + 1]), sS)
                pb3_free[dc] = sS.n
                op("scalar", lambda e, dbuf=dbuf: e.activation(
                    dbuf, dtmp_s[:], AF.Ln, bias=cst_s[:, 1:2]), sS)
                delta_ready[dc] = sS.n

            def emit_scan_dc(dc):
                dbuf = delta_s[:, (dc % 4) * TT2:(dc % 4 + 1) * TT2]
                W("vector", sS, delta_ready[dc])
                op("vector", lambda e, dbuf=dbuf: e.tensor_tensor(
                    du_s[:], dbuf, xc_s[:, dc * TT2:(dc + 1) * TT2],
                    op=ALU.mult), sV)
                du_done = sV.n
                for half in range(2):
                    ab = dA_bufs[(2 * dc + half) % 2]
                    W("scalar", sV, dA_scan_done[(dc * 2 + half) - 2])
                    dbv2 = dbuf.rearrange("p (b t) -> p b t", b=2)
                    for n8 in range(8):
                        n = half * 8 + n8
                        o = ab[:].rearrange("p (b n t) -> p b n t", b=2, n=8)[:, :, n8, :]
                        op("scalar", lambda e, o=o, dbv2=dbv2, dc=dc, n=n:
                           e.activation(o, dbv2, AF.Exp,
                                        scale=nA_s[:, dc * NST + n:dc * NST + n + 1]),
                           sS)
                    dA_done = sS.n
                    W("vector", sS, dA_done)
                    W("vector", sdma, bc_done)
                    scoff = dc * 32 + half * 16
                    for b in range(B2):
                        it = scan_it[0]
                        scan_it[0] += 1
                        dbuf_i = dBu_bufs[it % 2]
                        scv_b = scar_s[:, scoff + b * 8:scoff + b * 8 + 8]
                        W("gpsimd", sV, du_done)
                        W("gpsimd", sdma, bc_done)
                        W("gpsimd", sV, dbuf_guard[it % 2])
                        op("gpsimd", lambda e, b=b, half=half, dbuf_i=dbuf_i:
                           e.tensor_tensor(
                               dbuf_i[:].rearrange("p (n t) -> p n t", n=8),
                               du_s[:, b * T:(b + 1) * T].unsqueeze(1)
                               .broadcast_to([128, 8, T]),
                               Bbc_s[:, (b * NST + half * 8) * T:
                                     (b * NST + half * 8 + 8) * T]
                               .rearrange("p (n t) -> p n t", n=8),
                               op=ALU.mult), sG)
                        W("vector", sG)
                        for n8 in range(8):
                            op("vector", lambda e, ab=ab, b=b, n8=n8,
                               dbuf_i=dbuf_i, scol=scoff + b * 8 + n8:
                               e.tensor_tensor_scan(
                                   ss_s[:, n8 * T:(n8 + 1) * T],
                                   ab[:, b * 8 * T + n8 * T:
                                      b * 8 * T + (n8 + 1) * T],
                                   dbuf_i[:, n8 * T:(n8 + 1) * T],
                                   scar_s[:, scol:scol + 1],
                                   op0=ALU.mult, op1=ALU.add), sV)
                        if half == 1 and b == 0:
                            for b0 in range(B2):
                                tb0 = 3 * T * b0
                                op("vector", lambda e, b0=b0, tb0=tb0:
                                   e.tensor_tensor(
                                       ytmp_s[:, b0 * T:(b0 + 1) * T],
                                       tree_s[:, tb0:tb0 + T],
                                       tree_s[:, tb0 + T:tb0 + 2 * T],
                                       op=ALU.add), sV)
                        ssv = ss_s[:].rearrange("p (n t) -> p n t", n=8)
                        op("vector", lambda e, b=b, half=half, ssv=ssv,
                           dbuf_i=dbuf_i: e.tensor_tensor(
                               dbuf_i[:].rearrange("p (n t) -> p n t", n=8), ssv,
                               Cbc_s[:, (b * NST + half * 8) * T:
                                     (b * NST + half * 8 + 8) * T]
                               .rearrange("p (n t) -> p n t", n=8),
                               op=ALU.mult), sV)
                        op("vector", lambda e, ssv=ssv, scv_b=scv_b: e.tensor_copy(
                            scv_b, ssv[:, :, T - 1]), sV)
                        tb = 3 * T * b
                        op("vector", lambda e, dbuf_i=dbuf_i: e.tensor_tensor(
                            ss_s[:, 0:4 * T],
                            dbuf_i[:].rearrange("p (a two t) -> p a two t", two=2,
                                                t=T)[:, :, 0, :],
                            dbuf_i[:].rearrange("p (a two t) -> p a two t", two=2,
                                                t=T)[:, :, 1, :], op=ALU.add), sV)
                        op("vector", lambda e, tb=tb: e.tensor_tensor(
                            tree_s[:, tb:tb + 2 * T],
                            ss_s[:, 0:4 * T].rearrange(
                                "p (a two t) -> p a two t", two=2, t=T)[:, :, 0, :],
                            ss_s[:, 0:4 * T].rearrange(
                                "p (a two t) -> p a two t", two=2, t=T)[:, :, 1, :],
                            op=ALU.add), sV)
                        dbuf_guard[it % 2] = sV.n
                    if half == 1:
                        op("vector", lambda e: e.nop(cycle_cnt=700), sV)
                        for b in range(B2):
                            tb = 3 * T * b
                            op("vector", lambda e, b=b, tb=tb: e.tensor_tensor(
                                tree_s[:, tb + 2 * T:tb + 3 * T],
                                tree_s[:, tb:tb + T],
                                tree_s[:, tb + T:tb + 2 * T], op=ALU.add), sV)
                        op("vector", lambda e: e.nop(cycle_cnt=700), sV)
                        for b in range(B2):
                            tb = 3 * T * b
                            op("vector", lambda e, b=b, tb=tb: e.tensor_tensor(
                                ytmp_s[:, b * T:(b + 1) * T],
                                ytmp_s[:, b * T:(b + 1) * T],
                                tree_s[:, tb + 2 * T:tb + 3 * T], op=ALU.add), sV)
                        op("vector", lambda e: e.nop(cycle_cnt=700), sV)
                        op("vector", lambda e, dc=dc: e.scalar_tensor_tensor(
                            ss_s[:, 0:TT2],
                            xc_s[:, dc * TT2:(dc + 1) * TT2],
                            dpar_s[:, dc:dc + 1],
                            ytmp_s[:], op0=ALU.mult, op1=ALU.add), sV)
                        op("vector", lambda e: e.nop(cycle_cnt=700), sV)
                        op("vector", lambda e, dc=dc: e.tensor_tensor(
                            xc_s[:, dc * TT2:(dc + 1) * TT2],
                            ss_s[:, 0:TT2],
                            sz_s[:, dc * TT2:(dc + 1) * TT2],
                            op=ALU.mult), sV)
                    dA_scan_done[dc * 2 + half] = sV.n
                delta_used[dc] = sV.n

            down_evac = {}

            def emit_ffn_j(j):
                joff, w = JSLABS[j]
                nkc = w // 128
                for wi, (wT, pbs) in enumerate(((wgT_d, (4, 5)), (wuT_d, (6, 7)))):
                    for kh in range(2):
                        W("sync", sT)
                        for kc4 in range(4):
                            kc = kh * 4 + kc4
                            dma("sync", wgu_s[:, kc4 * 512:kc4 * 512 + w],
                                wT[kc * 128:(kc + 1) * 128, joff:joff + w], sdma)
                        W("tensor", sdma)
                        W("tensor", sS, xqfm_done)
                        if j > 0 and wi == 0 and kh == 0:
                            W("tensor", sV, down_evac[j - 1])
                        for b in range(B2):
                            for kc4 in range(4):
                                kc = kh * 4 + kc4
                                op("tensor", lambda e, b=b, kc=kc, kc4=kc4, w=w,
                                   pbx=pbs[b], kh=kh: e.matmul(
                                       pb[pbx][:, 0:w],
                                       xqfm_s[:, kc * TT2 + b * T:
                                              kc * TT2 + (b + 1) * T],
                                       wgu_s[:, kc4 * 512:kc4 * 512 + w],
                                       start=(kh == 0 and kc4 == 0),
                                       stop=(kh == 1 and kc4 == 3)), sT)
                gu_done = sT.n
                for b in range(B2):
                    tsl = tokscr_s[:, b * D:b * D + w]
                    q0 = qs_s[:, 0:w]
                    q1 = qs_s[:, D:D + w]
                    q2 = qs_s[:, 2 * D:2 * D + w]
                    W("scalar", sT, gu_done)
                    op("scalar", lambda e, b=b, tsl=tsl: e.activation(
                        tsl, pb[4 + b][:, 0:w], AF.Silu), sS)
                    if _DEBUG and c == 0 and j == 0 and b == 0:
                        W("vector", sT, gu_done)
                        op("vector", lambda e, w=w: e.tensor_copy(
                            qs_s[:, 0:w], pb[6][:, 0:w]), sV)
                        op("vector", lambda e, w=w: e.tensor_copy(
                            qs_s[:, D:D + w], pb[4][:, 0:w]), sV)
                        W("sync", sV)
                        dma("sync", dbg["dbg_up"], qs_s[:, 0:512], sdma)
                        dma("sync", dbg["dbg_gate"], qs_s[:, D:D + 512], sdma)
                        W("vector", sdma)
                    W("vector", sS)
                    op("vector", lambda e, b=b, tsl=tsl: e.tensor_tensor(
                        tsl, tsl, pb[6 + b][:, 0:w], op=ALU.mult), sV)
                    nblk = w // 16
                    W("scalar", sV)
                    op("scalar", lambda e, tsl=tsl, q0=q0: e.activation(
                        q0, tsl, AF.Abs), sS)
                    W("vector", sS)
                    op("vector", lambda e, q0=q0, nblk=nblk: e.reduce_max(
                        scl_s[:, 0:nblk].unsqueeze(2),
                        q0.rearrange("p (a b) -> p a b", b=16),
                        axis=AX.X, apply_absolute_value=False), sV)
                    op("vector", lambda e: e.nop(cycle_cnt=700), sV)
                    op("vector", lambda e, nblk=nblk: e.tensor_scalar(
                        scl_s[:, 176:176 + nblk], scl_s[:, 0:nblk], 1.0 / 6.0, 1e-12,
                        op0=ALU.mult, op1=ALU.max), sV)
                    op("vector", lambda e: e.nop(cycle_cnt=700), sV)
                    op("vector", lambda e, nblk=nblk: e.reciprocal(
                        scl_s[:, 0:nblk], scl_s[:, 176:176 + nblk]), sV)
                    op("vector", lambda e: e.nop(cycle_cnt=700), sV)
                    rsb = scl_s[:, 0:nblk].unsqueeze(2).broadcast_to([128, nblk, 16])
                    op("vector", lambda e, tsl=tsl, rsb=rsb, nblk=nblk:
                       e.tensor_tensor(
                           tsl.rearrange("p (a b) -> p a b", b=16),
                           tsl.rearrange("p (a b) -> p a b", b=16), rsb,
                           op=ALU.mult), sV)
                    W("scalar", sV)
                    op("scalar", lambda e, tsl=tsl, q0=q0: e.activation(
                        q0, tsl, AF.Abs), sS)
                    W("vector", sS)
                    op("vector", lambda e, q0=q0, q1=q1: e.tensor_single_scalar(
                        q1, q0, 0.25, op=ALU.is_gt), sV)
                    for th in (0.75, 1.25, 1.75):
                        op("vector", lambda e, th=th, q0=q0, q1=q1:
                           e.scalar_tensor_tensor(q1, q0, th, q1,
                                                  op0=ALU.is_gt, op1=ALU.add), sV)
                    op("vector", lambda e, q0=q0, q2=q2: e.tensor_single_scalar(
                        q2, q0, 2.5, op=ALU.is_gt), sV)
                    for th in (3.5, 5.0, 5.0):
                        op("vector", lambda e, th=th, q0=q0, q2=q2:
                           e.scalar_tensor_tensor(q2, q0, th, q2,
                                                  op0=ALU.is_gt, op1=ALU.add), sV)
                    W("scalar", sV)
                    op("scalar", lambda e, tsl=tsl, q0=q0: e.activation(
                        q0, tsl, AF.Sign), sS)
                    W("vector", sS)
                    op("vector", lambda e, q1=q1, q2=q2: e.scalar_tensor_tensor(
                        q1, q1, 0.5, q2, op0=ALU.mult, op1=ALU.add), sV)
                    op("vector", lambda e, q0=q0, q1=q1, q2=q2: e.tensor_tensor(
                        q2, q1, q0, op=ALU.mult), sV)
                    ssb = scl_s[:, 176:176 + nblk].unsqueeze(2).broadcast_to(
                        [128, nblk, 16])
                    op("vector", lambda e, tsl=tsl, q2=q2, ssb=ssb, nblk=nblk:
                       e.tensor_tensor(
                           tsl.rearrange("p (a b) -> p a b", b=16),
                           q2.rearrange("p (a b) -> p a b", b=16), ssb,
                           op=ALU.mult), sV)
                    hq_done = sV.n
                    if _DEBUG and c == 0 and j == 0 and b == 0:
                        W("sync", sV, hq_done)
                        dma("sync", dbg["dbg_h"], tokscr_s[:], sdma)
                        W("vector", sdma)
                        W("scalar", sdma)
                    W("tensor", sV, hq_done)
                    for r in range(nkc):
                        op("tensor", lambda e, b=b, r=r, tsl=tsl: e.transpose(
                            pb[4 + b][:, r * 128:(r + 1) * 128],
                            tsl[:, r * 128:(r + 1) * 128], ident_s[:]), sT)
                    W("vector", sT)
                    op("vector", lambda e, b=b, nkc=nkc: e.tensor_copy(
                        hfm_s[:, b * 1024:b * 1024 + nkc * 128],
                        pb[4 + b][:, 0:nkc * 128]), sV)
                    hqT_done = sV.n
                    W("tensor", sV, hqT_done)
                    for ds in range(2):
                        for kc4 in range(nkc):
                            i = wd_it[0]
                            wd_it[0] += 1
                            buf = wd_bufs[i % 2]
                            W("sync", sT, wd_guard[i % 2])
                            if i % 2 == 1:
                                W("sync", sT, prev_wout_done)
                            kc = joff // 128 + kc4
                            dma("sync", buf[:, 0:512],
                                wdT_d[kc * 128:(kc + 1) * 128,
                                      ds * 512:(ds + 1) * 512], sdma)
                            W("tensor", sdma, sdma.n)
                            op("tensor", lambda e, b=b, kc4=kc4, ds=ds,
                               buf=buf, nkc=nkc: e.matmul(
                                   pb[ds][:, 0:512],
                                   hfm_s[:, b * 1024 + kc4 * 128:
                                         b * 1024 + (kc4 + 1) * 128],
                                   buf[:, 0:512],
                                   start=(kc4 == 0), stop=(kc4 == nkc - 1)), sT)
                            wd_guard[i % 2] = sT.n
                    dn_done = sT.n
                    W("vector", sT, dn_done)
                    for ds in range(2):
                        op("vector", lambda e, b=b, ds=ds: e.tensor_tensor(
                            ffnout_s[:, b * D + ds * 512:b * D + (ds + 1) * 512],
                            pb[ds][:, 0:512],
                            ffnout_s[:, b * D + ds * 512:b * D + (ds + 1) * 512],
                            op=ALU.add), sV)
                down_evac[j] = sV.n

            ffn_after = {1: 0, 4: 1, 7: 2, 9: 3, 11: 4, 13: 5}
            emit_delta(0)
            emit_delta(1)
            for dc in range(KDI):
                if dc + 2 < KDI:
                    emit_delta(dc + 2)
                emit_scan_dc(dc)
                if dc in ffn_after:
                    emit_ffn_j(ffn_after[dc])
            scan_all_done = sV.n
            prev_scan_done = scan_all_done
            prev_pb3_free = pb3_free[KDI - 1]

            mm_done = {(-1, 0): prev_wout_done, (-1, 1): prev_wout_done}
            ld_done2 = {}
            W("scalar", sT, wd_guard[1])
            for dc in range(KDI):
                for ds in range(2):
                    W("scalar", sT, mm_done[(dc - 1, ds)])
                    dma("scalar", wout_s[:, ds * 512:(ds + 1) * 512],
                        wout_d[dc * 128:(dc + 1) * 128,
                               ds * 512:(ds + 1) * 512], qdma)
                    ld_done2[(dc, ds)] = qdma.n
                for ds in range(2):
                    W("tensor", qdma, ld_done2[(dc, ds)])
                    W("tensor", sV, scan_all_done)
                    for b in range(B2):
                        op("tensor", lambda e, dc=dc, b=b, ds=ds: e.matmul(
                            pb[4 + 2 * b + ds][:, 0:512],
                            xc_s[:, dc * TT2 + b * T:dc * TT2 + (b + 1) * T],
                            wout_s[:, ds * 512:(ds + 1) * 512],
                            start=(dc == 0), stop=(dc == KDI - 1)), sT)
                    mm_done[(dc, ds)] = sT.n
            wout_done = sT.n
            prev_wout_done = wout_done

            if _DEBUG and c == 0:
                W("sync", sV, scan_all_done)
                dma("sync", dbg["dbg_y"], xc_s[:], sdma)
                dma("sync", dbg["dbg_B"], Bbc_s[:], sdma)
                dma("sync", dbg["dbg_ffn"], ffnout_s[:], sdma)
                W("sync", sS)
                dma("sync", dbg["dbg_delta"], delta_s[:, TT2:2 * TT2], sdma)
                for _e in ("vector", "scalar", "tensor"):
                    W(_e, sdma)

            ck(4)
            W("vector", sT, wout_done)
            for b in range(B2):
                for ds in range(2):
                    op("vector", lambda e, b=b, ds=ds: e.tensor_tensor(
                        xntok_s[:, b * D + ds * 512:b * D + (ds + 1) * 512],
                        pb[4 + 2 * b + ds][:, 0:512],
                        ffnout_s[:, b * D + ds * 512:b * D + (ds + 1) * 512],
                        op=ALU.add), sV)
                op("vector", lambda e, b=b: e.tensor_tensor(
                    xntok_s[:, b * D:(b + 1) * D],
                    xntok_s[:, b * D:(b + 1) * D],
                    xtok_s[:, b * D:(b + 1) * D], op=ALU.add), sV)
                op("vector", lambda e, b=b: e.memset(
                    ffnout_s[:, b * D:(b + 1) * D], 0.0), sV)
            asm_done = sV.n
            chunk_out_read = asm_done
            W("sync", sV, asm_done)
            for b in range(B2):
                dma("sync", out_d[b, c * T:(c + 1) * T, :],
                    xntok_s[:, b * D:(b + 1) * D], sdma)
          except _StopRec:
            pass

        if _DEBUG:
            for _e in ("vector", "scalar", "tensor"):
                W("sync", own[_e])
            dma("sync", dbg["dbg_done"], ident_s[:], sdma)
        final_dma = sdma.n
        P["sync"].append(lambda e, v=final_dma: e.wait_ge(sdma.h, v))

        @block.sync
        def _(sync):
            for f in P["sync"]:
                f(sync)

        @block.tensor
        def _(tensor):
            for f in P["tensor"]:
                f(tensor)

        @block.vector
        def _(vector):
            for f in P["vector"]:
                f(vector)

        @block.scalar
        def _(scalar):
            for f in P["scalar"]:
                f(scalar)

        @block.gpsimd
        def _(gpsimd):
            for f in P["gpsimd"]:
                f(gpsimd)

    return nc


_NC_CACHE = {}


def kernel(**inputs):
    key = "nc"
    if key not in _NC_CACHE:
        _NC_CACHE[key] = build_nc()
    nc = _NC_CACHE[key]

    x = np.ascontiguousarray(np.asarray(inputs["L0_attn_x"], dtype=np.float32))
    wmap = {
        "ln_g": "L0_attn_ln_g", "ln_b": "L0_attn_ln_b", "W_in": "L0_attn_W_in",
        "conv_w": "L0_attn_conv_w", "conv_b": "L0_attn_conv_b",
        "W_xproj": "L0_attn_W_xproj", "W_dt": "L0_attn_W_dt",
        "b_dt": "L0_attn_b_dt", "A_log": "L0_attn_A_log",
        "D_param": "L0_attn_D_param", "W_out": "L0_attn_W_out",
        "w_gate": "L0_ffn_w_gate", "w_up": "L0_ffn_w_up",
        "w_down": "L0_ffn_w_down",
    }
    common = {k: np.ascontiguousarray(np.asarray(inputs[v], dtype=np.float32))
              for k, v in wmap.items()}
    common["ident"] = np.eye(128, dtype=np.float32)

    in_maps = []
    for c in range(8):
        m = dict(common)
        m["x"] = np.ascontiguousarray(x[2 * c:2 * c + 2])
        in_maps.append(m)

    res = run_bass_kernel_spmd(nc, in_maps, core_ids=list(range(8)))
    out = np.concatenate([np.asarray(r["out"]) for r in res.results], axis=0)
    return out


# version 24
# speedup vs baseline: 1.2111x; 1.0116x over previous
import numpy as np
import contextlib
import concourse.bass as bass
import concourse.mybir as mybir
from concourse.bass_utils import run_bass_kernel_spmd

F32 = mybir.dt.float32
AF = mybir.ActivationFunctionType
ALU = mybir.AluOpType
AX = mybir.AxisListType

B_FULL = 16
B2 = 2
L = 2048
D = 1024
DI = 2048
NST = 16
RDT = 64
IFF = 2816
T = 128
NCH = L // T
KD = D // 128
KDI = DI // 128
KI = IFF // 128
TT2 = 2 * T
JSLABS = [(0, 512), (512, 512), (1024, 512), (1536, 512), (2048, 512), (2560, 256)]

_DEBUG = False
_PHASES = 99

_NCHUNKS = NCH

ENGS = ("sync", "tensor", "vector", "scalar", "gpsimd")


class _StopRec(Exception):
    pass


class Sem:
    __slots__ = ("h", "n")

    def __init__(self, h):
        self.h = h
        self.n = 0


def build_nc():
    nc = bass.Bass("TRN2", target_bir_lowering=False, debug=False)

    x_d = nc.dram_tensor("x", [B2, L, D], F32, kind="ExternalInput").ap()
    lng_d = nc.dram_tensor("ln_g", [D], F32, kind="ExternalInput").ap()
    lnb_d = nc.dram_tensor("ln_b", [D], F32, kind="ExternalInput").ap()
    win_d = nc.dram_tensor("W_in", [D, 2 * DI], F32, kind="ExternalInput").ap()
    convw_d = nc.dram_tensor("conv_w", [DI, 1, 4], F32, kind="ExternalInput").ap()
    convb_d = nc.dram_tensor("conv_b", [DI], F32, kind="ExternalInput").ap()
    wxp_d = nc.dram_tensor("W_xproj", [DI, 96], F32, kind="ExternalInput").ap()
    wdt_d = nc.dram_tensor("W_dt", [RDT, DI], F32, kind="ExternalInput").ap()
    bdt_d = nc.dram_tensor("b_dt", [DI], F32, kind="ExternalInput").ap()
    alog_d = nc.dram_tensor("A_log", [DI, NST], F32, kind="ExternalInput").ap()
    dpar_d = nc.dram_tensor("D_param", [DI], F32, kind="ExternalInput").ap()
    wout_d = nc.dram_tensor("W_out", [DI, D], F32, kind="ExternalInput").ap()
    wg_d = nc.dram_tensor("w_gate", [IFF, D], F32, kind="ExternalInput").ap()
    wu_d = nc.dram_tensor("w_up", [IFF, D], F32, kind="ExternalInput").ap()
    wdn_d = nc.dram_tensor("w_down", [D, IFF], F32, kind="ExternalInput").ap()
    ident_d = nc.dram_tensor("ident", [128, 128], F32, kind="ExternalInput").ap()

    out_d = nc.dram_tensor("out", [B2, L, D], F32, kind="ExternalOutput").ap()

    winr_d = nc.dram_tensor("winr", [32, 128, 8 * 128], F32, kind="Internal").ap()
    wgT_d = nc.dram_tensor("wgT", [D, IFF], F32, kind="Internal").ap()
    wuT_d = nc.dram_tensor("wuT", [D, IFF], F32, kind="Internal").ap()
    wdT_d = nc.dram_tensor("wdT", [IFF, D], F32, kind="Internal").ap()

    dbg = {}
    if _DEBUG:
        for nm, shp in [("dbg_xn", [128, 2048]), ("dbg_hfm", [128, 2048]),
                        ("dbg_xq", [128, 2048]), ("dbg_xc", [128, 4096]),
                        ("dbg_sz", [128, 4096]), ("dbg_dbc", [96, 256]),
                        ("dbg_delta", [128, 256]), ("dbg_y", [128, 4096]),
                        ("dbg_ffn", [128, 2048]), ("dbg_B", [128, 4096]),
                        ("dbg_wq", [128, 1024]), ("dbg_stat", [128, 32]),
                        ("dbg_tokscr", [128, 2048]), ("dbg_done", [128, 128]),
                        ("dbg_scar", [128, 512]), ("dbg_dbu", [128, 1024]),
                        ("dbg_da", [128, 2048]), ("dbg_fix", [128, 64]),
                        ("dbg_wgt", [128, 2816]), ("dbg_wdt", [128, 1024]),
                        ("dbg_up", [128, 512]), ("dbg_gate", [128, 512]),
                        ("dbg_h", [128, 2048])]:
            dbg[nm] = nc.dram_tensor(nm, shp, F32, kind="ExternalOutput").ap()

    es = contextlib.ExitStack()
    with es:
        def sb(name, shape):
            return es.enter_context(nc.sbuf_tensor(name, shape, F32))

        ident_s = sb("ident_s", [128, 128])
        lngf_s = sb("lngf_s", [128, KD])
        lnbf_s = sb("lnbf_s", [128, KD])
        wxp_s = sb("wxp_s", [128, KDI * 96])
        wdt_s = sb("wdt_s", [RDT, DI])
        nA_s = sb("nA_s", [128, KDI * NST])
        convw_s = sb("convw_s", [128, KDI * 4])
        convb_s = sb("convb_s", [128, KDI])
        bdt_s = sb("bdt_s", [128, KDI])
        dpar_s = sb("dpar_s", [128, KDI])
        scar_s = sb("scar_s", [128, 512])
        halo_s = sb("halo_s", [128, KDI * 6])
        stat_s = sb("stat_s", [128, 32])
        cst_s = sb("cst_s", [128, 2])
        xtok_s = sb("xtok_s", [128, 2 * D])
        xntok_s = sb("xntok_s", [128, 2 * D])
        tokscr_s = sb("tokscr_s", [128, 2 * D])
        hfm_s = sb("hfm_s", [128, KD * TT2])
        xqfm_s = sb("xqfm_s", [128, KD * TT2])
        xc_s = sb("xc_s", [128, KDI * TT2])
        sz_s = sb("sz_s", [128, KDI * TT2])
        dtmp_s = sb("dtmp_s", [128, TT2])
        delta_s = sb("delta_s", [128, 4 * TT2])
        du_s = sb("du_s", [128, TT2])
        dbc_s = sb("dbc_s", [96, TT2])
        Bbc_s = sb("Bbc_s", [128, B2 * NST * T])
        Cbc_s = sb("Cbc_s", [128, B2 * NST * T])
        dA0_s = sb("dA0_s", [128, B2 * 8 * T])
        dA1_s = sb("dA1_s", [128, B2 * 8 * T])
        dBu0_s = sb("dBu0_s", [128, 8 * T])
        dBu1_s = sb("dBu1_s", [128, 8 * T])
        ss_s = sb("ss_s", [128, 8 * T])
        tree_s = sb("tree_s", [128, 2 * 3 * T])
        ytmp_s = sb("ytmp_s", [128, TT2])
        qs_s = sb("qs_s", [128, 3 * D])
        scl_s = sb("scl_s", [128, 2 * 176])
        ffnout_s = sb("ffnout_s", [128, 2 * D])
        xin_s = sb("xin_s", [128, 2 * (T + 3)])
        cacc_s = sb("cacc_s", [128, TT2])
        winb_s = sb("winb_s", [128, 2 * 8 * 128])
        wgu_s = sb("wgu_s", [128, 2048])
        wd_s = sb("wd_s", [128, 512])
        wout_s = sb("wout_s", [128, 1024])

        pb = [es.enter_context(nc.psum_tensor("pb%d" % i, [128, 512], F32))
              for i in range(8)]

        sdma = Sem(es.enter_context(nc.semaphore("sdma")))
        qdma = Sem(es.enter_context(nc.semaphore("qdma")))
        sT = Sem(es.enter_context(nc.semaphore("sT")))
        sV = Sem(es.enter_context(nc.semaphore("sV")))
        sS = Sem(es.enter_context(nc.semaphore("sS")))
        sG = Sem(es.enter_context(nc.semaphore("sG")))
        block = es.enter_context(nc.Block())

        P = {e: [] for e in ENGS}
        waited = {e: {} for e in ENGS}
        own = {"sync": None, "tensor": sT, "vector": sV, "scalar": sS,
               "gpsimd": sG}

        def op(eng, f, sig=None, k=1):
            if sig is not None:
                sig.n += k
                tgt = sig.h

                def thunk(e, f=f, tgt=tgt, k=k):
                    f(e).then_inc(tgt, k)
            else:
                def thunk(e, f=f):
                    f(e)
            P[eng].append(thunk)

        def W(eng, sem, v=None):
            v = sem.n if v is None else v
            if v <= 0 or sem is own[eng]:
                return
            if waited[eng].get(id(sem), -1) >= v:
                return
            waited[eng][id(sem)] = v
            P[eng].append(lambda e, s=sem.h, v=v: e.wait_ge(s, v))

        def dma(eng, out, in_, sig):
            op(eng, lambda e, o=out, i=in_: e.dma_start(out=o, in_=i), sig, 16)

        def dma_nc(eng, out, in_, sig):
            op(eng, lambda e, o=out, i=in_: e.dma_start(
                out=o, in_=i, allow_slow_non_contiguous=True), sig, 16)

        dma("sync", ident_s[:], ident_d, sdma)
        dma_nc("sync", lngf_s[:], lng_d.rearrange("(kc p) -> p kc", p=128), sdma)
        dma_nc("sync", lnbf_s[:], lnb_d.rearrange("(kc p) -> p kc", p=128), sdma)
        dma("sync", wxp_s[:].rearrange("p (dc f) -> p dc f", f=96),
            wxp_d.rearrange("(dc p) f -> p dc f", p=128), sdma)
        dma("sync", wdt_s[:], wdt_d, sdma)
        dma("sync", nA_s[:].rearrange("p (dc n) -> p dc n", n=NST),
            alog_d.rearrange("(dc p) n -> p dc n", p=128), sdma)
        dma("sync", convw_s[:].rearrange("p (dc k) -> p dc k", k=4),
            convw_d.rearrange("(dc p) o k -> p dc (o k)", p=128), sdma)
        dma_nc("sync", convb_s[:], convb_d.rearrange("(dc p) -> p dc", p=128), sdma)
        dma_nc("sync", bdt_s[:], bdt_d.rearrange("(dc p) -> p dc", p=128), sdma)
        dma_nc("sync", dpar_s[:], dpar_d.rearrange("(dc p) -> p dc", p=128), sdma)
        n_const_dma = sdma.n

        W("scalar", sdma, n_const_dma)
        op("scalar", lambda e: e.activation(qs_s[:, 0:KDI * NST], nA_s[:], AF.Exp), sS)
        W("vector", sS)
        op("vector", lambda e: e.tensor_scalar(nA_s[:], qs_s[:, 0:KDI * NST], -1.0,
                                               None, op0=ALU.mult), sV)
        op("vector", lambda e: e.memset(scar_s[:], 0.0), sV)
        op("vector", lambda e: e.memset(cst_s[:, 0:1], 1e-5), sV)
        op("vector", lambda e: e.memset(cst_s[:, 1:2], 1.0), sV)
        op("vector", lambda e: e.memset(halo_s[:], 0.0), sV)
        op("vector", lambda e: e.memset(ffnout_s[:], 0.0), sV)

        winr_out_done = {}
        for cs in range(32):
            buf = cs % 2
            if cs >= 2:
                W("sync", sdma, winr_out_done[cs - 2])
            dma("sync",
                winb_s[:, buf * 1024:(buf + 1) * 1024].rearrange(
                    "p (kc c) -> p kc c", kc=8),
                win_d[:, cs * 128:(cs + 1) * 128].rearrange(
                    "(kc p) c -> p kc c", p=128), sdma)
            W("sync", sdma, sdma.n)
            dma("sync", winr_d[cs], winb_s[:, buf * 1024:(buf + 1) * 1024], sdma)
            winr_out_done[cs] = sdma.n

        def quant_rows(src_ap, ncols, dst_d, rowchunk):
            W("sync", sT)
            dma("sync", xc_s[:, 0:ncols], src_ap, sdma)
            ld_done = sdma.n
            nblk = ncols // 16
            W("scalar", sdma, ld_done)
            W("scalar", sV)
            op("scalar", lambda e: e.activation(sz_s[:, 0:ncols], xc_s[:, 0:ncols],
                                                AF.Abs), sS)
            W("vector", sS)
            op("vector", lambda e: e.reduce_max(
                scl_s[:, 0:nblk].unsqueeze(2),
                sz_s[:, 0:ncols].rearrange("p (a b) -> p a b", b=16),
                axis=AX.X, apply_absolute_value=False), sV)
            op("vector", lambda e: e.nop(cycle_cnt=700), sV)
            op("vector", lambda e: e.tensor_scalar(
                scl_s[:, 176:176 + nblk], scl_s[:, 0:nblk], 1.0 / 6.0, 1e-12,
                op0=ALU.mult, op1=ALU.max), sV)
            op("vector", lambda e: e.nop(cycle_cnt=700), sV)
            op("vector", lambda e: e.reciprocal(scl_s[:, 0:nblk],
                                                scl_s[:, 176:176 + nblk]), sV)
            op("vector", lambda e: e.nop(cycle_cnt=700), sV)
            rsb = scl_s[:, 0:nblk].unsqueeze(2).broadcast_to([128, nblk, 16])
            op("vector", lambda e, rsb=rsb: e.tensor_tensor(
                xc_s[:, 0:ncols].rearrange("p (a b) -> p a b", b=16),
                xc_s[:, 0:ncols].rearrange("p (a b) -> p a b", b=16),
                rsb, op=ALU.mult), sV)
            W("scalar", sV)
            op("scalar", lambda e: e.activation(sz_s[:, 0:ncols], xc_s[:, 0:ncols],
                                                AF.Abs), sS)
            W("vector", sS)
            op("vector", lambda e: e.tensor_single_scalar(
                Bbc_s[:, 0:ncols], sz_s[:, 0:ncols], 0.25, op=ALU.is_gt), sV)
            for th in (0.75, 1.25, 1.75):
                op("vector", lambda e, th=th: e.scalar_tensor_tensor(
                    Bbc_s[:, 0:ncols], sz_s[:, 0:ncols], th, Bbc_s[:, 0:ncols],
                    op0=ALU.is_gt, op1=ALU.add), sV)
            op("vector", lambda e: e.tensor_single_scalar(
                Cbc_s[:, 0:ncols], sz_s[:, 0:ncols], 2.5, op=ALU.is_gt), sV)
            for th in (3.5, 5.0, 5.0):
                op("vector", lambda e, th=th: e.scalar_tensor_tensor(
                    Cbc_s[:, 0:ncols], sz_s[:, 0:ncols], th, Cbc_s[:, 0:ncols],
                    op0=ALU.is_gt, op1=ALU.add), sV)
            W("scalar", sV)
            op("scalar", lambda e: e.activation(sz_s[:, 0:ncols], xc_s[:, 0:ncols],
                                                AF.Sign), sS)
            W("vector", sS)
            op("vector", lambda e: e.scalar_tensor_tensor(
                Bbc_s[:, 0:ncols], Bbc_s[:, 0:ncols], 0.5, Cbc_s[:, 0:ncols],
                op0=ALU.mult, op1=ALU.add), sV)
            op("vector", lambda e: e.tensor_tensor(
                Cbc_s[:, 0:ncols], Bbc_s[:, 0:ncols], sz_s[:, 0:ncols],
                op=ALU.mult), sV)
            ssb = scl_s[:, 176:176 + nblk].unsqueeze(2).broadcast_to([128, nblk, 16])
            op("vector", lambda e, ssb=ssb: e.tensor_tensor(
                xc_s[:, 0:ncols].rearrange("p (a b) -> p a b", b=16),
                Cbc_s[:, 0:ncols].rearrange("p (a b) -> p a b", b=16),
                ssb, op=ALU.mult), sV)
            wq_done = sV.n
            if _DEBUG and rowchunk == 0 and ncols == D and dst_d is wgT_d:
                W("sync", sV, wq_done)
                dma("sync", dbg["dbg_wq"], xc_s[:, 0:1024], sdma)
            ntile = ncols // 128
            for r0 in range(0, ntile, 4):
                nr = min(4, ntile - r0)
                W("tensor", sV, wq_done)
                if r0 >= 4:
                    W("tensor", sV)
                for r in range(nr):
                    op("tensor", lambda e, r=r, r0=r0: e.transpose(
                        pb[0][:, r * 128:(r + 1) * 128],
                        xc_s[:, (r0 + r) * 128:(r0 + r + 1) * 128],
                        ident_s[:]), sT)
                W("vector", sT)
                W("vector", sdma)
                op("vector", lambda e, nr=nr: e.tensor_copy(
                    tokscr_s[:, 0:nr * 128], pb[0][:, 0:nr * 128]), sV)
                ev = sV.n
                for r in range(nr):
                    W("sync", sV, ev)
                    dma("sync", dst_d[(r0 + r) * 128:(r0 + r + 1) * 128,
                                      rowchunk * 128:(rowchunk + 1) * 128],
                        tokscr_s[:, r * 128:(r + 1) * 128], sdma)
            return sdma.n

        for rc in range(KI):
            quant_rows(wg_d[rc * 128:(rc + 1) * 128, :], D, wgT_d, rc)
        for rc in range(KI):
            quant_rows(wu_d[rc * 128:(rc + 1) * 128, :], D, wuT_d, rc)
        for rc in range(KD):
            quant_rows(wdn_d[rc * 128:(rc + 1) * 128, :], IFF, wdT_d, rc)
        prep_spill_done = sdma.n
        if _DEBUG:
            W("sync", sdma, prep_spill_done)
            dma("sync", dbg["dbg_wgt"], wgT_d[0:128, :], sdma)
            dma("sync", dbg["dbg_wdt"], wdT_d[0:128, :], sdma)

        chunk_out_read = 0
        prev_scan_done = 0
        prev_wout_done = 0
        wd_bufs = (wd_s, wout_s[:, 0:512])
        wd_it = [0]
        wd_guard = {0: 0, 1: 0}
        wgu_guard = {0: 0, 1: 0, 2: 0, 3: 0}
        prev_pb3_free = 0
        prev_bc_done = 0

        for c in range(min(NCH, _NCHUNKS)):
          try:
            def ck(k):
                if _PHASES < k:
                    raise _StopRec()
            W("sync", sV, chunk_out_read)
            for b in range(B2):
                dma("sync", xtok_s[:, b * D:(b + 1) * D], x_d[b, c * T:(c + 1) * T, :],
                    sdma)
            x_loaded = sdma.n
            if c == 0:
                W("sync", sdma, prep_spill_done)

            for b in range(B2):
                xsl = xtok_s[:, b * D:(b + 1) * D]
                xnsl = xntok_s[:, b * D:(b + 1) * D]
                tsl = tokscr_s[:, b * D:(b + 1) * D]
                s0 = 16 * b
                W("vector", sdma, x_loaded)
                op("vector", lambda e, xsl=xsl, s0=s0: e.tensor_reduce(
                    stat_s[:, s0:s0 + 1], xsl, axis=AX.X, op=ALU.add), sV)
                W("scalar", sV)
                op("scalar", lambda e, s0=s0: e.activation(
                    stat_s[:, s0 + 1:s0 + 2], stat_s[:, s0:s0 + 1], AF.Copy,
                    scale=1.0 / D), sS)
                W("vector", sS)
                op("vector", lambda e, xsl=xsl, tsl=tsl, s0=s0: e.tensor_scalar(
                    tsl, xsl, stat_s[:, s0 + 1:s0 + 2], None, op0=ALU.subtract), sV)
                W("scalar", sV)
                op("scalar", lambda e, tsl=tsl, s0=s0, b=b: e.activation(
                    qs_s[:, b * D:(b + 1) * D], tsl, AF.Square,
                    accum_out=stat_s[:, s0 + 2:s0 + 3]), sS)
                op("scalar", lambda e, s0=s0: e.activation(
                    stat_s[:, s0 + 3:s0 + 4], stat_s[:, s0 + 2:s0 + 3], AF.Sqrt,
                    scale=1.0 / D, bias=cst_s[:, 0:1]), sS)
                W("vector", sS)
                op("vector", lambda e, s0=s0: e.reciprocal(
                    stat_s[:, s0 + 4:s0 + 5], stat_s[:, s0 + 3:s0 + 4]), sV)
                op("vector", lambda e: e.nop(cycle_cnt=700), sV)
                op("vector", lambda e, xnsl=xnsl, tsl=tsl, s0=s0: e.tensor_scalar(
                    xnsl, tsl, stat_s[:, s0 + 4:s0 + 5], None, op0=ALU.mult), sV)
                op("vector", lambda e, xsl=xsl, xnsl=xnsl: e.tensor_tensor(
                    xsl, xsl, xnsl, op=ALU.add), sV)
                op("vector", lambda e, xnsl=xnsl, s0=s0: e.tensor_reduce(
                    stat_s[:, s0 + 5:s0 + 6], xnsl, axis=AX.X, op=ALU.add), sV)
                W("scalar", sV)
                op("scalar", lambda e, s0=s0: e.activation(
                    stat_s[:, s0 + 6:s0 + 7], stat_s[:, s0 + 5:s0 + 6], AF.Copy,
                    scale=1.0 / D), sS)
                W("vector", sS)
                op("vector", lambda e, xnsl=xnsl, tsl=tsl, s0=s0: e.tensor_scalar(
                    tsl, xnsl, stat_s[:, s0 + 6:s0 + 7], None, op0=ALU.subtract), sV)
                W("scalar", sV)
                op("scalar", lambda e, tsl=tsl, s0=s0, b=b: e.activation(
                    qs_s[:, b * D:(b + 1) * D], tsl, AF.Square,
                    accum_out=stat_s[:, s0 + 7:s0 + 8]), sS)
                op("scalar", lambda e, s0=s0: e.activation(
                    stat_s[:, s0 + 8:s0 + 9], stat_s[:, s0 + 7:s0 + 8], AF.Sqrt,
                    scale=1.0 / D, bias=cst_s[:, 0:1]), sS)
                W("vector", sS)
                op("vector", lambda e, s0=s0: e.reciprocal(
                    stat_s[:, s0 + 9:s0 + 10], stat_s[:, s0 + 8:s0 + 9]), sV)
                op("vector", lambda e: e.nop(cycle_cnt=700), sV)
                op("vector", lambda e, tsl=tsl, s0=s0: e.tensor_scalar(
                    tsl, tsl, stat_s[:, s0 + 9:s0 + 10], None, op0=ALU.mult), sV)
            h1_done = sV.n
            if _DEBUG and c == 0:
                W("sync", sV, h1_done)
                dma("sync", dbg["dbg_xn"], xntok_s[:], sdma)
                dma("sync", dbg["dbg_stat"], stat_s[:], sdma)
                for _e in ("vector", "scalar", "tensor"):
                    W(_e, sdma)
            ck(1)

            for half in range(2):
                W("tensor", sV, h1_done)
                if half == 1:
                    W("tensor", sV)
                for b in range(B2):
                    for r in range(4):
                        kc = half * 4 + r
                        op("tensor", lambda e, b=b, r=r, kc=kc: e.transpose(
                            pb[b][:, r * 128:(r + 1) * 128],
                            tokscr_s[:, b * D + kc * 128:b * D + (kc + 1) * 128],
                            ident_s[:]), sT)
                W("vector", sT)
                for b in range(B2):
                    for r in range(4):
                        kc = half * 4 + r
                        op("vector", lambda e, b=b, r=r, kc=kc: e.tensor_scalar(
                            hfm_s[:, kc * TT2 + b * T:kc * TT2 + (b + 1) * T],
                            pb[b][:, r * 128:(r + 1) * 128],
                            lngf_s[:, kc:kc + 1], lnbf_s[:, kc:kc + 1],
                            op0=ALU.mult, op1=ALU.add), sV)
            hfm_done = sV.n
            ck(1.3)

            for b in range(B2):
                xnsl = xntok_s[:, b * D:(b + 1) * D]
                tsl = tokscr_s[:, b * D:(b + 1) * D]
                q0 = qs_s[:, 0 * D:1 * D]
                q1 = qs_s[:, 1 * D:2 * D]
                q2 = qs_s[:, 2 * D:3 * D]
                W("scalar", sV)
                W("vector", sT)
                op("scalar", lambda e, xnsl=xnsl, q0=q0: e.activation(
                    q0, xnsl, AF.Abs), sS)
                W("vector", sS)
                op("vector", lambda e, q0=q0: e.reduce_max(
                    scl_s[:, 0:64].unsqueeze(2),
                    q0.rearrange("p (a b) -> p a b", b=16),
                    axis=AX.X, apply_absolute_value=False), sV)
                op("vector", lambda e: e.nop(cycle_cnt=700), sV)
                op("vector", lambda e: e.tensor_scalar(
                    scl_s[:, 176:176 + 64], scl_s[:, 0:64], 1.0 / 6.0, 1e-12,
                    op0=ALU.mult, op1=ALU.max), sV)
                op("vector", lambda e: e.nop(cycle_cnt=700), sV)
                op("vector", lambda e: e.reciprocal(
                    scl_s[:, 0:64], scl_s[:, 176:176 + 64]), sV)
                op("vector", lambda e: e.nop(cycle_cnt=700), sV)
                rsb = scl_s[:, 0:64].unsqueeze(2).broadcast_to([128, 64, 16])
                op("vector", lambda e, xnsl=xnsl, tsl=tsl, rsb=rsb: e.tensor_tensor(
                    tsl.rearrange("p (a b) -> p a b", b=16),
                    xnsl.rearrange("p (a b) -> p a b", b=16), rsb, op=ALU.mult), sV)
                W("scalar", sV)
                op("scalar", lambda e, tsl=tsl, q0=q0: e.activation(
                    q0, tsl, AF.Abs), sS)
                W("vector", sS)
                op("vector", lambda e, q0=q0, q1=q1: e.tensor_single_scalar(
                    q1, q0, 0.25, op=ALU.is_gt), sV)
                for th in (0.75, 1.25, 1.75):
                    op("vector", lambda e, th=th, q0=q0, q1=q1: e.scalar_tensor_tensor(
                        q1, q0, th, q1, op0=ALU.is_gt, op1=ALU.add), sV)
                op("vector", lambda e, q0=q0, q2=q2: e.tensor_single_scalar(
                    q2, q0, 2.5, op=ALU.is_gt), sV)
                for th in (3.5, 5.0, 5.0):
                    op("vector", lambda e, th=th, q0=q0, q2=q2: e.scalar_tensor_tensor(
                        q2, q0, th, q2, op0=ALU.is_gt, op1=ALU.add), sV)
                W("scalar", sV)
                op("scalar", lambda e, tsl=tsl, q0=q0: e.activation(
                    q0, tsl, AF.Sign), sS)
                W("vector", sS)
                op("vector", lambda e, q1=q1, q2=q2: e.scalar_tensor_tensor(
                    q1, q1, 0.5, q2, op0=ALU.mult, op1=ALU.add), sV)
                op("vector", lambda e, q0=q0, q1=q1, q2=q2: e.tensor_tensor(
                    q2, q1, q0, op=ALU.mult), sV)
                ssb = scl_s[:, 176:176 + 64].unsqueeze(2).broadcast_to([128, 64, 16])
                op("vector", lambda e, tsl=tsl, q2=q2, ssb=ssb: e.tensor_tensor(
                    tsl.rearrange("p (a b) -> p a b", b=16),
                    q2.rearrange("p (a b) -> p a b", b=16), ssb, op=ALU.mult), sV)
            xq_done = sV.n
            if _DEBUG and c == 0:
                W("sync", sV, xq_done)
                dma("sync", dbg["dbg_tokscr"], tokscr_s[:], sdma)
                for _e in ("vector", "scalar", "tensor"):
                    W(_e, sdma)
            ck(1.6)

            for half in range(2):
                W("tensor", sV, xq_done)
                if half == 1:
                    W("tensor", sS)
                for b in range(B2):
                    for r in range(4):
                        kc = half * 4 + r
                        op("tensor", lambda e, b=b, r=r, kc=kc: e.transpose(
                            pb[b][:, r * 128:(r + 1) * 128],
                            tokscr_s[:, b * D + kc * 128:b * D + (kc + 1) * 128],
                            ident_s[:]), sT)
                W("scalar", sT)
                for b in range(B2):
                    for r in range(4):
                        kc = half * 4 + r
                        op("scalar", lambda e, b=b, r=r, kc=kc: e.copy(
                            xqfm_s[:, kc * TT2 + b * T:kc * TT2 + (b + 1) * T],
                            pb[b][:, r * 128:(r + 1) * 128]), sS)
            xqfm_done = sS.n

            if _DEBUG and c == 0:
                W("sync", sV, xq_done)
                dma("sync", dbg["dbg_hfm"], hfm_s[:], sdma)
                W("sync", sS, xqfm_done)
                dma("sync", dbg["dbg_xq"], xqfm_s[:], sdma)
                for _e in ("vector", "scalar", "tensor"):
                    W(_e, sdma)
            ck(2)

            win_mm_done = {}
            if c > 0:
                win_mm_done[-2] = win_mm_done[-1] = prev_win_last
            evac_done = {}
            xproj_emitted = []

            def emit_xproj(dc):
                W("tensor", sS, silu_done[dc])
                op("tensor", lambda e, dc=dc: e.matmul(
                    pb[2][0:96, 0:TT2],
                    wxp_s[:, dc * 96:(dc + 1) * 96],
                    xc_s[:, dc * TT2:(dc + 1) * TT2],
                    start=(dc == 0), stop=(dc == KDI - 1)), sT)
                xproj_emitted.append(dc)

            silu_done = {}
            for cs in range(32):
                buf = cs % 2
                if cs >= 2 or c > 0:
                    W("sync", sT, win_mm_done[cs - 2])
                dma("sync",
                    winb_s[:, buf * 1024:(buf + 1) * 1024],
                    winr_d[cs], sdma)
                win_ld = sdma.n
                if 2 <= cs < 18:
                    emit_xproj(cs - 2)
                W("tensor", sdma, win_ld)
                W("tensor", sV, hfm_done)
                W("tensor", sS, xqfm_done)
                if cs >= 2:
                    W("tensor", evac_done[cs - 2][1], evac_done[cs - 2][0])
                for kc in range(8):
                    op("tensor", lambda e, buf=buf, kc=kc, cs=cs: e.matmul(
                        pb[buf][:, 0:TT2],
                        winb_s[:, buf * 1024 + kc * 128:buf * 1024 + (kc + 1) * 128],
                        hfm_s[:, kc * TT2:(kc + 1) * TT2],
                        start=(kc == 0), stop=(kc == 7)), sT)
                win_mm_done[cs] = sT.n
                if cs < 16:
                    dc = cs
                    W("vector", sT, win_mm_done[cs])
                    if dc >= 1:
                        W("vector", sS, silu_done[dc - 1])
                    for b in range(B2):
                        op("vector", lambda e, b=b, dc=dc: e.tensor_copy(
                            xin_s[:, b * (T + 3):b * (T + 3) + 3],
                            halo_s[:, dc * 6 + b * 3:dc * 6 + (b + 1) * 3]), sV)
                        op("vector", lambda e, b=b, buf=buf: e.tensor_copy(
                            xin_s[:, b * (T + 3) + 3:(b + 1) * (T + 3)],
                            pb[buf][:, b * T:(b + 1) * T]), sV)
                    evac_done[cs] = (sV.n, sV)
                    for b in range(B2):
                        op("vector", lambda e, b=b, dc=dc: e.tensor_copy(
                            halo_s[:, dc * 6 + b * 3:dc * 6 + (b + 1) * 3],
                            xin_s[:, b * (T + 3) + T:(b + 1) * (T + 3)]), sV)
                        op("vector", lambda e, b=b, dc=dc: e.tensor_scalar(
                            cacc_s[:, b * T:(b + 1) * T],
                            xin_s[:, b * (T + 3):b * (T + 3) + T],
                            convw_s[:, dc * 4:dc * 4 + 1], None, op0=ALU.mult), sV)
                        for k in range(1, 4):
                            op("vector", lambda e, b=b, dc=dc, k=k:
                               e.scalar_tensor_tensor(
                                   cacc_s[:, b * T:(b + 1) * T],
                                   xin_s[:, b * (T + 3) + k:b * (T + 3) + k + T],
                                   convw_s[:, dc * 4 + k:dc * 4 + k + 1],
                                   cacc_s[:, b * T:(b + 1) * T],
                                   op0=ALU.mult, op1=ALU.add), sV)
                    W("scalar", sV)
                    op("scalar", lambda e, dc=dc: e.activation(
                        xc_s[:, dc * TT2:(dc + 1) * TT2], cacc_s[:, 0:TT2],
                        AF.Silu, bias=convb_s[:, dc:dc + 1]), sS)
                    silu_done[dc] = sS.n
                else:
                    zc = cs - 16
                    W("scalar", sT, win_mm_done[cs])
                    op("scalar", lambda e, buf=buf, zc=zc: e.activation(
                        sz_s[:, zc * TT2:(zc + 1) * TT2], pb[buf][:, 0:TT2],
                        AF.Silu), sS)
                    evac_done[cs] = (sS.n, sS)
            for dc in range(KDI):
                if dc not in xproj_emitted:
                    emit_xproj(dc)
            xproj_done = sT.n
            prev_win_last = win_mm_done[31]

            W("vector", sT, xproj_done)
            W("vector", sdma, prev_bc_done)
            op("vector", lambda e: e.tensor_copy(dbc_s[:], pb[2][0:96, 0:TT2]), sV)
            dbc_done = sV.n

            if _DEBUG and c == 0:
                W("sync", sV, dbc_done)
                dma("sync", dbg["dbg_xc"], xc_s[:], sdma)
                dma("sync", dbg["dbg_sz"], sz_s[:], sdma)
                dma("sync", dbg["dbg_dbc"], dbc_s[:], sdma)
                for _e in ("vector", "scalar", "tensor"):
                    W(_e, sdma)

            W("sync", sV, dbc_done)
            for b in range(B2):
                for n in range(NST):
                    dma("sync",
                        Bbc_s[:, (b * NST + n) * T:(b * NST + n + 1) * T],
                        dbc_s[64 + n:65 + n, b * T:(b + 1) * T]
                        .unsqueeze(1).broadcast_to([1, 128, T]), sdma)
                    dma("sync",
                        Cbc_s[:, (b * NST + n) * T:(b * NST + n + 1) * T],
                        dbc_s[80 + n:81 + n, b * T:(b + 1) * T]
                        .unsqueeze(1).broadcast_to([1, 128, T]), sdma)
            ck(3)
            bc_done = sdma.n
            prev_bc_done = bc_done

            dA_bufs = (dA0_s, dA1_s)
            dBu_bufs = (dBu0_s, dBu1_s)
            dA_scan_done = {-2: prev_scan_done, -1: prev_scan_done}
            delta_used = {k: prev_scan_done for k in range(-4, 0)}
            scan_it = [0]
            dbuf_guard = {0: prev_scan_done, 1: prev_scan_done}
            delta_mm_done = {}
            pb3_free = {-1: prev_pb3_free}
            delta_ready = {}

            def emit_delta(dc):
                dbuf = delta_s[:, (dc % 4) * TT2:(dc % 4 + 1) * TT2]
                W("tensor", sS, pb3_free[dc - 1])
                W("tensor", sV, dbc_done)
                op("tensor", lambda e: e.matmul(
                    pb[3][:, 0:TT2],
                    wdt_s[:, dc * 128:(dc + 1) * 128],
                    dbc_s[0:64, :], start=True, stop=True), sT)
                delta_mm_done[dc] = sT.n
                W("scalar", sT, delta_mm_done[dc])
                W("scalar", sV, delta_used[dc - 4])
                op("scalar", lambda e: e.activation(
                    dtmp_s[:], pb[3][:, 0:TT2], AF.Exp,
                    bias=bdt_s[:, dc:dc + 1]), sS)
                pb3_free[dc] = sS.n
                op("scalar", lambda e, dbuf=dbuf: e.activation(
                    dbuf, dtmp_s[:], AF.Ln, bias=cst_s[:, 1:2]), sS)
                delta_ready[dc] = sS.n

            def emit_scan_dc(dc):
                dbuf = delta_s[:, (dc % 4) * TT2:(dc % 4 + 1) * TT2]
                W("vector", sS, delta_ready[dc])
                op("vector", lambda e, dbuf=dbuf: e.tensor_tensor(
                    du_s[:], dbuf, xc_s[:, dc * TT2:(dc + 1) * TT2],
                    op=ALU.mult), sV)
                du_done = sV.n
                for half in range(2):
                    ab = dA_bufs[(2 * dc + half) % 2]
                    W("scalar", sV, dA_scan_done[(dc * 2 + half) - 2])
                    dbv2 = dbuf.rearrange("p (b t) -> p b t", b=2)
                    for n8 in range(8):
                        n = half * 8 + n8
                        o = ab[:].rearrange("p (b n t) -> p b n t", b=2, n=8)[:, :, n8, :]
                        op("scalar", lambda e, o=o, dbv2=dbv2, dc=dc, n=n:
                           e.activation(o, dbv2, AF.Exp,
                                        scale=nA_s[:, dc * NST + n:dc * NST + n + 1]),
                           sS)
                    dA_done = sS.n
                    W("vector", sS, dA_done)
                    W("vector", sdma, bc_done)
                    scoff = dc * 32 + half * 16
                    for b in range(B2):
                        it = scan_it[0]
                        scan_it[0] += 1
                        dbuf_i = dBu_bufs[it % 2]
                        scv_b = scar_s[:, scoff + b * 8:scoff + b * 8 + 8]
                        W("gpsimd", sV, du_done)
                        W("gpsimd", sdma, bc_done)
                        W("gpsimd", sV, dbuf_guard[it % 2])
                        op("gpsimd", lambda e, b=b, half=half, dbuf_i=dbuf_i:
                           e.tensor_tensor(
                               dbuf_i[:].rearrange("p (n t) -> p n t", n=8),
                               du_s[:, b * T:(b + 1) * T].unsqueeze(1)
                               .broadcast_to([128, 8, T]),
                               Bbc_s[:, (b * NST + half * 8) * T:
                                     (b * NST + half * 8 + 8) * T]
                               .rearrange("p (n t) -> p n t", n=8),
                               op=ALU.mult), sG)
                        W("vector", sG)
                        for n8 in range(8):
                            op("vector", lambda e, ab=ab, b=b, n8=n8,
                               dbuf_i=dbuf_i, scol=scoff + b * 8 + n8:
                               e.tensor_tensor_scan(
                                   ss_s[:, n8 * T:(n8 + 1) * T],
                                   ab[:, b * 8 * T + n8 * T:
                                      b * 8 * T + (n8 + 1) * T],
                                   dbuf_i[:, n8 * T:(n8 + 1) * T],
                                   scar_s[:, scol:scol + 1],
                                   op0=ALU.mult, op1=ALU.add), sV)
                        if half == 1 and b == 0:
                            for b0 in range(B2):
                                tb0 = 3 * T * b0
                                op("vector", lambda e, b0=b0, tb0=tb0:
                                   e.tensor_tensor(
                                       ytmp_s[:, b0 * T:(b0 + 1) * T],
                                       tree_s[:, tb0:tb0 + T],
                                       tree_s[:, tb0 + T:tb0 + 2 * T],
                                       op=ALU.add), sV)
                        ssv = ss_s[:].rearrange("p (n t) -> p n t", n=8)
                        op("vector", lambda e, b=b, half=half, ssv=ssv,
                           dbuf_i=dbuf_i: e.tensor_tensor(
                               dbuf_i[:].rearrange("p (n t) -> p n t", n=8), ssv,
                               Cbc_s[:, (b * NST + half * 8) * T:
                                     (b * NST + half * 8 + 8) * T]
                               .rearrange("p (n t) -> p n t", n=8),
                               op=ALU.mult), sV)
                        op("vector", lambda e, ssv=ssv, scv_b=scv_b: e.tensor_copy(
                            scv_b, ssv[:, :, T - 1]), sV)
                        tb = 3 * T * b
                        op("vector", lambda e, dbuf_i=dbuf_i: e.tensor_tensor(
                            ss_s[:, 0:4 * T],
                            dbuf_i[:].rearrange("p (a two t) -> p a two t", two=2,
                                                t=T)[:, :, 0, :],
                            dbuf_i[:].rearrange("p (a two t) -> p a two t", two=2,
                                                t=T)[:, :, 1, :], op=ALU.add), sV)
                        op("vector", lambda e, tb=tb: e.tensor_tensor(
                            tree_s[:, tb:tb + 2 * T],
                            ss_s[:, 0:4 * T].rearrange(
                                "p (a two t) -> p a two t", two=2, t=T)[:, :, 0, :],
                            ss_s[:, 0:4 * T].rearrange(
                                "p (a two t) -> p a two t", two=2, t=T)[:, :, 1, :],
                            op=ALU.add), sV)
                        dbuf_guard[it % 2] = sV.n
                    if half == 1:
                        op("vector", lambda e: e.nop(cycle_cnt=700), sV)
                        for b in range(B2):
                            tb = 3 * T * b
                            op("vector", lambda e, b=b, tb=tb: e.tensor_tensor(
                                tree_s[:, tb + 2 * T:tb + 3 * T],
                                tree_s[:, tb:tb + T],
                                tree_s[:, tb + T:tb + 2 * T], op=ALU.add), sV)
                        op("vector", lambda e: e.nop(cycle_cnt=700), sV)
                        for b in range(B2):
                            tb = 3 * T * b
                            op("vector", lambda e, b=b, tb=tb: e.tensor_tensor(
                                ytmp_s[:, b * T:(b + 1) * T],
                                ytmp_s[:, b * T:(b + 1) * T],
                                tree_s[:, tb + 2 * T:tb + 3 * T], op=ALU.add), sV)
                        op("vector", lambda e: e.nop(cycle_cnt=700), sV)
                        op("vector", lambda e, dc=dc: e.scalar_tensor_tensor(
                            ss_s[:, 0:TT2],
                            xc_s[:, dc * TT2:(dc + 1) * TT2],
                            dpar_s[:, dc:dc + 1],
                            ytmp_s[:], op0=ALU.mult, op1=ALU.add), sV)
                        op("vector", lambda e: e.nop(cycle_cnt=700), sV)
                        op("vector", lambda e, dc=dc: e.tensor_tensor(
                            xc_s[:, dc * TT2:(dc + 1) * TT2],
                            ss_s[:, 0:TT2],
                            sz_s[:, dc * TT2:(dc + 1) * TT2],
                            op=ALU.mult), sV)
                    dA_scan_done[dc * 2 + half] = sV.n
                delta_used[dc] = sV.n

            down_evac = {}

            def emit_ffn_j(j):
                joff, w = JSLABS[j]
                nkc = w // 128
                for wi, (wT, pbs) in enumerate(((wgT_d, (4, 5)), (wuT_d, (6, 7)))):
                    for kc in range(8):
                        sl = kc % 4
                        W("sync", sT, wgu_guard[sl])
                        dma("sync", wgu_s[:, sl * 512:sl * 512 + w],
                            wT[kc * 128:(kc + 1) * 128, joff:joff + w], sdma)
                        W("tensor", sdma, sdma.n)
                        if kc == 0:
                            W("tensor", sS, xqfm_done)
                            if j > 0 and wi == 0:
                                W("tensor", sV, down_evac[j - 1])
                        for b in range(B2):
                            op("tensor", lambda e, b=b, kc=kc, sl=sl, w=w,
                               pbx=pbs[b]: e.matmul(
                                   pb[pbx][:, 0:w],
                                   xqfm_s[:, kc * TT2 + b * T:
                                          kc * TT2 + (b + 1) * T],
                                   wgu_s[:, sl * 512:sl * 512 + w],
                                   start=(kc == 0), stop=(kc == 7)), sT)
                        wgu_guard[sl] = sT.n
                gu_done = sT.n
                for b in range(B2):
                    tsl = tokscr_s[:, b * D:b * D + w]
                    q0 = qs_s[:, 0:w]
                    q1 = qs_s[:, D:D + w]
                    q2 = qs_s[:, 2 * D:2 * D + w]
                    W("scalar", sT, gu_done)
                    op("scalar", lambda e, b=b, tsl=tsl: e.activation(
                        tsl, pb[4 + b][:, 0:w], AF.Silu), sS)
                    if _DEBUG and c == 0 and j == 0 and b == 0:
                        W("vector", sT, gu_done)
                        op("vector", lambda e, w=w: e.tensor_copy(
                            qs_s[:, 0:w], pb[6][:, 0:w]), sV)
                        op("vector", lambda e, w=w: e.tensor_copy(
                            qs_s[:, D:D + w], pb[4][:, 0:w]), sV)
                        W("sync", sV)
                        dma("sync", dbg["dbg_up"], qs_s[:, 0:512], sdma)
                        dma("sync", dbg["dbg_gate"], qs_s[:, D:D + 512], sdma)
                        W("vector", sdma)
                    W("vector", sS)
                    op("vector", lambda e, b=b, tsl=tsl: e.tensor_tensor(
                        tsl, tsl, pb[6 + b][:, 0:w], op=ALU.mult), sV)
                    nblk = w // 16
                    W("scalar", sV)
                    op("scalar", lambda e, tsl=tsl, q0=q0: e.activation(
                        q0, tsl, AF.Abs), sS)
                    W("vector", sS)
                    op("vector", lambda e, q0=q0, nblk=nblk: e.reduce_max(
                        scl_s[:, 0:nblk].unsqueeze(2),
                        q0.rearrange("p (a b) -> p a b", b=16),
                        axis=AX.X, apply_absolute_value=False), sV)
                    op("vector", lambda e: e.nop(cycle_cnt=700), sV)
                    op("vector", lambda e, nblk=nblk: e.tensor_scalar(
                        scl_s[:, 176:176 + nblk], scl_s[:, 0:nblk], 1.0 / 6.0, 1e-12,
                        op0=ALU.mult, op1=ALU.max), sV)
                    op("vector", lambda e: e.nop(cycle_cnt=700), sV)
                    op("vector", lambda e, nblk=nblk: e.reciprocal(
                        scl_s[:, 0:nblk], scl_s[:, 176:176 + nblk]), sV)
                    op("vector", lambda e: e.nop(cycle_cnt=700), sV)
                    rsb = scl_s[:, 0:nblk].unsqueeze(2).broadcast_to([128, nblk, 16])
                    op("vector", lambda e, tsl=tsl, rsb=rsb, nblk=nblk:
                       e.tensor_tensor(
                           tsl.rearrange("p (a b) -> p a b", b=16),
                           tsl.rearrange("p (a b) -> p a b", b=16), rsb,
                           op=ALU.mult), sV)
                    W("scalar", sV)
                    op("scalar", lambda e, tsl=tsl, q0=q0: e.activation(
                        q0, tsl, AF.Abs), sS)
                    W("vector", sS)
                    op("vector", lambda e, q0=q0, q1=q1: e.tensor_single_scalar(
                        q1, q0, 0.25, op=ALU.is_gt), sV)
                    for th in (0.75, 1.25, 1.75):
                        op("vector", lambda e, th=th, q0=q0, q1=q1:
                           e.scalar_tensor_tensor(q1, q0, th, q1,
                                                  op0=ALU.is_gt, op1=ALU.add), sV)
                    op("vector", lambda e, q0=q0, q2=q2: e.tensor_single_scalar(
                        q2, q0, 2.5, op=ALU.is_gt), sV)
                    for th in (3.5, 5.0, 5.0):
                        op("vector", lambda e, th=th, q0=q0, q2=q2:
                           e.scalar_tensor_tensor(q2, q0, th, q2,
                                                  op0=ALU.is_gt, op1=ALU.add), sV)
                    W("scalar", sV)
                    op("scalar", lambda e, tsl=tsl, q0=q0: e.activation(
                        q0, tsl, AF.Sign), sS)
                    W("vector", sS)
                    op("vector", lambda e, q1=q1, q2=q2: e.scalar_tensor_tensor(
                        q1, q1, 0.5, q2, op0=ALU.mult, op1=ALU.add), sV)
                    op("vector", lambda e, q0=q0, q1=q1, q2=q2: e.tensor_tensor(
                        q2, q1, q0, op=ALU.mult), sV)
                    ssb = scl_s[:, 176:176 + nblk].unsqueeze(2).broadcast_to(
                        [128, nblk, 16])
                    op("vector", lambda e, tsl=tsl, q2=q2, ssb=ssb, nblk=nblk:
                       e.tensor_tensor(
                           tsl.rearrange("p (a b) -> p a b", b=16),
                           q2.rearrange("p (a b) -> p a b", b=16), ssb,
                           op=ALU.mult), sV)
                    hq_done = sV.n
                    if _DEBUG and c == 0 and j == 0 and b == 0:
                        W("sync", sV, hq_done)
                        dma("sync", dbg["dbg_h"], tokscr_s[:], sdma)
                        W("vector", sdma)
                        W("scalar", sdma)
                    W("tensor", sV, hq_done)
                    for r in range(nkc):
                        op("tensor", lambda e, b=b, r=r, tsl=tsl: e.transpose(
                            pb[4 + b][:, r * 128:(r + 1) * 128],
                            tsl[:, r * 128:(r + 1) * 128], ident_s[:]), sT)
                    W("vector", sT)
                    op("vector", lambda e, b=b, nkc=nkc: e.tensor_copy(
                        hfm_s[:, b * 1024:b * 1024 + nkc * 128],
                        pb[4 + b][:, 0:nkc * 128]), sV)
                    hqT_done = sV.n
                    W("tensor", sV, hqT_done)
                    for ds in range(2):
                        for kc4 in range(nkc):
                            i = wd_it[0]
                            wd_it[0] += 1
                            buf = wd_bufs[i % 2]
                            W("sync", sT, wd_guard[i % 2])
                            if i % 2 == 1:
                                W("sync", sT, prev_wout_done)
                            kc = joff // 128 + kc4
                            dma("sync", buf[:, 0:512],
                                wdT_d[kc * 128:(kc + 1) * 128,
                                      ds * 512:(ds + 1) * 512], sdma)
                            W("tensor", sdma, sdma.n)
                            op("tensor", lambda e, b=b, kc4=kc4, ds=ds,
                               buf=buf, nkc=nkc: e.matmul(
                                   pb[ds][:, 0:512],
                                   hfm_s[:, b * 1024 + kc4 * 128:
                                         b * 1024 + (kc4 + 1) * 128],
                                   buf[:, 0:512],
                                   start=(kc4 == 0), stop=(kc4 == nkc - 1)), sT)
                            wd_guard[i % 2] = sT.n
                    dn_done = sT.n
                    W("vector", sT, dn_done)
                    for ds in range(2):
                        op("vector", lambda e, b=b, ds=ds: e.tensor_tensor(
                            ffnout_s[:, b * D + ds * 512:b * D + (ds + 1) * 512],
                            pb[ds][:, 0:512],
                            ffnout_s[:, b * D + ds * 512:b * D + (ds + 1) * 512],
                            op=ALU.add), sV)
                down_evac[j] = sV.n

            ffn_after = {1: 0, 4: 1, 7: 2, 9: 3, 11: 4, 13: 5}
            emit_delta(0)
            emit_delta(1)
            for dc in range(KDI):
                if dc + 2 < KDI:
                    emit_delta(dc + 2)
                emit_scan_dc(dc)
                if dc in ffn_after:
                    emit_ffn_j(ffn_after[dc])
            scan_all_done = sV.n
            prev_scan_done = scan_all_done
            prev_pb3_free = pb3_free[KDI - 1]

            mm_done = {(-1, 0): prev_wout_done, (-1, 1): prev_wout_done}
            ld_done2 = {}
            W("scalar", sT, wd_guard[1])
            for dc in range(KDI):
                for ds in range(2):
                    W("scalar", sT, mm_done[(dc - 1, ds)])
                    dma("scalar", wout_s[:, ds * 512:(ds + 1) * 512],
                        wout_d[dc * 128:(dc + 1) * 128,
                               ds * 512:(ds + 1) * 512], qdma)
                    ld_done2[(dc, ds)] = qdma.n
                for ds in range(2):
                    W("tensor", qdma, ld_done2[(dc, ds)])
                    W("tensor", sV, scan_all_done)
                    for b in range(B2):
                        op("tensor", lambda e, dc=dc, b=b, ds=ds: e.matmul(
                            pb[4 + 2 * b + ds][:, 0:512],
                            xc_s[:, dc * TT2 + b * T:dc * TT2 + (b + 1) * T],
                            wout_s[:, ds * 512:(ds + 1) * 512],
                            start=(dc == 0), stop=(dc == KDI - 1)), sT)
                    mm_done[(dc, ds)] = sT.n
            wout_done = sT.n
            prev_wout_done = wout_done

            if _DEBUG and c == 0:
                W("sync", sV, scan_all_done)
                dma("sync", dbg["dbg_y"], xc_s[:], sdma)
                dma("sync", dbg["dbg_B"], Bbc_s[:], sdma)
                dma("sync", dbg["dbg_ffn"], ffnout_s[:], sdma)
                W("sync", sS)
                dma("sync", dbg["dbg_delta"], delta_s[:, TT2:2 * TT2], sdma)
                for _e in ("vector", "scalar", "tensor"):
                    W(_e, sdma)

            ck(4)
            W("vector", sT, wout_done)
            for b in range(B2):
                for ds in range(2):
                    op("vector", lambda e, b=b, ds=ds: e.tensor_tensor(
                        xntok_s[:, b * D + ds * 512:b * D + (ds + 1) * 512],
                        pb[4 + 2 * b + ds][:, 0:512],
                        ffnout_s[:, b * D + ds * 512:b * D + (ds + 1) * 512],
                        op=ALU.add), sV)
                op("vector", lambda e, b=b: e.tensor_tensor(
                    xntok_s[:, b * D:(b + 1) * D],
                    xntok_s[:, b * D:(b + 1) * D],
                    xtok_s[:, b * D:(b + 1) * D], op=ALU.add), sV)
                op("vector", lambda e, b=b: e.memset(
                    ffnout_s[:, b * D:(b + 1) * D], 0.0), sV)
            asm_done = sV.n
            chunk_out_read = asm_done
            W("sync", sV, asm_done)
            for b in range(B2):
                dma("sync", out_d[b, c * T:(c + 1) * T, :],
                    xntok_s[:, b * D:(b + 1) * D], sdma)
          except _StopRec:
            pass

        if _DEBUG:
            for _e in ("vector", "scalar", "tensor"):
                W("sync", own[_e])
            dma("sync", dbg["dbg_done"], ident_s[:], sdma)
        final_dma = sdma.n
        P["sync"].append(lambda e, v=final_dma: e.wait_ge(sdma.h, v))

        @block.sync
        def _(sync):
            for f in P["sync"]:
                f(sync)

        @block.tensor
        def _(tensor):
            for f in P["tensor"]:
                f(tensor)

        @block.vector
        def _(vector):
            for f in P["vector"]:
                f(vector)

        @block.scalar
        def _(scalar):
            for f in P["scalar"]:
                f(scalar)

        @block.gpsimd
        def _(gpsimd):
            for f in P["gpsimd"]:
                f(gpsimd)

    return nc


_NC_CACHE = {}


def kernel(**inputs):
    key = "nc"
    if key not in _NC_CACHE:
        _NC_CACHE[key] = build_nc()
    nc = _NC_CACHE[key]

    x = np.ascontiguousarray(np.asarray(inputs["L0_attn_x"], dtype=np.float32))
    wmap = {
        "ln_g": "L0_attn_ln_g", "ln_b": "L0_attn_ln_b", "W_in": "L0_attn_W_in",
        "conv_w": "L0_attn_conv_w", "conv_b": "L0_attn_conv_b",
        "W_xproj": "L0_attn_W_xproj", "W_dt": "L0_attn_W_dt",
        "b_dt": "L0_attn_b_dt", "A_log": "L0_attn_A_log",
        "D_param": "L0_attn_D_param", "W_out": "L0_attn_W_out",
        "w_gate": "L0_ffn_w_gate", "w_up": "L0_ffn_w_up",
        "w_down": "L0_ffn_w_down",
    }
    common = {k: np.ascontiguousarray(np.asarray(inputs[v], dtype=np.float32))
              for k, v in wmap.items()}
    common["ident"] = np.eye(128, dtype=np.float32)

    in_maps = []
    for c in range(8):
        m = dict(common)
        m["x"] = np.ascontiguousarray(x[2 * c:2 * c + 2])
        in_maps.append(m)

    res = run_bass_kernel_spmd(nc, in_maps, core_ids=list(range(8)))
    out = np.concatenate([np.asarray(r["out"]) for r in res.results], axis=0)
    return out
